# version 12
# speedup vs baseline: 1.1611x; 1.0084x over previous
import numpy as np
import ml_dtypes
import concourse.bass as bass
import concourse.mybir as mybir
from concourse.bass_utils import run_bass_kernel_spmd

BF = ml_dtypes.bfloat16
F32 = mybir.dt.float32
BF16 = mybir.dt.bfloat16
AF = mybir.ActivationFunctionType
ALU = mybir.AluOpType
AX = mybir.AxisListType

B_CORE = 2
T = 2048
D = 1024
NTOK = B_CORE * T
NCH = D // 128



class Sem:

    def __init__(self, stack, nc, name):
        self.h = stack.enter_context(nc.semaphore(name))
        self.n = 0

    def inc(self, instr, k=1):
        instr.then_inc(self.h, k)
        self.n += k
        return self.n


class P:

    def __init__(self, nc, block, stack):
        self.nc = nc
        self.block = block
        self.stack = stack
        self.sems = {}

    def sem(self, name):
        if name not in self.sems:
            self.sems[name] = Sem(self.stack, self.nc, name)
        return self.sems[name]

    def mark(self, eng, ins):
        s = self.sem("ord_" + eng)
        s.n += 1
        ins.then_inc(s.h, 1)
        return s.n

    def wmark(self, eng, e, n=None):
        s = self.sem("ord_" + eng)
        e.wait_ge(s.h, s.n if n is None else n)

    def barrier(self):
        return

    def _old_barrier(self):
        names = ["bar_sync", "bar_tensor", "bar_vector", "bar_scalar", "bar_gpsimd"]
        sems = [self.sem(n) for n in names]
        targets = []
        for eng_name, s in zip(["sync", "tensor", "vector", "scalar", "gpsimd"], names):
            pass
        engs = ["sync", "tensor", "vector", "scalar", "gpsimd"]
        for eng, s in zip(engs, sems):
            def mk(eng=eng, s=s):
                def body(e):
                    e.drain()
                    e.sem_inc(s.h, 1)
                    s.n += 1
                return body
            getattr(self.block, eng)(mk())
        waits = [(s.h, s.n) for s in sems]
        for eng in engs:
            def mk(eng=eng):
                def body(e):
                    for h, n in waits:
                        e.wait_ge(h, n)
                return body
            getattr(self.block, eng)(mk())


def bcast_ap(t_ap, offset_elems, dims):
    return bass.AP(tensor=t_ap.tensor, offset=t_ap.offset + offset_elems, ap=dims)



def emit_transpose_in(p, x_dram, res_dram, consts):
    _emit_transpose(p, "ti", res_dram, consts, x_dram=x_dram, direction="in")


def emit_transpose_out(p, res_dram, y_dram, consts):
    _emit_transpose(p, "to", res_dram, consts, y_dram=y_dram, direction="out")


def _emit_transpose(p, tag, res_dram, consts, x_dram=None, y_dram=None, direction="in"):
    nc, block = p.nc, p.block
    ntile = NTOK // 128
    s_ld = [p.sem(tag + "_ld0"), p.sem(tag + "_ld1")]
    s_pe = p.sem(tag + "_pe")
    s_ev = p.sem(tag + "_ev")
    s_st = [p.sem(tag + "_st0"), p.sem(tag + "_st1")]
    ld0 = [s.n for s in s_ld]
    pe0, ev0 = s_pe.n, s_ev.n
    st0 = [s.n for s in s_st]
    rv = res_dram.rearrange("(j q) c -> q j c", q=128)

    with (
        nc.sbuf_tensor(tag + "_xt", [128, 2 * D], F32) as xt,
        nc.sbuf_tensor(tag + "_yt", [128, 2 * D], F32) as yt,
        nc.psum_tensor(tag + "_ps0", [128, 1024], F32) as ps0,
        nc.psum_tensor(tag + "_ps1", [128, 1024], F32) as ps1,
    ):
        ps = [ps0, ps1]

        @block.sync
        def _(e):
            for i in range(ntile):
                k = i % 2
                if i >= 2:
                    e.wait_ge(s_pe.h, pe0 + 8 * (i - 1))
                if direction == "in":
                    src = x_dram[i * 128:(i + 1) * 128, :]
                    dst = xt[:, k * D:k * D + D]
                else:
                    src = rv[:, :, i * 128:(i + 1) * 128]
                    dst = xt.rearrange("q (bb j c) -> q bb j c", bb=2, j=8)[:, k, :, :]
                e.dma_start(out=dst, in_=src).then_inc(s_ld[k].h, 16)
        for k in range(2):
            s_ld[k].n = ld0[k] + 16 * ((ntile + 1 - k) // 2)

        @block.tensor
        def _(e):
            ident = consts["ident_f32"]
            for i in range(ntile):
                k = i % 2
                e.wait_ge(s_ld[k].h, ld0[k] + 16 * (i // 2 + 1))
                if i >= 2:
                    e.wait_ge(s_ev.h, ev0 + (i - 1))
                for j in range(8):
                    pt = ps[k][:, j * 128:j * 128 + 128]
                    ins = e.transpose(pt, xt[:, k * D + j * 128:k * D + j * 128 + 128],
                                      ident[:])
                    ins.then_inc(s_pe.h, 1)
        s_pe.n = pe0 + 8 * ntile

        @block.vector
        def _(e):
            for i in range(ntile):
                k = i % 2
                if i >= 2:
                    e.wait_ge(s_st[k].h, st0[k] + 16 * (i // 2))
                e.wait_ge(s_pe.h, pe0 + 8 * (i + 1))
                ins = e.tensor_copy(out=yt[:, k * D:k * D + D], in_=ps[k][:, 0:D])
                ins.then_inc(s_ev.h, 1)
        s_ev.n = ev0 + ntile

        @block.gpsimd
        def _(e):
            for i in range(ntile):
                k = i % 2
                e.wait_ge(s_ev.h, ev0 + (i + 1))
                if direction == "in":
                    src = yt.rearrange("q (bb j c) -> q bb j c", bb=2, j=8)[:, k, :, :]
                    dst = rv[:, :, i * 128:(i + 1) * 128]
                else:
                    src = yt[:, k * D:k * D + D]
                    dst = y_dram[i * 128:(i + 1) * 128, :]
                e.dma_start(out=dst, in_=src).then_inc(s_st[k].h, 16)
            for k in range(2):
                e.wait_ge(s_st[k].h, st0[k] + 16 * ((ntile + 1 - k) // 2))
        for k in range(2):
            s_st[k].n = st0[k] + 16 * ((ntile + 1 - k) // 2)



LN_EPS = 1e-5


def ln_workspace(p, stack, tag, ps0=None, ps1=None):
    nc = p.nc
    if ps0 is None:
        ps0 = stack.enter_context(nc.psum_tensor(tag + "_p0", [128, 512], F32))
        ps1 = stack.enter_context(nc.psum_tensor(tag + "_p1", [128, 512], F32))
    ws = {
        "rt": stack.enter_context(nc.sbuf_tensor(tag + "_rt", [128, 8 * 512], F32)),
        "ones1": stack.enter_context(nc.sbuf_tensor(tag + "_o1", [1, 128], F32)),
        "onesc": stack.enter_context(nc.sbuf_tensor(tag + "_oc", [128, 1], F32)),
        "srow": stack.enter_context(nc.sbuf_tensor(tag + "_srow", [1, 4 * 512], F32)),
        "tmp2": stack.enter_context(nc.sbuf_tensor(tag + "_tmp2", [128, 1024], F32)),
        "xb2": stack.enter_context(nc.sbuf_tensor(tag + "_xb2", [128, 1024], BF16)),
        "sq2": stack.enter_context(nc.sbuf_tensor(tag + "_sq2", [128, 1024], BF16)),
        "ones1b": stack.enter_context(nc.sbuf_tensor(tag + "_o1b", [1, 128], BF16)),
        "onescb": stack.enter_context(nc.sbuf_tensor(tag + "_ocb", [128, 1], BF16)),
        "srowb": stack.enter_context(nc.sbuf_tensor(tag + "_srb", [1, 1024], BF16)),
        "eps1": stack.enter_context(nc.sbuf_tensor(tag + "_eps", [1, 1], F32)),
        "spc": stack.enter_context(nc.sbuf_tensor(tag + "_spc", [1, 2], F32)),
        "ps0": ps0,
        "ps1": ps1,
        "s_ld": p.sem(tag + "_lld"),
        "s_pe": p.sem(tag + "_lpe"),
        "s_dv": p.sem(tag + "_ldv"),
        "s_ac": p.sem(tag + "_lac"),
    }
    return ws


def emit_ln_init(p, ws):
    block = p.block
    s_dv = ws["s_dv"]

    @block.vector
    def _(e):
        e.memset(ws["ones1"][:], 1.0)
        e.memset(ws["eps1"][:], LN_EPS)
        e.memset(ws["spc"][:], 0.0)
        e.memset(ws["ones1b"][:], 1.0)
        e.memset(ws["onescb"][:], 1.0)
        ins = e.memset(ws["onesc"][:], 1.0)
        ins.then_inc(s_dv.h, 1)
    s_dv.n += 1
    ws["init_done"] = s_dv.n


def emit_ln_tile(p, res_dram, ws, t0, xhat, xh_off, xhat_f32=None, pre_apply_waits=(), pre_load_waits=(), pre_stats_waits=(), xh_stride=512, acc_out=None, acc_off=0, acc_stride=512):
    nc, block = p.nc, p.block
    rv = res_dram.rearrange("(j q) c -> q j c", q=128)
    s_ld, s_pe, s_dv, s_ac = ws["s_ld"], ws["s_pe"], ws["s_dv"], ws["s_ac"]
    rt, srow = ws["rt"], ws["srow"]
    tmp2 = ws["tmp2"]
    ps0, ps1 = ws["ps0"], ws["ps1"]

    ld_t = s_ld.n + 16
    dv_base = s_dv.n
    pe_base = s_pe.n
    ac_base = s_ac.n

    ordv = p.sem("ord_v")
    ordv_n = ordv.n

    @block.sync
    def _(e):
        if pe_base > 0:
            e.wait_ge(s_pe.h, pe_base)
        if ordv_n > 0:
            e.wait_ge(ordv.h, ordv_n)
        for h_, n_ in pre_load_waits:
            if n_ > 0:
                e.wait_ge(h_, n_)
        e.dma_start(out=rt.rearrange("q (j c) -> q j c", j=8),
                    in_=rv[:, :, t0:t0 + 512]).then_inc(s_ld.h, 16)
    s_ld.n = ld_t

    sq_base = s_dv.n
    ac_cp_base = s_ac.n
    pe_base2 = s_pe.n

    @block.scalar
    def _(e, pe_base2=pe_base2, ac_base=ac_base, ac_cp_base=ac_cp_base):
        e.wait_ge(s_ld.h, ld_t)
        if ac_base > 0:
            e.wait_ge(s_ac.h, ac_base)
        for j in range(8):
            if j >= 2:
                e.wait_ge(s_pe.h, pe_base2 + j - 1)
            ins = e.activation(out=ws["xb2"][:, (j % 2) * 512:(j % 2) * 512 + 512],
                               in_=rt[:, j * 512:(j + 1) * 512], func=AF.Copy)
            ins.then_inc(s_ac.h, 1)
    s_ac.n += 8

    @block.vector
    def _(e, sq_base=sq_base, pe_base2=pe_base2):
        e.wait_ge(s_ld.h, ld_t)
        e.wait_ge(s_dv.h, sq_base)
        for j in range(8):
            if j >= 2:
                e.wait_ge(s_pe.h, pe_base2 + j - 1)
            ins = e.tensor_mul(out=ws["sq2"][:, (j % 2) * 512:(j % 2) * 512 + 512],
                               in0=rt[:, j * 512:(j + 1) * 512],
                               in1=rt[:, j * 512:(j + 1) * 512])
            ins.then_inc(s_dv.h, 1)
    s_dv.n += 8
    sq_done_all = s_dv.n

    @block.tensor
    def _(e, sq_base=sq_base, ac_base=ac_base, ac_cp_base=ac_cp_base):
        e.wait_ge(s_dv.h, ws["init_done"])
        if ac_base > 0:
            e.wait_ge(s_ac.h, ac_base)
        for h_, n_ in pre_stats_waits:
            if n_ > 0:
                e.wait_ge(h_, n_)
        for j in range(8):
            e.wait_ge(s_ac.h, ac_cp_base + j + 1)
            e.matmul(ps0[0:1, 0:512], ws["onescb"][:],
                     ws["xb2"][:, (j % 2) * 512:(j % 2) * 512 + 512],
                     start=(j == 0), stop=(j == 7), skip_group_check=True)
            e.wait_ge(s_dv.h, sq_base + j + 1)
            ins = e.matmul(ps1[0:1, 0:512], ws["onescb"][:],
                           ws["sq2"][:, (j % 2) * 512:(j % 2) * 512 + 512],
                           start=(j == 0), stop=(j == 7), skip_group_check=True)
            ins.then_inc(s_pe.h, 1)
    s_pe.n += 8
    stats_done = s_pe.n

    @block.vector
    def _(e):
        e.wait_ge(s_pe.h, stats_done)
        e.wait_ge(s_dv.h, sq_done_all)
        mu = srow[:, 0:512]
        t_ = srow[:, 512:1024]
        ins = e.tensor_scalar_mul(out=mu, in0=ps0[0:1, 0:512], scalar1=1.0 / D)
        p.wmark("v", e, p.mark("v", ins))
        ins = e.tensor_mul(out=t_, in0=mu, in1=mu)
        p.wmark("v", e, p.mark("v", ins))
        e.scalar_tensor_tensor(out=t_, in0=ps1[0:1, 0:512], scalar=1.0 / D,
                               in1=t_, op0=ALU.mult, op1=ALU.subtract).then_inc(s_dv.h, 1)
    s_dv.n += 1
    var_done = s_dv.n

    @block.scalar
    def _(e):
        e.wait_ge(s_dv.h, var_done)
        t_ = srow[:, 512:1024]
        a = srow[:, 1024:1536]
        ins = e.activation(out=t_, in_=t_, func=AF.Ln, bias=ws["eps1"][:])
        p.wmark("s", e, p.mark("s", ins))
        ins = e.activation(out=a, in_=t_, func=AF.Exp, scale=-0.5)
        ins.then_inc(s_dv.h, 1)
    s_dv.n += 1
    a_done = s_dv.n

    @block.vector
    def _(e):
        e.wait_ge(s_dv.h, a_done)
        p.wmark("v", e)
        b = srow[:, 1536:2048]
        ins = e.scalar_tensor_tensor(out=b, in0=srow[:, 0:512], scalar=-1.0,
                                     in1=srow[:, 1024:1536], op0=ALU.mult, op1=ALU.mult)
        p.wmark("v", e, p.mark("v", ins))
        ins = e.tensor_copy(out=ws["srowb"][:, 0:512], in_=srow[:, 1024:1536])
        p.wmark("v", e, p.mark("v", ins))
        ins = e.tensor_copy(out=ws["srowb"][:, 512:1024], in_=srow[:, 1536:2048])
        ins.then_inc(s_dv.h, 1)
    s_dv.n += 1
    b_done = s_dv.n

    @block.tensor
    def _(e):
        e.wait_ge(s_dv.h, b_done)
        e.matmul(ps0[:, 0:512], ws["ones1b"][:], ws["srowb"][:, 0:512], start=True, stop=True)
        ins = e.matmul(ps1[:, 0:512], ws["ones1b"][:], ws["srowb"][:, 512:1024], start=True, stop=True)
        ins.then_inc(s_pe.h, 1)
    s_pe.n += 1
    bc_done = s_pe.n

    @block.vector
    def _(e):
        e.wait_ge(s_pe.h, bc_done)
        if ac_base > 0:
            e.wait_ge(s_ac.h, ac_base)
        for h_, n_ in pre_apply_waits:
            if n_ > 0:
                e.wait_ge(h_, n_)
        for j in range(8):
            jj = j % 2
            ins = e.tensor_mul(out=ws["tmp2"][:, jj * 512:(jj + 1) * 512],
                               in0=rt[:, j * 512:(j + 1) * 512], in1=ps0[:, 0:512])
            p.wmark("v", e, p.mark("v", ins))
            ins = e.tensor_tensor(out=xhat[:, xh_off + j * xh_stride:xh_off + j * xh_stride + 512],
                                  in0=ws["tmp2"][:, jj * 512:(jj + 1) * 512],
                                  in1=ps1[:, 0:512], op=ALU.add)
            if xhat_f32 is not None:
                ins = e.tensor_tensor(out=xhat_f32[:, j * 512:(j + 1) * 512],
                                      in0=ws["tmp2"][:, jj * 512:(jj + 1) * 512],
                                      in1=ps1[:, 0:512], op=ALU.add)
            if acc_out is not None:
                ins = e.tensor_tensor(out=acc_out[:, acc_off + j * acc_stride:acc_off + j * acc_stride + 512],
                                      in0=ws["tmp2"][:, jj * 512:(jj + 1) * 512],
                                      in1=ps1[:, 0:512], op=ALU.add)
            if j < 7:
                p.wmark("v", e, p.mark("v", ins))
        ins.then_inc(s_ac.h, 1)
    s_ac.n += 1



def emit_precast(p, pairs, tag, early_jobs=None, stack=None):
    nc, block = p.nc, p.block
    s_ld = [p.sem(tag + "_ld0"), p.sem(tag + "_ld1")]
    s_st = [p.sem(tag + "_st0"), p.sem(tag + "_st1")]
    ld0 = [x.n for x in s_ld]
    st0 = [x.n for x in s_st]
    W = 4096
    jobs = []
    for src, dst in pairs:
        R, M = src.shape
        assert R % 128 == 0 and M <= W
        for r0 in range(0, R, 128):
            jobs.append((src, dst, r0, M))
    if stack is not None:
        buf = stack.enter_context(nc.sbuf_tensor(tag + "_buf", [128, 2 * W], BF16))
        cmgr = contextlib.nullcontext()
    else:
        cmgr = nc.sbuf_tensor(tag + "_buf", [128, 2 * W], BF16)
    with cmgr as _buf:
        if stack is None:
            buf = _buf
        @block.gpsimd
        def _(e):
            for i, (src, dst, r0, M) in enumerate(jobs):
                k = i % 2
                if i >= 2:
                    e.wait_ge(s_st[k].h, st0[k] + 16 * (i // 2))
                e.dma_start(out=buf[:, k * W:k * W + M],
                            in_=src[r0:r0 + 128, :]).then_inc(s_ld[k].h, 16)
        for k in range(2):
            s_ld[k].n = ld0[k] + 16 * ((len(jobs) + 1 - k) // 2)

        @block.sync
        def _(e):
            for i, (src, dst, r0, M) in enumerate(jobs):
                k = i % 2
                e.wait_ge(s_ld[k].h, ld0[k] + 16 * (i // 2 + 1))
                e.dma_start(out=dst[r0:r0 + 128, :],
                            in_=buf[:, k * W:k * W + M]).then_inc(s_st[k].h, 16)
        for k in range(2):
            s_st[k].n = st0[k] + 16 * ((len(jobs) + 1 - k) // 2)
        if early_jobs is None:
            @block.sync
            def _(e):
                for k in range(2):
                    e.wait_ge(s_st[k].h, s_st[k].n)
        if early_jobs is not None:
            ne = early_jobs
            p.pc_info = {
                "sems": (s_st[0], s_st[1]),
                "early": (st0[0] + 16 * ((ne + 1) // 2), st0[1] + 16 * (ne // 2)),
                "full": (s_st[0].n, s_st[1].n),
            }


def emit_l1f(p, res_dram, wd):
    nc, block = p.nc, p.block
    rv = res_dram.rearrange("(j q) c -> q j c", q=128)

    with contextlib.ExitStack() as stack:
        ps = [stack.enter_context(nc.psum_tensor("f1_ps%d" % i, [128, 512], F32))
              for i in range(8)]
        ws = ln_workspace(p, stack, "f1", ps0=ps[0], ps1=ps[1])
        xhat = stack.enter_context(nc.sbuf_tensor("f1_xh", [128, 8 * 512], BF16))
        mk_sb = stack.enter_context(nc.sbuf_tensor("f1_mk", [128, 32 * 128], BF16))
        up_sb = stack.enter_context(nc.sbuf_tensor("f1_up", [128, 32 * 512], BF16))
        y_sb = stack.enter_context(nc.sbuf_tensor("f1_y", [128, 16 * 512], BF16))
        h_sb = stack.enter_context(nc.sbuf_tensor("f1_h", [128, 64 * 512], BF16))
        oacc = stack.enter_context(nc.sbuf_tensor("f1_oa", [128, 8 * 512], F32))
        wband = stack.enter_context(nc.sbuf_tensor("f1_wb", [128, 4 * 1024], BF16))

        s_w = p.sem("f1_w")
        s_band = [p.sem("f1_b%d" % i) for i in range(4)]
        s_pe = p.sem("f1_pe")
        s_dv = p.sem("f1_dv")
        s_ac = p.sem("f1_ac")
        s_st = p.sem("f1_st")

        emit_ln_init(p, ws)

        @block.gpsimd
        def _(e):
            e.dma_start(out=mk_sb.rearrange("q (g m) -> q g m", g=32),
                        in_=wd["maker_bd"].rearrange("h c q m -> q (h c) m")).then_inc(s_w.h, 16)
            e.dma_start(out=up_sb.rearrange("q (g m) -> q g m", g=32),
                        in_=wd["up_bd"].rearrange("h c q m -> q (h c) m")).then_inc(s_w.h, 16)
        s_w.n += 32
        w_done = s_w.n

        @block.tensor
        def _(e):
            e.wait_ge(s_w.h, w_done)

        band_gen = [0] * 4
        wctbv = wd["wctb"]

        for tt in range(8):
            t0 = tt * 512
            prev_pe = s_pe.n
            prev_dv = s_dv.n
            emit_ln_tile(p, res_dram, ws, t0, xhat, 0,
                         pre_apply_waits=[(s_pe.h, prev_pe)],
                         pre_load_waits=[(s_dv.h, prev_dv)],
                         pre_stats_waits=[(s_dv.h, prev_dv)])
            xh_ready = ws["s_ac"].n

            for half in range(2):
                peY0, dvY0 = s_pe.n, s_dv.n

                @block.tensor
                def _(e, half=half, xh_ready=xh_ready, peY0=peY0, dvY0=dvY0):
                    e.wait_ge(ws["s_ac"].h, xh_ready)
                    if dvY0 > 0:
                        e.wait_ge(s_dv.h, dvY0)
                    for gi in range(16):
                        hh, ch = gi // 8, gi % 8
                        g = (half * 2 + hh) * 8 + ch
                        if gi >= 4:
                            e.wait_ge(s_dv.h, dvY0 + gi - 3)
                        e.matmul(ps[gi % 4][:], mk_sb[:, g * 128:(g + 1) * 128],
                                 xhat[:, ch * 512:(ch + 1) * 512],
                                 start=True, stop=True).then_inc(s_pe.h, 1)
                s_pe.n = peY0 + 16

                @block.vector
                def _(e, peY0=peY0):
                    for gi in range(16):
                        e.wait_ge(s_pe.h, peY0 + gi + 1)
                        e.tensor_copy(out=y_sb[:, gi * 512:(gi + 1) * 512],
                                      in_=ps[gi % 4][:]).then_inc(s_dv.h, 1)
                s_dv.n = dvY0 + 16
                dvY_end = s_dv.n

                peZ0, acZ0 = s_pe.n, s_ac.n

                @block.tensor
                def _(e, half=half, peZ0=peZ0, acZ0=acZ0, dvY_end=dvY_end):
                    e.wait_ge(s_dv.h, dvY_end)
                    for gi in range(16):
                        hh, ch = gi // 8, gi % 8
                        g = (half * 2 + hh) * 8 + ch
                        if gi >= 2:
                            e.wait_ge(s_ac.h, acZ0 + gi - 1)
                        bb = 4 * (gi % 2)
                        for m in range(4):
                            ins = e.matmul(ps[bb + m][:],
                                           up_sb[:, g * 512 + m * 128:g * 512 + m * 128 + 128],
                                           y_sb[:, gi * 512:(gi + 1) * 512],
                                           start=True, stop=True)
                        ins.then_inc(s_pe.h, 1)
                s_pe.n = peZ0 + 16

                @block.scalar
                def _(e, half=half, peZ0=peZ0):
                    for gi in range(16):
                        e.wait_ge(s_pe.h, peZ0 + gi + 1)
                        bb = 4 * (gi % 2)
                        for m in range(4):
                            kc = (gi // 8) * 32 + (gi % 8) * 4 + m
                            ins = e.activation(out=h_sb[:, kc * 512:(kc + 1) * 512],
                                               in_=ps[bb + m][:], func=AF.Gelu)
                        ins.then_inc(s_ac.h, 1)
                s_ac.n = acZ0 + 16
                ac_end = s_ac.n

                peC0 = s_pe.n
                bdma0 = [band_gen[r] for r in range(4)]

                @block.sync
                def _(e, half=half, peC0=peC0, bdma0=bdma0):
                    for kc in range(64):
                        r = kc % 4
                        gen = bdma0[r] + kc // 4
                        if kc >= 4:
                            e.wait_ge(s_pe.h, peC0 + kc - 3)
                        elif peC0 > 0:
                            e.wait_ge(s_pe.h, peC0)
                        e.dma_start(out=wband[:, r * 1024:(r + 1) * 1024],
                                    in_=wctbv[half * 8192 + kc * 128:half * 8192 + (kc + 1) * 128, :]
                                    ).then_inc(s_band[r].h, 16)
                for r in range(4):
                    band_gen[r] += 16
                    s_band[r].n += 16 * 16

                @block.tensor
                def _(e, half=half, peC0=peC0, ac_end=ac_end, bdma0=bdma0):
                    e.wait_ge(s_ac.h, ac_end)
                    for kc in range(64):
                        r = kc % 4
                        e.wait_ge(s_band[r].h, 16 * (bdma0[r] + kc // 4 + 1))
                        for m in range(8):
                            ins = e.matmul(ps[m][:],
                                           wband[:, r * 1024 + m * 128:r * 1024 + (m + 1) * 128],
                                           h_sb[:, kc * 512:(kc + 1) * 512],
                                           start=(kc == 0), stop=(kc == 63),
                                           skip_group_check=True)
                        ins.then_inc(s_pe.h, 1)
                s_pe.n = peC0 + 64
                peC_end = s_pe.n

                st_prev = s_st.n
                dv_half0 = s_dv.n

                @block.vector
                def _(e, half=half, peC_end=peC_end, st_prev=st_prev,
                      dv_half0=dv_half0):
                    e.wait_ge(s_pe.h, peC_end)
                    if half == 0:
                        if st_prev > 0:
                            e.wait_ge(s_st.h, st_prev)
                        for m in range(8):
                            ins = e.tensor_copy(out=oacc[:, m * 512:(m + 1) * 512], in_=ps[m][:])
                        ins.then_inc(s_dv.h, 1)
                    else:
                        e.wait_ge(s_dv.h, dv_half0)
                        for m in range(8):
                            ins = e.tensor_tensor(out=oacc[:, m * 512:(m + 1) * 512],
                                                  in0=oacc[:, m * 512:(m + 1) * 512],
                                                  in1=ps[m][:], op=ALU.add)
                        p.wmark("v", e, p.mark("v", ins))
                        for m in range(8):
                            ins = e.tensor_tensor(out=oacc[:, m * 512:(m + 1) * 512],
                                                  in0=oacc[:, m * 512:(m + 1) * 512],
                                                  in1=ws["rt"][:, m * 512:(m + 1) * 512],
                                                  op=ALU.add)
                        ins.then_inc(s_dv.h, 1)
                s_dv.n += 1

            dv_done = s_dv.n

            @block.sync
            def _(e, t0=t0, dv_done=dv_done):
                e.wait_ge(s_dv.h, dv_done)
                e.dma_start(out=rv[:, :, t0:t0 + 512],
                            in_=oacc.rearrange("q (j c) -> q j c", j=8)).then_inc(s_st.h, 16)
            s_st.n += 16

        @block.sync
        def _(e):
            e.wait_ge(s_st.h, s_st.n)




def emit_precast(p, pairs, tag, early_jobs=None, stack=None):
    nc, block = p.nc, p.block
    s_ld = [p.sem(tag + "_ld0"), p.sem(tag + "_ld1")]
    s_st = [p.sem(tag + "_st0"), p.sem(tag + "_st1")]
    ld0 = [x.n for x in s_ld]
    st0 = [x.n for x in s_st]
    W = 4096
    jobs = []
    for src, dst in pairs:
        R, M = src.shape
        assert R % 128 == 0 and M <= W
        for r0 in range(0, R, 128):
            jobs.append((src, dst, r0, M))
    if stack is not None:
        buf = stack.enter_context(nc.sbuf_tensor(tag + "_buf", [128, 2 * W], BF16))
        cmgr = contextlib.nullcontext()
    else:
        cmgr = nc.sbuf_tensor(tag + "_buf", [128, 2 * W], BF16)
    with cmgr as _buf:
        if stack is None:
            buf = _buf
        @block.gpsimd
        def _(e):
            for i, (src, dst, r0, M) in enumerate(jobs):
                k = i % 2
                if i >= 2:
                    e.wait_ge(s_st[k].h, st0[k] + 16 * (i // 2))
                e.dma_start(out=buf[:, k * W:k * W + M],
                            in_=src[r0:r0 + 128, :]).then_inc(s_ld[k].h, 16)
        for k in range(2):
            s_ld[k].n = ld0[k] + 16 * ((len(jobs) + 1 - k) // 2)

        @block.sync
        def _(e):
            for i, (src, dst, r0, M) in enumerate(jobs):
                k = i % 2
                e.wait_ge(s_ld[k].h, ld0[k] + 16 * (i // 2 + 1))
                e.dma_start(out=dst[r0:r0 + 128, :],
                            in_=buf[:, k * W:k * W + M]).then_inc(s_st[k].h, 16)
        for k in range(2):
            s_st[k].n = st0[k] + 16 * ((len(jobs) + 1 - k) // 2)
        if early_jobs is None:
            @block.sync
            def _(e):
                for k in range(2):
                    e.wait_ge(s_st[k].h, s_st[k].n)
        if early_jobs is not None:
            ne = early_jobs
            p.pc_info = {
                "sems": (s_st[0], s_st[1]),
                "early": (st0[0] + 16 * ((ne + 1) // 2), st0[1] + 16 * (ne // 2)),
                "full": (s_st[0].n, s_st[1].n),
            }


def emit_l1f(p, res_dram, wd):
    nc, block = p.nc, p.block
    rv = res_dram.rearrange("(j q) c -> q j c", q=128)

    with contextlib.ExitStack() as stack:
        ps = [stack.enter_context(nc.psum_tensor("f1_ps%d" % i, [128, 512], F32))
              for i in range(8)]
        ws = ln_workspace(p, stack, "f1", ps0=ps[0], ps1=ps[1])
        xhat = stack.enter_context(nc.sbuf_tensor("f1_xh", [128, 8 * 512], BF16))
        mk_sb = stack.enter_context(nc.sbuf_tensor("f1_mk", [128, 32 * 128], BF16))
        up_sb = stack.enter_context(nc.sbuf_tensor("f1_up", [128, 32 * 512], BF16))
        y_sb = stack.enter_context(nc.sbuf_tensor("f1_y", [128, 16 * 512], BF16))
        h_sb = stack.enter_context(nc.sbuf_tensor("f1_h", [128, 64 * 512], BF16))
        oacc = stack.enter_context(nc.sbuf_tensor("f1_oa", [128, 8 * 512], F32))
        wband = stack.enter_context(nc.sbuf_tensor("f1_wb", [128, 4 * 1024], BF16))

        s_w = p.sem("f1_w")
        s_band = [p.sem("f1_b%d" % i) for i in range(4)]
        s_pe = p.sem("f1_pe")
        s_dv = p.sem("f1_dv")
        s_ac = p.sem("f1_ac")
        s_st = p.sem("f1_st")

        emit_ln_init(p, ws)

        @block.gpsimd
        def _(e):
            e.dma_start(out=mk_sb.rearrange("q (g m) -> q g m", g=32),
                        in_=wd["maker_bd"].rearrange("h c q m -> q (h c) m")).then_inc(s_w.h, 16)
            e.dma_start(out=up_sb.rearrange("q (g m) -> q g m", g=32),
                        in_=wd["up_bd"].rearrange("h c q m -> q (h c) m")).then_inc(s_w.h, 16)
        s_w.n += 32
        w_done = s_w.n

        @block.tensor
        def _(e):
            e.wait_ge(s_w.h, w_done)

        band_gen = [0] * 4
        wctbv = wd["wctb"]

        for tt in range(8):
            t0 = tt * 512
            prev_pe = s_pe.n
            prev_dv = s_dv.n
            emit_ln_tile(p, res_dram, ws, t0, xhat, 0,
                         pre_apply_waits=[(s_pe.h, prev_pe)],
                         pre_load_waits=[(s_dv.h, prev_dv)],
                         pre_stats_waits=[(s_dv.h, prev_dv)])
            xh_ready = ws["s_ac"].n

            for half in range(2):
                peY0, dvY0 = s_pe.n, s_dv.n

                @block.tensor
                def _(e, half=half, xh_ready=xh_ready, peY0=peY0, dvY0=dvY0):
                    e.wait_ge(ws["s_ac"].h, xh_ready)
                    if dvY0 > 0:
                        e.wait_ge(s_dv.h, dvY0)
                    for gi in range(16):
                        hh, ch = gi // 8, gi % 8
                        g = (half * 2 + hh) * 8 + ch
                        if gi >= 4:
                            e.wait_ge(s_dv.h, dvY0 + gi - 3)
                        e.matmul(ps[gi % 4][:], mk_sb[:, g * 128:(g + 1) * 128],
                                 xhat[:, ch * 512:(ch + 1) * 512],
                                 start=True, stop=True).then_inc(s_pe.h, 1)
                s_pe.n = peY0 + 16

                @block.vector
                def _(e, peY0=peY0):
                    for gi in range(16):
                        e.wait_ge(s_pe.h, peY0 + gi + 1)
                        e.tensor_copy(out=y_sb[:, gi * 512:(gi + 1) * 512],
                                      in_=ps[gi % 4][:]).then_inc(s_dv.h, 1)
                s_dv.n = dvY0 + 16
                dvY_end = s_dv.n

                peZ0, acZ0 = s_pe.n, s_ac.n

                @block.tensor
                def _(e, half=half, peZ0=peZ0, acZ0=acZ0, dvY_end=dvY_end):
                    e.wait_ge(s_dv.h, dvY_end)
                    for gi in range(16):
                        hh, ch = gi // 8, gi % 8
                        g = (half * 2 + hh) * 8 + ch
                        if gi >= 2:
                            e.wait_ge(s_ac.h, acZ0 + gi - 1)
                        bb = 4 * (gi % 2)
                        for m in range(4):
                            ins = e.matmul(ps[bb + m][:],
                                           up_sb[:, g * 512 + m * 128:g * 512 + m * 128 + 128],
                                           y_sb[:, gi * 512:(gi + 1) * 512],
                                           start=True, stop=True)
                        ins.then_inc(s_pe.h, 1)
                s_pe.n = peZ0 + 16

                @block.scalar
                def _(e, half=half, peZ0=peZ0):
                    for gi in range(16):
                        e.wait_ge(s_pe.h, peZ0 + gi + 1)
                        bb = 4 * (gi % 2)
                        for m in range(4):
                            kc = (gi // 8) * 32 + (gi % 8) * 4 + m
                            ins = e.activation(out=h_sb[:, kc * 512:(kc + 1) * 512],
                                               in_=ps[bb + m][:], func=AF.Gelu)
                        ins.then_inc(s_ac.h, 1)
                s_ac.n = acZ0 + 16
                ac_end = s_ac.n

                peC0 = s_pe.n
                bdma0 = [band_gen[r] for r in range(4)]

                @block.sync
                def _(e, half=half, peC0=peC0, bdma0=bdma0):
                    for kc in range(64):
                        r = kc % 4
                        gen = bdma0[r] + kc // 4
                        if kc >= 4:
                            e.wait_ge(s_pe.h, peC0 + kc - 3)
                        elif peC0 > 0:
                            e.wait_ge(s_pe.h, peC0)
                        e.dma_start(out=wband[:, r * 1024:(r + 1) * 1024],
                                    in_=wctbv[half * 8192 + kc * 128:half * 8192 + (kc + 1) * 128, :]
                                    ).then_inc(s_band[r].h, 16)
                for r in range(4):
                    band_gen[r] += 16
                    s_band[r].n += 16 * 16

                @block.tensor
                def _(e, half=half, peC0=peC0, ac_end=ac_end, bdma0=bdma0):
                    e.wait_ge(s_ac.h, ac_end)
                    for kc in range(64):
                        r = kc % 4
                        e.wait_ge(s_band[r].h, 16 * (bdma0[r] + kc // 4 + 1))
                        for m in range(8):
                            ins = e.matmul(ps[m][:],
                                           wband[:, r * 1024 + m * 128:r * 1024 + (m + 1) * 128],
                                           h_sb[:, kc * 512:(kc + 1) * 512],
                                           start=(kc == 0), stop=(kc == 63),
                                           skip_group_check=True)
                        ins.then_inc(s_pe.h, 1)
                s_pe.n = peC0 + 64
                peC_end = s_pe.n

                st_prev = s_st.n
                dv_half0 = s_dv.n

                @block.vector
                def _(e, half=half, peC_end=peC_end, st_prev=st_prev,
                      dv_half0=dv_half0):
                    e.wait_ge(s_pe.h, peC_end)
                    if half == 0:
                        if st_prev > 0:
                            e.wait_ge(s_st.h, st_prev)
                        for m in range(8):
                            ins = e.tensor_copy(out=oacc[:, m * 512:(m + 1) * 512], in_=ps[m][:])
                        ins.then_inc(s_dv.h, 1)
                    else:
                        e.wait_ge(s_dv.h, dv_half0)
                        for m in range(8):
                            ins = e.tensor_tensor(out=oacc[:, m * 512:(m + 1) * 512],
                                                  in0=oacc[:, m * 512:(m + 1) * 512],
                                                  in1=ps[m][:], op=ALU.add)
                        p.wmark("v", e, p.mark("v", ins))
                        for m in range(8):
                            ins = e.tensor_tensor(out=oacc[:, m * 512:(m + 1) * 512],
                                                  in0=oacc[:, m * 512:(m + 1) * 512],
                                                  in1=ws["rt"][:, m * 512:(m + 1) * 512],
                                                  op=ALU.add)
                        ins.then_inc(s_dv.h, 1)
                s_dv.n += 1

            dv_done = s_dv.n

            @block.sync
            def _(e, t0=t0, dv_done=dv_done):
                e.wait_ge(s_dv.h, dv_done)
                e.dma_start(out=rv[:, :, t0:t0 + 512],
                            in_=oacc.rearrange("q (j c) -> q j c", j=8)).then_inc(s_st.h, 16)
            s_st.n += 16

        @block.sync
        def _(e):
            e.wait_ge(s_st.h, s_st.n)



def emit_l0f(p, res_dram, wd, consts):
    nc, block = p.nc, p.block
    grow_d = nc.dram_tensor("l0f_grow", [14, 2048], F32, kind="Internal").ap()
    rv = res_dram.rearrange("(j q) c -> q j c", q=128)
    G_TOK = 2048
    N_GRP = NTOK // G_TOK
    N_TL = G_TOK // 512
    N_TB = G_TOK // 128

    with contextlib.ExitStack() as stack:
        ps = [stack.enter_context(nc.psum_tensor("f0_ps%d" % i, [128, 512], F32))
              for i in range(8)]
        ws = ln_workspace(p, stack, "f0", ps0=ps[0], ps1=ps[1])
        xhat = stack.enter_context(nc.sbuf_tensor("f0_xh", [128, 8 * G_TOK], BF16))
        acc = stack.enter_context(nc.sbuf_tensor("f0_acc", [128, 8 * G_TOK], F32))
        w1r = stack.enter_context(nc.sbuf_tensor("f0_w1", [128, 2 * 8192], BF16))
        w2r = stack.enter_context(nc.sbuf_tensor("f0_w2", [128, 8192], BF16))
        hb = stack.enter_context(nc.sbuf_tensor("f0_hb", [128, 8 * 512], BF16))
        gql = stack.enter_context(nc.sbuf_tensor("f0_gq", [128, G_TOK], BF16))
        gate_sb = stack.enter_context(nc.sbuf_tensor("f0_gt", [128, 8 * 14], BF16))
        gA = stack.enter_context(nc.sbuf_tensor("f0_gA", [128, 16 * 14], F32))
        gB = stack.enter_context(nc.sbuf_tensor("f0_gB", [128, 16 * 14], F32))
        gS = stack.enter_context(nc.sbuf_tensor("f0_gS", [128, 6 * 16], F32))
        gstg = stack.enter_context(nc.sbuf_tensor("f0_gr", [14, 128], F32))
        gtmpb = stack.enter_context(nc.sbuf_tensor("f0_gmb", [1, G_TOK], BF16))
        b1s = stack.enter_context(nc.sbuf_tensor("f0_b1", [128, 16], F32))
        b2s = stack.enter_context(nc.sbuf_tensor("f0_b2", [128, 8], F32))
        tm = stack.enter_context(nc.sbuf_tensor("f0_tm", [128, 512], F32))

        s_pe = p.sem("f0_pe")
        s_dv = p.sem("f0_dv")
        s_ac = p.sem("f0_ac")
        s_w1 = [p.sem("f0_w1a"), p.sem("f0_w1b")]
        s_w2 = [p.sem("f0_w2a"), p.sem("f0_w2b")]
        s_g = p.sem("f0_g")
        s_st = p.sem("f0_st")

        emit_ln_init(p, ws)

        @block.gpsimd
        def _(e):
            e.dma_start(out=gate_sb.rearrange("q (c n) -> q c n", c=8),
                        in_=wd["gateT"].rearrange("(c q) n -> q c n", q=128)
                        ).then_inc(s_g.h, 16)
        s_g.n += 16
        gate_done = s_g.n

        w1v = wd["w1_b"].rearrange("(e d) f -> e d f", e=14)
        w2v = wd["w2_b"].rearrange("(e f) d -> e f d", e=14)
        w_free = {"w1": [0, 0], "w2": [0]}

        for grp in range(N_GRP):
            gbase = grp * G_TOK

            for t in range(N_TL):
                prev_pe = s_pe.n
                prev_dv = s_dv.n
                emit_ln_tile(p, res_dram, ws, gbase + t * 512, xhat, t * 512,
                             xhat_f32=None, xh_stride=G_TOK,
                             acc_out=acc, acc_off=t * 512, acc_stride=G_TOK,
                             pre_apply_waits=[(s_pe.h, prev_pe), (s_st.h, s_st.n)],
                             pre_load_waits=[(s_dv.h, prev_dv)],
                             pre_stats_waits=[(s_dv.h, prev_dv), (s_ac.h, s_ac.n)])

            pe_g0 = s_pe.n
            ln_apply = ws["s_ac"].n

            @block.tensor
            def _(e, pe_g0=pe_g0, ln_apply=ln_apply, dv0=s_dv.n):
                e.wait_ge(ws["s_ac"].h, ln_apply)
                e.wait_ge(s_g.h, gate_done)
                if dv0 > 0:
                    e.wait_ge(s_dv.h, dv0)
                for tb in range(N_TB):
                    pb = ps[2 + tb % 2]
                    for k in range(8):
                        ins = e.matmul(pb[:, 0:14],
                                       xhat[:, k * G_TOK + tb * 128:k * G_TOK + (tb + 1) * 128],
                                       gate_sb[:, k * 14:(k + 1) * 14],
                                       start=(k == 0), stop=(k == 7))
                    ins.then_inc(s_pe.h, 1)
                    if tb >= 1:
                        e.wait_ge(s_dv.h, dv0 + 2 * tb)
            s_pe.n = pe_g0 + N_TB

            @block.vector
            def _(e, pe_g0=pe_g0, dv0=s_dv.n):
                for tb in range(N_TB):
                    pb = ps[2 + tb % 2]
                    e.wait_ge(s_pe.h, pe_g0 + tb + 1)
                    m1 = gS[:, 0 * 16 + tb:0 * 16 + tb + 1]
                    ins = e.reduce_max(out=m1, in_=pb[:, 0:14], axis=AX.X)
                    p.wmark("v", e, p.mark("v", ins))
                    ins = e.tensor_scalar(out=gB[:, tb * 14:(tb + 1) * 14], in0=pb[:, 0:14],
                                          scalar1=m1, scalar2=None, op0=ALU.is_ge)
                    p.wmark("v", e, p.mark("v", ins))
                    ins = e.scalar_tensor_tensor(out=gA[:, tb * 14:(tb + 1) * 14],
                                                 in0=gB[:, tb * 14:(tb + 1) * 14],
                                                 scalar=-1e30, in1=pb[:, 0:14],
                                                 op0=ALU.mult, op1=ALU.add)
                    ins.then_inc(s_dv.h, 1)
                    e.wait_ge(s_dv.h, dv0 + 2 * tb + 1)
                    m2 = gS[:, 1 * 16 + tb:1 * 16 + tb + 1]
                    ins = e.reduce_max(out=m2, in_=gA[:, tb * 14:(tb + 1) * 14], axis=AX.X)
                    p.wmark("v", e, p.mark("v", ins))
                    ins = e.tensor_tensor(out=gS[:, 2 * 16 + tb:2 * 16 + tb + 1],
                                          in0=m1, in1=m2, op=ALU.subtract)
                    ins.then_inc(s_dv.h, 1)
                p.wmark("v", e)
            s_dv.n += 2 * N_TB
            d_done = s_dv.n

            @block.scalar
            def _(e, d_done=d_done):
                e.wait_ge(s_dv.h, d_done)
                ins = e.activation(out=gS[:, 3 * 16:3 * 16 + N_TB], in_=gS[:, 2 * 16:2 * 16 + N_TB],
                                   func=AF.Sigmoid)
                ins.then_inc(s_ac.h, 1)
            s_ac.n += 1
            sig_done = s_ac.n

            @block.vector
            def _(e, sig_done=sig_done):
                e.wait_ge(s_ac.h, sig_done)
                ins = e.tensor_scalar(out=gS[:, 4 * 16:4 * 16 + N_TB], in0=gS[:, 3 * 16:3 * 16 + N_TB],
                                      scalar1=-1.0, scalar2=1.0, op0=ALU.mult, op1=ALU.add)
                p.wmark("v", e, p.mark("v", ins))
                for tb in range(N_TB):
                    w0 = gS[:, 3 * 16 + tb:3 * 16 + tb + 1]
                    w1_ = gS[:, 4 * 16 + tb:4 * 16 + tb + 1]
                    m2 = gS[:, 1 * 16 + tb:1 * 16 + tb + 1]
                    ins = e.tensor_scalar(out=gA[:, tb * 14:(tb + 1) * 14],
                                          in0=gA[:, tb * 14:(tb + 1) * 14],
                                          scalar1=m2, scalar2=w1_,
                                          op0=ALU.is_ge, op1=ALU.mult)
                    p.wmark("v", e, p.mark("v", ins))
                    ins = e.scalar_tensor_tensor(out=gB[:, tb * 14:(tb + 1) * 14],
                                                 in0=gB[:, tb * 14:(tb + 1) * 14], scalar=w0,
                                                 in1=gA[:, tb * 14:(tb + 1) * 14],
                                                 op0=ALU.mult, op1=ALU.add)
                    if tb < N_TB - 1:
                        p.wmark("v", e, p.mark("v", ins))
                    else:
                        ins.then_inc(s_dv.h, 1)
            s_dv.n += 1
            G_done = s_dv.n

            pe_t0 = s_pe.n

            @block.tensor
            def _(e, G_done=G_done, pe_t0=pe_t0, dvb=s_dv.n):
                e.wait_ge(s_dv.h, G_done)
                for tb in range(N_TB):
                    pb = ps[4 + tb % 2]
                    if tb >= 2:
                        e.wait_ge(s_dv.h, dvb + tb - 1)
                    ins = e.transpose(pb[0:14, 0:128], gB[:, tb * 14:(tb + 1) * 14],
                                      consts["ident_f32"][:])
                    ins.then_inc(s_pe.h, 1)
            s_pe.n = pe_t0 + N_TB

            st_gd0 = s_st.n

            @block.vector
            def _(e, pe_t0=pe_t0, g_all=s_g.n, st_gd0=st_gd0, dv0=s_dv.n):
                e.wait_ge(s_g.h, g_all)
                for tb in range(N_TB):
                    pb = ps[4 + tb % 2]
                    e.wait_ge(s_pe.h, pe_t0 + tb + 1)
                    if tb >= 1:
                        e.wait_ge(s_st.h, st_gd0 + 16 * tb)
                    ins = e.tensor_copy(out=gstg[:, :], in_=pb[0:14, 0:128])
                    ins.then_inc(s_dv.h, 1)
            s_dv.n += N_TB
            grow_cp0 = s_dv.n - N_TB

            @block.sync
            def _(e, grp=grp, grow_cp0=grow_cp0):
                for tb in range(N_TB):
                    e.wait_ge(s_dv.h, grow_cp0 + tb + 1)
                    e.dma_start(out=grow_d[:, tb * 128:(tb + 1) * 128],
                                in_=gstg[:, :]).then_inc(s_st.h, 16)
            s_st.n += 16 * N_TB
            grow_done_st = s_st.n
            grow_done = s_dv.n

            for ei in range(14):
                gld0 = s_g.n

                @block.gpsimd
                def _(e, ei=ei, grp=grp, grow_done=grow_done, dve_need=s_dv.n,
                      ac_need=s_ac.n):
                    if ei == 0:
                        e.wait_ge(s_dv.h, grow_done)
                    else:
                        e.wait_ge(s_dv.h, dve_need)
                    if ac_need > 0:
                        e.wait_ge(s_ac.h, ac_need)
                    e.wait_ge(s_st.h, grow_done_st)
                    e.dma_start(out=gtmpb[:], in_=grow_d[ei:ei + 1, :]).then_inc(s_g.h, 16)
                    e.dma_start(out=b1s[:], in_=wd["b1T"][ei]).then_inc(s_g.h, 16)
                    e.dma_start(out=b2s[:], in_=wd["b2T"][ei]).then_inc(s_g.h, 16)
                s_g.n += 48
                g_ld = s_g.n

                pe_b0 = s_pe.n

                @block.tensor
                def _(e, g_ld=g_ld, pe_b0=pe_b0, dv0=s_dv.n):
                    e.wait_ge(s_g.h, g_ld)
                    if dv0 > 0:
                        e.wait_ge(s_dv.h, dv0)
                    for t in range(N_TL):
                        pb = ps[2 + t % 2]
                        if t >= 2:
                            e.wait_ge(s_dv.h, dv0 + t - 1)
                        ins = e.matmul(pb[:], ws["ones1b"][:],
                                       gtmpb[:, t * 512:(t + 1) * 512], start=True, stop=True)
                        ins.then_inc(s_pe.h, 1)
                s_pe.n = pe_b0 + N_TL

                @block.vector
                def _(e, pe_b0=pe_b0):
                    for t in range(N_TL):
                        pb = ps[2 + t % 2]
                        e.wait_ge(s_pe.h, pe_b0 + t + 1)
                        ins = e.tensor_copy(out=gql[:, t * 512:(t + 1) * 512], in_=pb[:])
                        ins.then_inc(s_dv.h, 1)
                s_dv.n += N_TL
                gq_done = s_dv.n

                for fh in range(2):
                    slot = fh

                    @block.gpsimd
                    def _(e, ei=ei, fh=fh, slot=slot, grp=grp,
                          w1_need=w_free["w1"][fh]):
                        if w1_need > 0:
                            e.wait_ge(s_pe.h, w1_need)
                        e.dma_start(out=w1r[:, slot * 8192:(slot + 1) * 8192]
                                    .rearrange("q (kc c) -> q kc c", kc=8),
                                    in_=w1v[ei].rearrange("(kc q) f -> q kc f", q=128)
                                    [:, :, fh * 1024:(fh + 1) * 1024]).then_inc(s_w1[slot].h, 16)
                    s_w1[slot].n += 16

                for fh in range(2):
                    slot = fh
                    w1_ld = s_w1[slot].n

                    @block.gpsimd
                    def _(e, ei=ei, fh=fh, w2_need=max(w_free["w2"])):
                        if w2_need > 0:
                            e.wait_ge(s_pe.h, w2_need)
                        e.dma_start(out=w2r[:, 0:8192]
                                    .rearrange("q (kc c) -> q kc c", kc=8),
                                    in_=w2v[ei][fh * 1024:(fh + 1) * 1024, :]
                                    .rearrange("(kc q) d -> q kc d", q=128)).then_inc(s_w2[0].h, 16)
                    s_w2[0].n += 16
                    w2_ld = s_w2[0].n
                    for t in range(N_TL):
                        pe_h0 = s_pe.n
                        ac_h0 = s_ac.n
                        dv_h0 = s_dv.n

                        @block.tensor
                        def _(e, fh=fh, t=t, slot=slot, pe_h0=pe_h0, w1_ld=w1_ld,
                              w2_ld=w2_ld, gq_done=gq_done, dv_h0=dv_h0, ac_h0=ac_h0):
                            e.wait_ge(s_w1[slot].h, w1_ld)
                            if fh == 0 and t == 0:
                                e.wait_ge(s_dv.h, gq_done)

                            for w in range(2):
                                if dv_h0 > 0:
                                    e.wait_ge(s_dv.h, dv_h0 - (1 - w))
                                for m in range(4 * w, 4 * w + 4):
                                    for k in range(8):
                                        ins = e.matmul(
                                            ps[m][:],
                                            w1r[:, slot * 8192 + (k * 8 + m) * 128:
                                                slot * 8192 + (k * 8 + m) * 128 + 128],
                                            xhat[:, k * G_TOK + t * 512:k * G_TOK + (t + 1) * 512],
                                            start=(k == 0), stop=(k == 7),
                                            skip_group_check=True)
                                ins.then_inc(s_pe.h, 1)
                            e.wait_ge(s_w2[0].h, w2_ld)
                            for w in range(2):
                                e.wait_ge(s_ac.h, ac_h0 + 2)
                                for m in range(4 * w, 4 * w + 4):
                                    for k in range(8):
                                        ins = e.matmul(
                                            ps[m][:],
                                            w2r[:, (k * 8 + m) * 128:
                                                (k * 8 + m) * 128 + 128],
                                            hb[:, k * 512:(k + 1) * 512],
                                            start=(k == 0), stop=(k == 7),
                                            skip_group_check=True)
                                ins.then_inc(s_pe.h, 1)
                        s_pe.n = pe_h0 + 4
                        if t == N_TL - 1:
                            w_free["w1"][fh] = pe_h0 + 2
                            w_free["w2"][0] = s_pe.n

                        @block.scalar
                        def _(e, fh=fh, pe_h0=pe_h0, ac_h0=ac_h0, dv_h0=dv_h0):
                            for w in range(2):
                                e.wait_ge(s_pe.h, pe_h0 + w + 1)
                                if dv_h0 > 0:
                                    pass
                                for m in range(4 * w, 4 * w + 4):
                                    ins = e.activation(out=hb[:, m * 512:(m + 1) * 512],
                                                       in_=ps[m][:], func=AF.Gelu,
                                                       bias=b1s[:, fh * 8 + m:fh * 8 + m + 1])
                                ins.then_inc(s_ac.h, 1)
                        s_ac.n = ac_h0 + 2

                        @block.vector
                        def _(e, t=t, pe_h0=pe_h0, dv_h0=dv_h0, dvs0=s_dv.n):
                            e.wait_ge(s_dv.h, dvs0)
                            for w in range(2):
                                e.wait_ge(s_pe.h, pe_h0 + 2 + w + 1)
                                if w == 1:
                                    e.wait_ge(s_dv.h, dvs0 + 1)
                                for m in range(4 * w, 4 * w + 4):
                                    ins = e.tensor_mul(out=tm[:, 0:512],
                                                       in0=ps[m][:],
                                                       in1=gql[:, t * 512:(t + 1) * 512])
                                    p.wmark("v", e, p.mark("v", ins))
                                    ins = e.tensor_tensor(
                                        out=acc[:, m * G_TOK + t * 512:m * G_TOK + (t + 1) * 512],
                                        in0=acc[:, m * G_TOK + t * 512:m * G_TOK + (t + 1) * 512],
                                        in1=tm[:, 0:512], op=ALU.add)
                                    if m != 4 * w + 3:
                                        p.wmark("v", e, p.mark("v", ins))
                                ins.then_inc(s_dv.h, 1)
                        s_dv.n += 2

                @block.vector
                def _(e, dvp=s_dv.n):
                    p.wmark("v", e)
                    e.wait_ge(s_dv.h, dvp)
                    for m in range(8):
                        for t in range(N_TL):
                            ins = e.scalar_tensor_tensor(
                                out=acc[:, m * G_TOK + t * 512:m * G_TOK + (t + 1) * 512],
                                in0=gql[:, t * 512:(t + 1) * 512],
                                scalar=b2s[:, m:m + 1],
                                in1=acc[:, m * G_TOK + t * 512:m * G_TOK + (t + 1) * 512],
                                op0=ALU.mult, op1=ALU.add)
                    ins.then_inc(s_dv.h, 1)
                s_dv.n += 1

            all_exp = s_dv.n
            for t in range(N_TL):
                ld_t = None

                @block.sync
                def _(e, t=t, grp=grp, all_exp=all_exp, dvn=s_dv.n):
                    e.wait_ge(s_dv.h, dvn)
                    e.dma_start(out=ws["rt"].rearrange("q (j c) -> q j c", j=8),
                                in_=rv[:, :, grp * G_TOK + t * 512:grp * G_TOK + (t + 1) * 512]
                                ).then_inc(ws["s_ld"].h, 16)
                ws["s_ld"].n += 16
                rt_ld = ws["s_ld"].n

                @block.vector
                def _(e, t=t, rt_ld=rt_ld, all_exp=all_exp):
                    e.wait_ge(ws["s_ld"].h, rt_ld)
                    e.wait_ge(s_dv.h, all_exp)
                    p.wmark("v", e)
                    for m in range(8):
                        ins = e.tensor_scalar(out=tm[:, 0:512],
                                              in0=acc[:, m * G_TOK + t * 512:m * G_TOK + (t + 1) * 512],
                                              scalar1=-100.0, scalar2=100.0,
                                              op0=ALU.max, op1=ALU.min)
                        p.wmark("v", e, p.mark("v", ins))
                        ins = e.tensor_tensor(
                            out=acc[:, m * G_TOK + t * 512:m * G_TOK + (t + 1) * 512],
                            in0=ws["rt"][:, m * 512:(m + 1) * 512],
                            in1=tm[:, 0:512], op=ALU.add)
                        if m < 7:
                            p.wmark("v", e, p.mark("v", ins))
                    ins.then_inc(s_dv.h, 1)
                s_dv.n += 1
                wr_ok = s_dv.n

                @block.sync
                def _(e, t=t, grp=grp, wr_ok=wr_ok):
                    e.wait_ge(s_dv.h, wr_ok)
                    e.dma_start(out=rv[:, :, grp * G_TOK + t * 512:grp * G_TOK + (t + 1) * 512],
                                in_=acc.rearrange("q (j c) -> q j c", j=8)
                                [:, :, t * 512:(t + 1) * 512]).then_inc(s_st.h, 16)
                s_st.n += 16

        @block.sync
        def _(e):
            e.wait_ge(s_st.h, s_st.n)



QCH = [3] * 10 + [2]
QW_ = [w * 32 for w in QCH]


def emit_l1a(p, res_dram, wd, consts):
    nc, block = p.nc, p.block
    TQ = 2048

    with contextlib.ExitStack() as stack:
        psA = stack.enter_context(nc.psum_tensor("a1_psA", [128, 512], F32))
        pd0 = stack.enter_context(nc.psum_tensor("a1_pd0", [128, 1024], F32))
        pd1 = stack.enter_context(nc.psum_tensor("a1_pd1", [128, 1024], F32))
        psN = stack.enter_context(nc.psum_tensor("a1_psN", [128, 512], F32))
        tp0 = stack.enter_context(nc.psum_tensor("a1_tp0", [128, 512], BF16))
        ws = ln_workspace(p, stack, "a1", ps0=pd1[:, 0:512], ps1=pd1[:, 512:1024])
        qwin = stack.enter_context(nc.sbuf_tensor("a1_qw", [128, 11 * 512], BF16))
        kr = stack.enter_context(nc.sbuf_tensor("a1_kr", [128, 11 * TQ], BF16))
        dw = stack.enter_context(nc.sbuf_tensor("a1_dw", [128, 8 * 512], BF16))
        vte = stack.enter_context(nc.sbuf_tensor("a1_vt", [128, 16 * 16 * 65], BF16))
        xd = stack.enter_context(nc.sbuf_tensor("a1_xd", [128, 8 * TQ], BF16))
        wring = stack.enter_context(nc.sbuf_tensor("a1_wrg", [128, 2 * 1024], BF16))
        wprj = stack.enter_context(nc.sbuf_tensor("a1_wp", [128, 2 * 1024], BF16))
        rq = stack.enter_context(nc.sbuf_tensor("a1_rq", [128, 512], BF16))
        ropec = stack.enter_context(nc.sbuf_tensor("a1_rc", [96, 2 * 2048], BF16))
        perm = stack.enter_context(nc.sbuf_tensor("a1_pm", [128, 128], BF16))
        masks = stack.enter_context(nc.sbuf_tensor("a1_mk", [128, 4 * 512], BF16))
        prb = stack.enter_context(nc.sbuf_tensor("a1_pb", [128, 2 * 1024], BF16))
        dn = stack.enter_context(nc.sbuf_tensor("a1_dn", [128, 3 * 512], F32))
        asb = stack.enter_context(nc.sbuf_tensor("a1_asb", [128, 4 * 512], F32))
        srow2 = stack.enter_context(nc.sbuf_tensor("a1_s2", [1, 512], F32))
        srow2b = stack.enter_context(nc.sbuf_tensor("a1_s2b", [1, 2 * 512], BF16))
        lamv = stack.enter_context(nc.sbuf_tensor("a1_lv", [32, 4], F32))
        lamb = stack.enter_context(nc.sbuf_tensor("a1_lb", [128, 2], F32))
        c08 = None
        vsb = stack.enter_context(nc.sbuf_tensor("a1_vsb", [128, 512], BF16))

        s_pe = p.sem("a1_pe")
        s_dv = p.sem("a1_dv")
        s_ac = p.sem("a1_ac")
        s_w = p.sem("a1_w")
        s_wr = [p.sem("a1_wr0"), p.sem("a1_wr1")]
        s_st = p.sem("a1_st")
        wfree = {0: 0, 1: 0}
        obank = [pd0[:, 0:512], pd0[:, 512:1024], pd1[:, 0:512], pd1[:, 512:1024]]

        emit_ln_init(p, ws)

        @block.gpsimd
        def _(e):
            e.dma_start(out=ropec[:, 0:2048], in_=consts["kc2"]).then_inc(s_w.h, 16)
            e.dma_start(out=ropec[:, 2048:4096], in_=consts["ks2"]).then_inc(s_w.h, 16)
            e.dma_start(out=perm[:], in_=consts["perm96"]).then_inc(s_w.h, 16)
            e.dma_start(out=masks.rearrange("q (s c) -> q s c", s=4),
                        in_=consts["cmask"].rearrange("s q c -> q s c")).then_inc(s_w.h, 16)
            e.dma_start(out=lamv[:, 0:1], in_=wd["lq1"]).then_inc(s_w.h, 16)
            e.dma_start(out=lamv[:, 1:2], in_=wd["lk1"]).then_inc(s_w.h, 16)
            e.dma_start(out=lamv[:, 2:3], in_=wd["lq2"]).then_inc(s_w.h, 16)
            e.dma_start(out=lamv[:, 3:4], in_=wd["lk2"]).then_inc(s_w.h, 16)
        s_w.n += 8 * 16
        w_done = s_w.n

        @block.tensor
        def _(e):
            e.wait_ge(s_w.h, w_done)
            e.wait_ge(ws["s_dv"].h, ws["init_done"])
            e.matmul(psN[0:1, 0:1], lamv[:, 0:1], lamv[:, 1:2], start=True, stop=True)
            ins = e.matmul(psN[0:1, 1:2], lamv[:, 2:3], lamv[:, 3:4], start=True, stop=True)
            ins.then_inc(s_pe.h, 1)
        s_pe.n += 1
        lam_mm = s_pe.n

        @block.scalar
        def _(e, lam_mm=lam_mm):
            e.wait_ge(s_pe.h, lam_mm)
            ins = e.activation(out=lamb[0:1, 0:2], in_=psN[0:1, 0:2], func=AF.Exp)
            ins.then_inc(s_ac.h, 1)
        s_ac.n += 1
        lam_e = s_ac.n

        @block.vector
        def _(e, lam_e=lam_e):
            e.wait_ge(s_ac.h, lam_e)
            ins = e.tensor_tensor(out=lamb[0:1, 0:1], in0=lamb[0:1, 0:1],
                                  in1=lamb[0:1, 1:2], op=ALU.subtract)
            p.wmark("v", e, p.mark("v", ins))
            ins = e.tensor_scalar_add(out=lamb[0:1, 0:1], in0=lamb[0:1, 0:1], scalar1=0.2)
            p.wmark("v", e, p.mark("v", ins))
            ins = e.memset(lamb[:, 1:2], 0.0)
            ins.then_inc(s_dv.h, 1)
        s_dv.n += 1
        lam_done = s_dv.n

        @block.tensor
        def _(e, lam_done=lam_done):
            e.wait_ge(s_dv.h, lam_done)
            ins = e.matmul(psN[:, 0:1], ws["ones1"][:], lamb[0:1, 0:1],
                           start=True, stop=True)
            ins.then_inc(s_pe.h, 1)
        s_pe.n += 1
        lam_b = s_pe.n

        @block.vector
        def _(e, lam_b=lam_b):
            e.wait_ge(s_pe.h, lam_b)
            ins = e.tensor_copy(out=lamb[:, 1:2], in_=psN[:, 0:1])
            ins.then_inc(s_dv.h, 1)
        s_dv.n += 1

        wqkv = wd["wqkvp_b"]
        offs = []
        o = 0
        for wdt in QW_:
            offs.append((o, wdt)); o += wdt
        koffs = []
        for wdt in QW_:
            koffs.append((o, wdt)); o += wdt
        voffs = [(o + 128 * i, 128) for i in range(8)]

        for b in range(B_CORE):
            tb0 = b * TQ

            for t in range(4):
                prev_pe = s_pe.n
                prev_dv = s_dv.n
                emit_ln_tile(p, res_dram, ws, tb0 + t * 512, xd, t * 512,
                             xh_stride=TQ,
                             pre_apply_waits=[(s_pe.h, prev_pe), (s_st.h, s_st.n)],
                             pre_load_waits=[(s_dv.h, prev_dv), (s_st.h, s_st.n)],
                             pre_stats_waits=[(s_dv.h, prev_dv), (s_ac.h, s_ac.n)])
            xh_ready = ws["s_ac"].n

            specs = ([("k", c, koffs[c][0], koffs[c][1]) for c in range(11)]
                     + [("v", c, voffs[c][0], voffs[c][1]) for c in range(8)])
            def emit_proj_chunk(kind, c, co, cw, slot, tlist, qw=None):

                @block.gpsimd
                def _(e, co=co, cw=cw, slot=slot, wf=wfree[slot]):
                    if wf > 0:
                        e.wait_ge(s_pe.h, wf)
                    e.dma_start(out=wring[:, slot * 1024:slot * 1024 + 8 * cw]
                                .rearrange("q (k c) -> q k c", k=8),
                                in_=wqkv.rearrange("(k q) n -> q k n", q=128)
                                [:, :, co:co + cw]).then_inc(s_wr[slot].h, 16)
                s_wr[slot].n += 16
                wr_ld = s_wr[slot].n

                for t in tlist:
                    pe_c0 = s_pe.n

                    @block.tensor
                    def _(e, cw=cw, slot=slot, wr_ld=wr_ld, xh_ready=xh_ready,
                          t=t, dv_need=s_dv.n, ac_need=s_ac.n):
                        e.wait_ge(ws["s_ac"].h, xh_ready)
                        e.wait_ge(s_wr[slot].h, wr_ld)
                        if dv_need > 0:
                            e.wait_ge(s_dv.h, dv_need)
                        if ac_need > 0:
                            e.wait_ge(s_ac.h, ac_need)
                        for k in range(8):
                            ins = e.matmul(psA[0:cw, :],
                                           wring[:, slot * 1024 + k * cw:slot * 1024 + (k + 1) * cw],
                                           xd[:, k * TQ + t * 512:k * TQ + (t + 1) * 512],
                                           start=(k == 0), stop=(k == 7))
                        ins.then_inc(s_pe.h, 1)
                    s_pe.n = pe_c0 + 1
                    proj_t = s_pe.n

                    if kind in ("q", "k"):
                        @block.vector
                        def _(e, cw=cw, proj_t=proj_t, dvn=s_dv.n):
                            e.wait_ge(s_pe.h, proj_t)
                            if dvn > 0:
                                e.wait_ge(s_dv.h, dvn)
                            p.wmark("v", e)
                            ins = e.tensor_copy(out=rq[0:cw, :], in_=psA[0:cw, :])
                            ins.then_inc(s_dv.h, 1)
                        s_dv.n += 1
                        rq_done = s_dv.n

                        pe_p0 = s_pe.n

                        @block.tensor
                        def _(e, cw=cw, rq_done=rq_done, dvw=s_dv.n):
                            e.wait_ge(s_dv.h, rq_done)
                            ins = e.matmul(psN[0:cw, :], perm[0:cw, 0:cw], rq[0:cw, :],
                                           start=True, stop=True)
                            ins.then_inc(s_pe.h, 1)
                        s_pe.n = pe_p0 + 1
                        perm_done = s_pe.n

                        @block.vector
                        def _(e, kind=kind, c=c, cw=cw, t=t, perm_done=perm_done):
                            e.wait_ge(s_pe.h, perm_done)
                            tab = 0
                            ins = e.tensor_mul(out=dn[0:cw, 0:512], in0=rq[0:cw, :],
                                               in1=ropec[0:cw, tab + t * 512:tab + (t + 1) * 512])
                            p.wmark("v", e, p.mark("v", ins))
                            ins = e.tensor_mul(out=dn[0:cw, 512:1024], in0=psN[0:cw, :],
                                               in1=ropec[0:cw, tab + 2048 + t * 512:tab + 2048 + (t + 1) * 512])
                            p.wmark("v", e, p.mark("v", ins))
                            if kind == "q":
                                dst = qwin[:, c * 512:(c + 1) * 512]
                            else:
                                dst = kr[:, c * TQ + t * 512:c * TQ + (t + 1) * 512]
                            ins = e.tensor_tensor(out=dst[0:cw, :],
                                                  in0=dn[0:cw, 0:512], in1=dn[0:cw, 512:1024],
                                                  op=ALU.add)
                            ins.then_inc(s_dv.h, 1)
                        s_dv.n += 1
                    else:
                        @block.scalar
                        def _(e, proj_t=proj_t, dvn=s_dv.n):
                            e.wait_ge(s_pe.h, proj_t)
                            if dvn > 0:
                                e.wait_ge(s_dv.h, dvn)
                            ins = e.copy(out=vsb[:], in_=psA[:])
                            ins.then_inc(s_ac.h, 1)
                        s_ac.n += 1
                        v_ev = s_ac.n

                        pe_t0 = s_pe.n

                        @block.tensor
                        def _(e, v_ev=v_ev, pe_t0=pe_t0, dvn=s_dv.n,
                              ordn=p.sem("ord_v").n, acn=s_ac.n):
                            e.wait_ge(s_ac.h, v_ev)
                            if ordn > 0:
                                e.wait_ge(p.sem("ord_v").h, ordn)
                            for i4 in range(4):
                                if i4 >= 1:
                                    e.wait_ge(s_dv.h, dvn + i4)
                                ins = e.transpose(tp0[:, 0:128],
                                                  vsb[:, i4 * 128:(i4 + 1) * 128],
                                                  consts["ident_bf16"][:])
                                ins.then_inc(s_pe.h, 1)
                        s_pe.n = pe_t0 + 4

                        @block.vector
                        def _(e, c=c, t=t, pe_t0=pe_t0):
                            for i4 in range(4):
                                e.wait_ge(s_pe.h, pe_t0 + i4 + 1)
                                kt = t * 4 + i4
                                vh0, vh1 = 2 * c, 2 * c + 1
                                ins = e.tensor_copy(
                                    out=vte[:, (vh0 * 16 + kt) * 65:(vh0 * 16 + kt) * 65 + 64],
                                    in_=tp0[:, 0:64])
                                p.wmark("v", e, p.mark("v", ins))
                                ins = e.tensor_copy(
                                    out=vte[:, (vh1 * 16 + kt) * 65:(vh1 * 16 + kt) * 65 + 64],
                                    in_=tp0[:, 64:128])
                                ins.then_inc(s_dv.h, 1)
                        s_dv.n += 4
                wfree[slot] = s_pe.n

            for si, (kind, c, co, cw) in enumerate(specs):
                emit_proj_chunk(kind, c, co, cw, si % 2, [0, 1, 2, 3])

            @block.vector
            def _(e, dvn=s_dv.n):
                e.wait_ge(s_dv.h, dvn)
                va = bcast_ap(vte[:, :], 64, [[16 * 16 * 65, 128], [65, 256], [0, 1]])
                ins = e.memset(va, 1.0)
                ins.then_inc(s_dv.h, 1)
            s_dv.n += 1
            vones = s_dv.n

            for qw in range(4):
                KT = 4 * (qw + 1)
                for c in range(11):
                    emit_proj_chunk("q", c, offs[c][0], offs[c][1], c % 2, [qw], qw=qw)
                for j in range(8):
                    heads = [2 * j, 2 * j + 1, 2 * j + 16, 2 * j + 17]
                    pe_q0 = s_pe.n
                    ac_q0 = s_ac.n
                    dv_q0 = s_dv.n
                    NP = KT // 2
                    pdx = [pd0, pd1]
                    sc_idx = {}
                    pv_idx = {}
                    ev_idx = {}
                    cnt = 0
                    for hq in range(4):
                        for ktp in range(NP):
                            cnt += 2
                            sc_idx[(hq, ktp)] = cnt
                            if ktp >= 1:
                                cnt += 2
                                pv_idx[(hq, ktp - 1)] = cnt
                        cnt += 2
                        pv_idx[(hq, NP - 1)] = cnt
                    diag_ord = {}
                    nd = 0
                    diag_pref = []
                    for hq in range(4):
                        for ktp in range(NP):
                            if 2 * ktp + 1 >= qw * 4:
                                nd += 1
                                diag_ord[(hq, ktp)] = nd
                        diag_pref.append(nd)
                    ev_dv = [dv_q0 + 1 + diag_pref[hq] + hq + 1 for hq in range(4)]

                    @block.tensor
                    def _(e, qw=qw, heads=heads, KT=KT, NP=NP, pe_q0=pe_q0,
                          ac_q0=ac_q0, dv_q0=dv_q0, vones=vones,
                          diag_ord=dict(diag_ord), ordn=p.sem("ord_v").n,
                          ev_dv=list(ev_dv)):
                        e.wait_ge(s_dv.h, vones)
                        e.wait_ge(s_dv.h, dv_q0)
                        if ac_q0 > 0:
                            e.wait_ge(s_ac.h, ac_q0)
                        if ordn > 0:
                            e.wait_ge(p.sem("ord_v").h, ordn)

                        def pv(hq, ktp):
                            h = heads[hq]
                            vh = h % 16
                            if ktp == 0 and hq >= 1:
                                e.wait_ge(s_dv.h, ev_dv[hq - 1])
                            if (hq, ktp) in diag_ord:
                                e.wait_ge(s_dv.h, dv_q0 + 1 + hq + diag_ord[(hq, ktp)])
                            else:
                                pg = hq * NP + ktp
                                e.wait_ge(s_ac.h, ac_q0 + pg + 1)
                            for u in range(2):
                                kt = 2 * ktp + u
                                e.matmul(psA[0:65, :],
                                         vte[:, (vh * 16 + kt) * 65:(vh * 16 + kt + 1) * 65],
                                         prb[:, (ktp % 2) * 1024 + u * 512:
                                             (ktp % 2) * 1024 + (u + 1) * 512],
                                         start=(kt == 0), stop=(kt == KT - 1),
                                         skip_group_check=True).then_inc(s_pe.h, 1)

                        for hq in range(4):
                            h = heads[hq]
                            cq, off = h // 3, (h % 3) * 32
                            for ktp in range(NP):
                                pg = hq * NP + ktp
                                if pg >= 2:
                                    e.wait_ge(s_ac.h, ac_q0 + pg - 1)
                                for u in range(2):
                                    kt = 2 * ktp + u
                                    e.matmul(pdx[ktp % 2][:, u * 512:(u + 1) * 512],
                                             kr[:, cq * TQ + kt * 128:cq * TQ + (kt + 1) * 128]
                                             [off:off + 32, :],
                                             qwin[:, cq * 512:(cq + 1) * 512]
                                             [off:off + 32, :],
                                             start=True, stop=True).then_inc(s_pe.h, 1)
                                if ktp >= 1:
                                    pv(hq, ktp - 1)
                            pv(hq, NP - 1)
                    s_pe.n = pe_q0 + 4 * (2 * KT)

                    @block.scalar
                    def _(e, KT=KT, NP=NP, qw=qw, pe_q0=pe_q0, ac_q0=ac_q0,
                          sc_idx=dict(sc_idx), pv_idx=dict(pv_idx)):
                        for hq in range(4):
                            for ktp in range(NP):
                                pg = hq * NP + ktp
                                e.wait_ge(s_pe.h, pe_q0 + sc_idx[(hq, ktp)])
                                if pg >= 2:
                                    c0, c1 = divmod(pg - 2, NP)
                                    e.wait_ge(s_pe.h, pe_q0 + pv_idx[(c0, c1)])
                                ins = e.activation(
                                    out=prb[:, (ktp % 2) * 1024:(ktp % 2) * 1024 + 1024],
                                    in_=pdx[ktp % 2][:, :], func=AF.Exp,
                                    scale=float(1.0 / np.sqrt(32.0)))
                                ins.then_inc(s_ac.h, 1)
                    s_ac.n = ac_q0 + 4 * NP

                    @block.vector
                    def _(e, KT=KT, NP=NP, qw=qw, heads=heads, ac_q0=ac_q0, dv_q0=dv_q0,
                          pe_q0=pe_q0, pv_idx=dict(pv_idx), diag_ord=dict(diag_ord),
                          ordn=p.sem("ord_v").n):
                        if ordn > 0:
                            e.wait_ge(p.sem("ord_v").h, ordn)
                        e.engine_nop().then_inc(s_dv.h, 1)
                        for hq in range(4):
                            for ktp in range(NP):
                                if (hq, ktp) in diag_ord:
                                    pg = hq * NP + ktp
                                    e.wait_ge(s_ac.h, ac_q0 + pg + 1)
                                    kt0 = 2 * ktp
                                    sm = (kt0 - qw * 4)
                                    ins = e.tensor_mul(
                                        out=prb[:, (ktp % 2) * 1024:(ktp % 2) * 1024 + 1024],
                                        in0=prb[:, (ktp % 2) * 1024:(ktp % 2) * 1024 + 1024],
                                        in1=masks[:, sm * 512:(sm + 2) * 512])
                                    ins.then_inc(s_dv.h, 1)
                            e.wait_ge(s_pe.h, pe_q0 + pv_idx[(hq, NP - 1)])
                            ins = e.tensor_copy(out=asb[0:65, hq * 512:hq * 512 + 512],
                                                in_=psA[0:65, :])
                            ins.then_inc(s_dv.h, 1)
                    s_dv.n = dv_q0 + 1 + nd + 4
                    a_evac = s_dv.n

                    for pi in range(2):
                        h0 = heads[pi]
                        emit_l1a_pair(p, asb, pi, pi + 2, h0, qw, b, dw, dn, srow2,
                                      lamb, c08, ws, consts, a_evac, s_pe, s_dv, s_ac, TQ,
                                      ps2=psN, srow2b=srow2b)

                dasm_done = p.sem("ord_v").n
                t = qw
                pe_o0 = s_pe.n

                wp_ld = []
                for m in range(8):
                    slot = m % 2

                    @block.gpsimd
                    def _(e, m=m, slot=slot, wf=wfree[slot], pe_o0=pe_o0):
                        if m >= 2:
                            e.wait_ge(s_pe.h, pe_o0 + m - 1)
                        elif wf > 0:
                            e.wait_ge(s_pe.h, wf)
                        e.dma_start(out=wprj[:, slot * 1024:(slot + 1) * 1024]
                                    .rearrange("q (k c) -> q k c", k=8),
                                    in_=wd["wproj_b"].rearrange("(k q) n -> q k n", q=128)
                                    [:, :, m * 128:(m + 1) * 128]).then_inc(s_wr[slot].h, 16)
                    s_wr[slot].n += 16
                    wp_ld.append(s_wr[slot].n)

                @block.tensor
                def _(e, t=t, pe_o0=pe_o0, dasm_done=dasm_done, dvn=s_dv.n, b=b,
                      dv_w1=s_dv.n + 1, wp_ld=tuple(wp_ld)):
                    e.wait_ge(p.sem("ord_v").h, dasm_done)
                    if dvn > 0:
                        e.wait_ge(s_dv.h, dvn)
                    for m in range(8):
                        slot = m % 2
                        e.wait_ge(s_wr[slot].h, wp_ld[m])
                        if m == 4:
                            e.wait_ge(s_dv.h, dv_w1)
                        for k in range(8):
                            ins = e.matmul(obank[m % 4][:, :],
                                           wprj[:, slot * 1024 + k * 128:slot * 1024 + (k + 1) * 128],
                                           dw[:, k * 512:(k + 1) * 512],
                                           start=(k == 0), stop=(k == 7),
                                           skip_group_check=True)
                        ins.then_inc(s_pe.h, 1)
                s_pe.n = pe_o0 + 8
                wfree[0] = s_pe.n
                wfree[1] = s_pe.n

                @block.sync
                def _(e, t=t, b=b, lds=ws["s_ld"].n, dvn=s_dv.n, ordn=p.sem("ord_v").n,
                      stn=s_st.n):
                    if dvn > 0:
                        e.wait_ge(s_dv.h, dvn)
                    if ordn > 0:
                        e.wait_ge(p.sem("ord_v").h, ordn)
                    if stn > 0:
                        e.wait_ge(s_st.h, stn)
                    e.dma_start(out=ws["rt"].rearrange("q (j c) -> q j c", j=8),
                                in_=res_dram.rearrange("(j q) c -> q j c", q=128)
                                [:, :, b * TQ + t * 512:b * TQ + (t + 1) * 512]
                                ).then_inc(ws["s_ld"].h, 16)
                ws["s_ld"].n += 16
                rt_ld = ws["s_ld"].n

                @block.vector
                def _(e, t=t, pe_o0=pe_o0, rt_ld=rt_ld):
                    e.wait_ge(ws["s_ld"].h, rt_ld)
                    for mw in range(2):
                        e.wait_ge(s_pe.h, pe_o0 + 4 * (mw + 1))
                        for m4 in range(4):
                            m = mw * 4 + m4
                            ins = e.tensor_tensor(out=ws["rt"][:, m * 512:(m + 1) * 512],
                                                  in0=ws["rt"][:, m * 512:(m + 1) * 512],
                                                  in1=obank[m4][:, :], op=ALU.add)
                            if mw == 0 and m4 == 3:
                                ins.then_inc(s_dv.h, 1)
                            elif not (mw == 1 and m4 == 3):
                                p.wmark("v", e, p.mark("v", ins))
                    ins.then_inc(s_dv.h, 1)
                s_dv.n += 2
                wr_ok = s_dv.n

                @block.sync
                def _(e, t=t, b=b, wr_ok=wr_ok):
                    e.wait_ge(s_dv.h, wr_ok)
                    e.dma_start(out=res_dram.rearrange("(j q) c -> q j c", q=128)
                                [:, :, b * TQ + t * 512:b * TQ + (t + 1) * 512],
                                in_=ws["rt"].rearrange("q (j c) -> q j c", j=8)
                                ).then_inc(s_st.h, 16)
                s_st.n += 16

        @block.sync
        def _(e):
            e.wait_ge(s_st.h, s_st.n)


def emit_l1a_pair(p, asb, ia, ib, h0, qw, b, xd, dn, srow2, lamb, c08, ws,
                  consts, a_evac, s_pe, s_dv, s_ac, TQ, ps2=None, srow2b=None):
    nc, block = p.nc, p.block
    A1 = asb[0:64, ia * 512:ia * 512 + 512]
    A2 = asb[0:64, ib * 512:ib * 512 + 512]
    s1 = asb[64:65, ia * 512:ia * 512 + 512]
    s2 = asb[64:65, ib * 512:ib * 512 + 512]

    @block.scalar
    def _(e, a_evac=a_evac, acn=s_ac.n, dvn=s_dv.n):
        e.wait_ge(s_dv.h, max(a_evac, dvn))
        ins = e.activation(out=srow2[:, 0:512], in_=s1, func=AF.Ln)
        p.wmark("s", e, p.mark("s", ins))
        ins = e.activation(out=srow2[:, 0:512], in_=srow2[:, 0:512], func=AF.Exp,
                           scale=-1.0)
        ins.then_inc(s_ac.h, 1)
    s_ac.n += 1
    r1_done = s_ac.n

    @block.vector
    def _(e, r1_done=r1_done, pen=s_pe.n):
        e.wait_ge(s_ac.h, r1_done)
        e.wait_ge(s_pe.h, pen)
        p.wmark("v", e)
        ins = e.tensor_copy(out=srow2b[:, 0:512], in_=srow2[:, 0:512])
        ins.then_inc(s_dv.h, 1)
    s_dv.n += 1
    r1b_done = s_dv.n

    @block.scalar
    def _(e, r1b_done=r1b_done):
        e.wait_ge(s_dv.h, r1b_done)
        ins = e.activation(out=srow2[:, 0:512], in_=s2, func=AF.Ln)
        p.wmark("s", e, p.mark("s", ins))
        ins = e.activation(out=srow2[:, 0:512], in_=srow2[:, 0:512], func=AF.Exp,
                           scale=-1.0)
        ins.then_inc(s_ac.h, 1)
    s_ac.n += 1
    r2_done = s_ac.n

    @block.vector
    def _(e, r2_done=r2_done):
        e.wait_ge(s_ac.h, r2_done)
        p.wmark("v", e)
        ins = e.tensor_scalar_mul(out=srow2b[:, 512:1024], in0=srow2[:, 0:512],
                                  scalar1=lamb[0:1, 0:1])
        ins.then_inc(s_dv.h, 1)
    s_dv.n += 1
    rec_done = s_dv.n

    pe_b0 = s_pe.n

    @block.tensor
    def _(e, rec_done=rec_done, pe_b0=pe_b0, ordn=p.sem("ord_v").n):
        e.wait_ge(s_dv.h, rec_done)
        if ordn > 0:
            e.wait_ge(p.sem("ord_v").h, ordn)
        e.matmul(ps2[0:64, 0:512], ws["ones1b"][:, 0:64], srow2b[:, 0:512],
                 start=True, stop=True).then_inc(s_pe.h, 1)
        ins = e.matmul(ps2[64:128, 0:512], ws["ones1b"][:, 64:128], srow2b[:, 512:1024],
                       start=True, stop=True)
        ins.then_inc(s_pe.h, 1)
    s_pe.n = pe_b0 + 2
    bc_done = s_pe.n

    @block.vector
    def _(e, bc_done=bc_done):
        e.wait_ge(s_pe.h, bc_done)
        p.wmark("v", e)
        ins = e.tensor_mul(out=dn[0:64, 1024:1536], in0=A1, in1=ps2[0:64, 0:512])
        p.wmark("v", e, p.mark("v", ins))
        ins = e.tensor_mul(out=dn[0:64, 0:512], in0=A2, in1=ps2[64:128, 0:512])
        p.wmark("v", e, p.mark("v", ins))
        ins = e.tensor_tensor(out=dn[0:64, 1024:1536], in0=dn[0:64, 1024:1536],
                              in1=dn[0:64, 0:512], op=ALU.subtract)
        p.wmark("v", e, p.mark("v", ins))
        ins = e.tensor_mul(out=dn[0:64, 0:512], in0=dn[0:64, 1024:1536],
                           in1=dn[0:64, 1024:1536])
        ins.then_inc(s_dv.h, 1)
    s_dv.n += 1
    sq_done = s_dv.n

    pe_m0 = s_pe.n

    @block.tensor
    def _(e, sq_done=sq_done, pe_m0=pe_m0):
        e.wait_ge(s_dv.h, sq_done)
        e.matmul(ps2[0:1, 0:512], ws["onesc"][0:64, :], dn[0:64, 0:512],
                 start=True, stop=True).then_inc(s_pe.h, 1)
    s_pe.n = pe_m0 + 1

    @block.scalar
    def _(e, pe_m0=pe_m0):
        e.wait_ge(s_pe.h, pe_m0 + 1)
        ins = e.activation(out=srow2[:, 0:512], in_=ps2[0:1, 0:512], func=AF.Ln,
                           bias=ws["eps1"][:], scale=1.0 / 64)
        p.wmark("s", e, p.mark("s", ins))
        ins = e.activation(out=srow2[:, 0:512], in_=srow2[:, 0:512], func=AF.Exp,
                           scale=-0.5)
        ins.then_inc(s_ac.h, 1)
    s_ac.n += 1
    rstd_done = s_ac.n

    @block.vector
    def _(e, rstd_done=rstd_done):
        e.wait_ge(s_ac.h, rstd_done)
        p.wmark("v", e)
        ins = e.tensor_scalar_mul(out=srow2b[:, 0:512], in0=srow2[:, 0:512],
                                  scalar1=0.8)
        ins.then_inc(s_dv.h, 1)
    s_dv.n += 1
    rstdb = s_dv.n

    pe_m1 = s_pe.n

    @block.tensor
    def _(e, rstdb=rstdb, pe_m1=pe_m1):
        e.wait_ge(s_dv.h, rstdb)
        e.matmul(ps2[0:64, 0:512], ws["ones1b"][:, 0:64], srow2b[:, 0:512],
                 start=True, stop=True).then_inc(s_pe.h, 1)
    s_pe.n = pe_m1 + 1

    @block.vector
    def _(e, pe_m1=pe_m1, h0=h0):
        e.wait_ge(s_pe.h, pe_m1 + 1)
        ch, ro = h0 // 2, (h0 % 2) * 64
        ins = e.tensor_mul(
            out=xd[:, ch * 512:(ch + 1) * 512][ro:ro + 64, :],
            in0=dn[0:64, 1024:1536], in1=ps2[0:64, 0:512])
        p.wmark("v", e, p.mark("v", ins))



MIXC = 0.05


def emit_l0a(p, res_dram, wd, consts):
    nc, block = p.nc, p.block
    TQ = 2048
    C = 16

    qt_d = nc.dram_tensor("l0a_qt", [2, TQ, 1024], BF16, kind="Internal").ap()
    kt_d = nc.dram_tensor("l0a_kt", [2, TQ, 1024], BF16, kind="Internal").ap()
    vt_d = nc.dram_tensor("l0a_vt", [2, TQ, 1024], BF16, kind="Internal").ap()
    ut_d = nc.dram_tensor("l0a_ut", [2, TQ, 1024], BF16, kind="Internal").ap()
    ot_d = nc.dram_tensor("l0a_ot", [2, TQ, 1024], BF16, kind="Internal").ap()
    fs_d = nc.dram_tensor("l0a_fs", [2, TQ, 48], F32, kind="Internal").ap()

    with contextlib.ExitStack() as stack:
        ps = [stack.enter_context(nc.psum_tensor("a0_ps%d" % i, [128, 512], F32))
              for i in range(6)]
        ws = ln_workspace(p, stack, "a0", ps0=ps[0], ps1=ps[1])
        scur = stack.enter_context(nc.sbuf_tensor("a0_scur", [128, 2 * 16], F32))
        elan = stack.enter_context(nc.sbuf_tensor("a0_el", [128, 2 * 2 * 8 * 64], F32))
        slan = stack.enter_context(nc.sbuf_tensor("a0_sl", [128, 2 * 2 * 8 * 64], F32))
        rlan = stack.enter_context(nc.sbuf_tensor("a0_rl", [128, 4 * 2 * 8 * 64], F32))
        rten = stack.enter_context(nc.sbuf_tensor("a0_rt4", [128, 4 * 16], F32))
        rscr = stack.enter_context(nc.sbuf_tensor("a0_rsc", [128, 64], F32))
        mcon = stack.enter_context(nc.sbuf_tensor("a0_mc", [128, 16], F32))
        rrow = stack.enter_context(nc.sbuf_tensor("a0_rr", [16, 4 * 2 * 64], F32))
        selm = stack.enter_context(nc.sbuf_tensor("a0_sel", [16, 8 * 128], F32))
        wbfs = stack.enter_context(nc.sbuf_tensor("a0_wbfs", [128, 8 * 48], BF16))
        bfs = stack.enter_context(nc.sbuf_tensor("a0_bfs", [128, 32], F32))
        fssc = stack.enter_context(nc.sbuf_tensor("a0_fs", [128, 16 * 48], F32))
        sfst = stack.enter_context(nc.sbuf_tensor("a0_sf", [128, 1024], F32))
        ssst = stack.enter_context(nc.sbuf_tensor("a0_ss", [128, 1024], F32))
        d1t = stack.enter_context(nc.sbuf_tensor("a0_d1", [128, 1024], F32))
        d2t = stack.enter_context(nc.sbuf_tensor("a0_d2", [128, 1024], F32))

        s_pe = p.sem("a0_pe")
        s_dv = p.sem("a0_dv")
        s_ac = p.sem("a0_ac")
        s_gp = p.sem("a0_gp")
        s_w = p.sem("a0_w")
        s_wr = [p.sem("a0_wr0"), p.sem("a0_wr1")]
        s_ld2 = [p.sem("a0_ld0"), p.sem("a0_ld1")]
        s_stg = [p.sem("a0_sg0"), p.sem("a0_sg1")]
        s_st = p.sem("a0_st")
        wfree = {0: 0, 1: 0}

        emit_ln_init(p, ws)

        @block.gpsimd
        def _(e):
            e.dma_start(out=wbfs.rearrange("q (k n) -> q k n", k=8),
                        in_=wd["wbfsT"].rearrange("(k q) n -> q k n", q=128)
                        ).then_inc(s_w.h, 16)
            e.dma_start(out=bfs[:, 0:16], in_=wd["bfdT"]).then_inc(s_w.h, 16)
            e.dma_start(out=bfs[:, 16:32], in_=wd["bsdT"]).then_inc(s_w.h, 16)
            e.dma_start(out=selm.rearrange("q (l c) -> q l c", l=8),
                        in_=consts["sel2h"].rearrange("l q c -> q l c")).then_inc(s_w.h, 16)
        s_w.n += 64
        w_done = s_w.n

        def emit_P(b, half):
            hb0 = half * 1024
            sfx = "_%d_%d" % (b, half)
            with contextlib.ExitStack() as st2:
                xh = st2.enter_context(nc.sbuf_tensor("a0_xh" + sfx, [128, 8 * 1024], BF16))
                wring = st2.enter_context(nc.sbuf_tensor("a0_wrg" + sfx, [128, 2 * 8 * 512], BF16))
                stg = st2.enter_context(nc.sbuf_tensor("a0_stg" + sfx, [128, 1024], BF16))
                fst = st2.enter_context(nc.sbuf_tensor("a0_fst" + sfx, [128, 48], F32))

                for t5 in range(2):
                    prev_pe = s_pe.n
                    prev_dv = s_dv.n
                    emit_ln_tile(p, res_dram, ws, b * TQ + hb0 + t5 * 512, xh, t5 * 512,
                                 xh_stride=1024,
                                 pre_apply_waits=[(s_pe.h, prev_pe), (s_st.h, s_st.n)],
                                 pre_load_waits=[(s_dv.h, prev_dv), (s_st.h, s_st.n)],
                                 pre_stats_waits=[(s_dv.h, prev_dv), (s_ac.h, s_ac.n)])
                xh_ready = ws["s_ac"].n

                for pi in range(6):
                    proj = pi // 2
                    oh = pi % 2
                    slot = pi % 2
                    dname = [qt_d, kt_d, vt_d][proj]

                    @block.gpsimd
                    def _(e, pi=pi, proj=proj, oh=oh, slot=slot, wf=wfree[slot],
                          stn=(s_st.n if pi == 0 else 0), dvn=(s_dv.n if pi == 0 else 0),
                          pc=p.pc_info):
                        if wf > 0:
                            e.wait_ge(s_pe.h, wf)
                        if stn > 0:
                            e.wait_ge(s_st.h, stn)
                        if dvn > 0:
                            e.wait_ge(s_dv.h, dvn)
                        if pc is not None:
                            e.wait_ge(pc["sems"][0].h, pc["early"][0])
                            e.wait_ge(pc["sems"][1].h, pc["early"][1])
                        e.dma_start(out=wring[:, slot * 4096:(slot + 1) * 4096]
                                    .rearrange("q (k c) -> q k c", k=8),
                                    in_=wd["wqkvT_b"].rearrange("(k q) n -> q k n", q=128)
                                    [:, :, proj * 1024 + oh * 512:proj * 1024 + oh * 512 + 512]
                                    ).then_inc(s_wr[slot].h, 16)
                    s_wr[slot].n += 16
                    wr_ld = s_wr[slot].n

                    for t8 in range(8):
                        pe_c0 = s_pe.n
                        bank = 2 + (t8 % 2)

                        @block.tensor
                        def _(e, slot=slot, t8=t8, bank=bank, wr_ld=wr_ld,
                              xh_ready=xh_ready, acn=s_ac.n):
                            e.wait_ge(ws["s_ac"].h, xh_ready)
                            e.wait_ge(s_wr[slot].h, wr_ld)
                            if acn > 0:
                                e.wait_ge(s_ac.h, acn)
                            for k in range(8):
                                ins = e.matmul(ps[bank][:, :],
                                               xh[:, k * 1024 + t8 * 128:k * 1024 + (t8 + 1) * 128],
                                               wring[:, slot * 4096 + k * 512:slot * 4096 + (k + 1) * 512],
                                               start=(k == 0), stop=(k == 7))
                            ins.then_inc(s_pe.h, 1)
                        s_pe.n = pe_c0 + 1
                        mm_done = s_pe.n

                        sslot = t8 % 2

                        @block.scalar
                        def _(e, bank=bank, mm_done=mm_done, sslot=sslot,
                              stn=s_stg[t8 % 2].n):
                            e.wait_ge(s_pe.h, mm_done)
                            if stn > 0:
                                e.wait_ge(s_stg[sslot].h, stn)
                            ins = e.activation(out=stg[:, sslot * 512:sslot * 512 + 512],
                                               in_=ps[bank][:, :], func=AF.Silu)
                            ins.then_inc(s_ac.h, 1)
                        s_ac.n += 1
                        ev_done = s_ac.n

                        @block.sync
                        def _(e, b=b, hb0=hb0, t8=t8, oh=oh, dname=dname, ev_done=ev_done,
                              sslot=sslot):
                            e.wait_ge(s_ac.h, ev_done)
                            e.dma_start(out=dname[b, hb0 + t8 * 128:hb0 + (t8 + 1) * 128,
                                                  oh * 512:oh * 512 + 512],
                                        in_=stg[:, sslot * 512:sslot * 512 + 512]
                                        ).then_inc(s_stg[sslot].h, 16)
                        s_stg[sslot].n += 16
                    wfree[slot] = s_pe.n

                for t8 in range(8):
                    pe_c0 = s_pe.n

                    @block.tensor
                    def _(e, t8=t8, xh_ready=xh_ready, wdn=w_done, dvn=s_dv.n):
                        e.wait_ge(ws["s_ac"].h, xh_ready)
                        e.wait_ge(s_w.h, wdn)
                        if dvn > 0:
                            e.wait_ge(s_dv.h, dvn)
                        for k in range(8):
                            ins = e.matmul(ps[4][:, 0:48],
                                           xh[:, k * 1024 + t8 * 128:k * 1024 + (t8 + 1) * 128],
                                           wbfs[:, k * 48:(k + 1) * 48],
                                           start=(k == 0), stop=(k == 7))
                        ins.then_inc(s_pe.h, 1)
                    s_pe.n = pe_c0 + 1
                    mm_done = s_pe.n

                    @block.vector
                    def _(e, mm_done=mm_done, acn=s_ac.n, stn=s_st.n):
                        e.wait_ge(s_pe.h, mm_done)
                        p.wmark("v", e)
                        if acn > 0:
                            e.wait_ge(s_ac.h, acn)
                        if stn > 0:
                            e.wait_ge(s_st.h, stn)
                        ins = e.tensor_tensor(out=fst[:, 16:32], in0=ps[4][:, 16:32],
                                              in1=bfs[:, 0:16], op=ALU.add)
                        p.wmark("v", e, p.mark("v", ins))
                        ins = e.tensor_tensor(out=fst[:, 32:48], in0=ps[4][:, 32:48],
                                              in1=bfs[:, 16:32], op=ALU.add)
                        p.wmark("v", e, p.mark("v", ins))
                        ins = e.tensor_copy(out=fst[:, 0:16], in_=ps[4][:, 0:16])
                        ins.then_inc(s_dv.h, 1)
                    s_dv.n += 1
                    badd = s_dv.n

                    @block.scalar
                    def _(e, badd=badd, stn=s_st.n):
                        e.wait_ge(s_dv.h, badd)
                        if stn > 0:
                            e.wait_ge(s_st.h, stn)
                        ins = e.activation(out=fst[:, 0:48], in_=fst[:, 0:48], func=AF.Sigmoid)
                        ins.then_inc(s_ac.h, 1)
                    s_ac.n += 1
                    sg_done = s_ac.n

                    @block.sync
                    def _(e, b=b, hb0=hb0, t8=t8, sg_done=sg_done):
                        e.wait_ge(s_ac.h, sg_done)
                        e.dma_start(out=fs_d[b, hb0 + t8 * 128:hb0 + (t8 + 1) * 128, 0:16],
                                    in_=fst[:, 16:32]).then_inc(s_st.h, 16)
                        e.dma_start(out=fs_d[b, hb0 + t8 * 128:hb0 + (t8 + 1) * 128, 16:32],
                                    in_=fst[:, 32:48]).then_inc(s_st.h, 16)
                        e.dma_start(out=fs_d[b, hb0 + t8 * 128:hb0 + (t8 + 1) * 128, 32:48],
                                    in_=fst[:, 0:16]).then_inc(s_st.h, 16)
                    s_st.n += 48

        def emit_U(b, half):
            hb0 = half * 1024
            sfx = "_%d_%d" % (b, half)
            with contextlib.ExitStack() as st2:
                ksb = st2.enter_context(nc.sbuf_tensor("a0_ksb" + sfx, [128, 1024], F32))
                vsb = st2.enter_context(nc.sbuf_tensor("a0_vsb" + sfx, [128, 1024], F32))
                bsb = st2.enter_context(nc.sbuf_tensor("a0_bsb" + sfx, [128, 16], F32))
                usb = st2.enter_context(nc.sbuf_tensor("a0_usb" + sfx, [128, 1024], BF16))
                for t8 in range(8):
                    k0 = s_ld2[0].n

                    @block.gpsimd
                    def _(e, b=b, hb0=hb0, t8=t8, dvn=s_dv.n, stn=(s_st.n if t8 == 0 else 0),
                          sg0=(s_stg[0].n if t8 == 0 else 0), sg1=(s_stg[1].n if t8 == 0 else 0)):
                        if dvn > 0:
                            e.wait_ge(s_dv.h, dvn)
                        if stn > 0:
                            e.wait_ge(s_st.h, stn)
                        if sg0 > 0:
                            e.wait_ge(s_stg[0].h, sg0)
                        if sg1 > 0:
                            e.wait_ge(s_stg[1].h, sg1)
                        r0, r1 = hb0 + t8 * 128, hb0 + (t8 + 1) * 128
                        e.dma_start(out=ksb[:], in_=kt_d[b, r0:r1, :]).then_inc(s_ld2[0].h, 16)
                        e.dma_start(out=vsb[:], in_=vt_d[b, r0:r1, :]).then_inc(s_ld2[0].h, 16)
                        e.dma_start(out=bsb[:], in_=fs_d[b, r0:r1, 32:48]).then_inc(s_ld2[0].h, 16)
                    s_ld2[0].n += 48
                    ld_done = s_ld2[0].n

                    @block.vector
                    def _(e, ld_done=ld_done, stn=s_st.n):
                        e.wait_ge(s_ld2[0].h, ld_done)
                        if stn > 0:
                            e.wait_ge(s_st.h, stn)
                        ins = e.tensor_mul(out=ksb[:], in0=ksb[:], in1=vsb[:])
                        p.wmark("v", e, p.mark("v", ins))
                        bb = bcast_ap(bsb[:, :], 0, [[16, 128], [1, 16], [0, 64]])
                        ins = e.tensor_tensor(out=usb.rearrange("q (h d) -> q h d", h=16),
                                              in0=ksb.rearrange("q (h d) -> q h d", h=16),
                                              in1=bb, op=ALU.mult)
                        ins.then_inc(s_dv.h, 1)
                    s_dv.n += 1
                    u_done = s_dv.n

                    @block.sync
                    def _(e, b=b, hb0=hb0, t8=t8, u_done=u_done):
                        e.wait_ge(s_dv.h, u_done)
                        e.dma_start(out=ut_d[b, hb0 + t8 * 128:hb0 + (t8 + 1) * 128, :],
                                    in_=usb[:]).then_inc(s_st.h, 16)
                    s_st.n += 16

        def emit_S(half, set_idx):
            hb0 = half * 1024
            sfx = "_%d" % half
            with contextlib.ExitStack() as st2:
                usc = st2.enter_context(nc.sbuf_tensor("a0_usc" + sfx, [128, 16 * 1024], BF16))
                qsc = st2.enter_context(nc.sbuf_tensor("a0_qsc" + sfx, [128, 16 * 1024], BF16))

                ld0 = s_ld2[1].n

                @block.gpsimd
                def _(e, hb0=hb0, stn=s_st.n, dvn=s_dv.n, gpn=s_gp.n,
                      sg0=s_stg[0].n, sg1=s_stg[1].n):
                    e.wait_ge(s_st.h, stn)
                    if sg0 > 0:
                        e.wait_ge(s_stg[0].h, sg0)
                    if sg1 > 0:
                        e.wait_ge(s_stg[1].h, sg1)
                    if dvn > 0:
                        e.wait_ge(s_dv.h, dvn)
                    for b in range(2):
                        src = ut_d[b, hb0:hb0 + 1024, :].rearrange("(c i) l -> c (i l)", i=C)
                        e.dma_start(out=usc[b * 64:(b + 1) * 64, :], in_=src).then_inc(s_ld2[1].h, 16)
                        srq = qt_d[b, hb0:hb0 + 1024, :].rearrange("(c i) l -> c (i l)", i=C)
                        e.dma_start(out=qsc[b * 64:(b + 1) * 64, :], in_=srq).then_inc(s_ld2[1].h, 16)
                        srf = fs_d[b, hb0:hb0 + 1024, :].rearrange("(c i) l -> c (i l)", i=C)
                        e.dma_start(out=fssc[b * 64:(b + 1) * 64, :], in_=srf).then_inc(s_ld2[1].h, 16)
                s_ld2[1].n += 96
                ld_done = s_ld2[1].n

                @block.vector
                def _(e, dvn=s_dv.n):
                    if dvn > 0:
                        e.wait_ge(s_dv.h, dvn)
                    e.memset(sfst[:], 0.0)
                    e.memset(ssst[:], 0.0)
                    ins = e.memset(d1t[:], 0.0)
                    ins.then_inc(s_dv.h, 1)
                s_dv.n += 1

                @block.gpsimd
                def _(e, gpn=s_gp.n):
                    if gpn > 0:
                        e.wait_ge(s_gp.h, gpn)
                    e.memset(rten[:, 0:16], 1.0)
                    e.memset(rten[:, 16:32], 0.0)
                    e.memset(rten[:, 32:48], 0.0)
                    e.memset(mcon[:], MIXC)
                    ins = e.memset(rten[:, 48:64], 1.0)
                    ins.then_inc(s_gp.h, 1)
                s_gp.n += 1
                r_init = s_gp.n

                @block.vector
                def _(e, ld_done=ld_done, init_dv=s_dv.n):
                    e.wait_ge(s_ld2[1].h, ld_done)
                    e.wait_ge(s_dv.h, init_dv)
                    p.wmark("v", e)
                    for i in range(C):
                        fb = bcast_ap(fssc[:, :], i * 48, [[16 * 48, 128], [1, 16], [0, 64]])
                        sb = bcast_ap(fssc[:, :], i * 48 + 16, [[16 * 48, 128], [1, 16], [0, 64]])
                        u_i = usc[:, i * 1024:(i + 1) * 1024]
                        m1 = e.tensor_tensor(out=d1t.rearrange("q (h d) -> q h d", h=16),
                                             in0=sfst.rearrange("q (h d) -> q h d", h=16),
                                             in1=fb, op=ALU.mult)
                        ma = p.mark("v", m1)
                        m2 = e.tensor_tensor(out=d2t.rearrange("q (h d) -> q h d", h=16),
                                             in0=ssst.rearrange("q (h d) -> q h d", h=16),
                                             in1=sb, op=ALU.mult)
                        mb = p.mark("v", m2)
                        p.wmark("v", e, ma)
                        m3 = e.tensor_tensor(out=d1t[:], in0=d1t[:], in1=u_i, op=ALU.add)
                        mc = p.mark("v", m3)
                        p.wmark("v", e, mb)
                        m4 = e.tensor_tensor(out=d2t[:], in0=d2t[:], in1=u_i, op=ALU.add)
                        md = p.mark("v", m4)
                        p.wmark("v", e, md)
                        m5 = e.scalar_tensor_tensor(out=sfst[:], in0=d2t[:], scalar=MIXC,
                                                    in1=d1t[:], op0=ALU.mult, op1=ALU.add)
                        me = p.mark("v", m5)
                        m6 = e.scalar_tensor_tensor(out=ssst[:], in0=d1t[:], scalar=MIXC,
                                                    in1=d2t[:], op0=ALU.mult, op1=ALU.add)
                        mf = p.mark("v", m6)
                        p.wmark("v", e, mf)
                    e.engine_nop().then_inc(s_dv.h, 1)
                s_dv.n += 1
                e_done = s_dv.n

                @block.gpsimd
                def _(e, ld_done=ld_done, r_init=r_init):
                    e.wait_ge(s_ld2[1].h, ld_done)
                    e.wait_ge(s_gp.h, r_init)
                    for i in range(C):
                        f_i = fssc[:, i * 48:i * 48 + 16]
                        s_i = fssc[:, i * 48 + 16:i * 48 + 32]
                        for (ra, rb2) in ((0, 2), (1, 3)):
                            m1 = e.tensor_tensor(out=rscr[:, 0:16], in0=rten[:, ra * 16:(ra + 1) * 16],
                                                 in1=f_i, op=ALU.mult)
                            m2 = e.tensor_tensor(out=rscr[:, 16:32], in0=rten[:, rb2 * 16:(rb2 + 1) * 16],
                                                 in1=s_i, op=ALU.mult)
                            b2 = p.mark("g", m2)
                            p.wmark("g", e, b2)
                            m2a = e.tensor_tensor(out=rscr[:, 32:48], in0=rscr[:, 0:16],
                                                  in1=mcon[:], op=ALU.mult)
                            m2b = e.tensor_tensor(out=rscr[:, 48:64], in0=rscr[:, 16:32],
                                                  in1=mcon[:], op=ALU.mult)
                            b3 = p.mark("g", m2b)
                            p.wmark("g", e, b3)
                            m3 = e.tensor_tensor(out=rten[:, ra * 16:(ra + 1) * 16],
                                                 in0=rscr[:, 0:16], in1=rscr[:, 48:64], op=ALU.add)
                            m4 = e.tensor_tensor(out=rten[:, rb2 * 16:(rb2 + 1) * 16],
                                                 in0=rscr[:, 16:32], in1=rscr[:, 32:48], op=ALU.add)
                            d = p.mark("g", m4)
                            p.wmark("g", e, d)
                    e.engine_nop().then_inc(s_gp.h, 1)
                s_gp.n += 1
                r_done = s_gp.n


                pe_t0 = s_pe.n

                @block.tensor
                def _(e, e_done=e_done, r_done=r_done, pe_t0=pe_t0, dvn=s_dv.n,
                      acn=s_ac.n):
                    e.wait_ge(s_dv.h, e_done)
                    e.wait_ge(s_gp.h, r_done)
                    if acn > 0:
                        e.wait_ge(s_ac.h, acn)
                    n = 0
                    for b in range(2):
                        for sti, sten in ((0, sfst), (1, ssst)):
                            for lc in range(8):
                                if n >= 4:
                                    e.wait_ge(s_dv.h, dvn + n - 3)
                                ins = e.transpose(ps[2 + n % 4][0:128, 0:64],
                                                  sten[b * 64:(b + 1) * 64, lc * 128:(lc + 1) * 128],
                                                  consts["ident_f32"][b * 64:(b + 1) * 64,
                                                                      b * 64:(b + 1) * 64])
                                ins.then_inc(s_pe.h, 1)
                                n += 1
                    for b in range(2):
                        for en in range(4):
                            if True:
                                e.wait_ge(s_dv.h, dvn + n - 3)
                            ins = e.transpose(ps[2 + n % 4][0:16, 0:64],
                                              rten[b * 64:(b + 1) * 64, en * 16:(en + 1) * 16],
                                              consts["ident_f32"][b * 64:(b + 1) * 64,
                                                                  b * 64:(b + 1) * 64])
                            ins.then_inc(s_pe.h, 1)
                            n += 1
                s_pe.n = pe_t0 + 40

                @block.vector
                def _(e, pe_t0=pe_t0, dvn=s_dv.n):
                    n = 0
                    for b in range(2):
                        for sti in range(2):
                            for lc in range(8):
                                e.wait_ge(s_pe.h, pe_t0 + n + 1)
                                dst = elan[:, ((sti * 2 + b) * 8 + lc) * 64:
                                           ((sti * 2 + b) * 8 + lc) * 64 + 64]
                                ins = e.tensor_copy(out=dst, in_=ps[2 + n % 4][0:128, 0:64])
                                ins.then_inc(s_dv.h, 1)
                                n += 1
                    for b in range(2):
                        for en in range(4):
                            e.wait_ge(s_pe.h, pe_t0 + n + 1)
                            dst = rrow[:, (en * 2 + b) * 64:(en * 2 + b) * 64 + 64]
                            ins = e.tensor_copy(out=dst, in_=ps[2 + n % 4][0:16, 0:64])
                            ins.then_inc(s_dv.h, 1)
                            n += 1
                s_dv.n += 40
                tr_done = s_dv.n

                pe_r0 = s_pe.n

                @block.tensor
                def _(e, tr_done=tr_done, pe_r0=pe_r0, dvn=s_dv.n):
                    e.wait_ge(s_dv.h, tr_done)
                    n = 0
                    for en in range(4):
                        for b in range(2):
                            for lc in range(8):
                                if n >= 4:
                                    e.wait_ge(s_dv.h, dvn + n - 3)
                                ins = e.matmul(ps[2 + n % 4][:, 0:64],
                                               selm[:, lc * 128:(lc + 1) * 128],
                                               rrow[:, (en * 2 + b) * 64:(en * 2 + b) * 64 + 64],
                                               start=True, stop=True)
                                ins.then_inc(s_pe.h, 1)
                                n += 1
                s_pe.n = pe_r0 + 64

                @block.vector
                def _(e, pe_r0=pe_r0):
                    n = 0
                    for en in range(4):
                        for b in range(2):
                            for lc in range(8):
                                e.wait_ge(s_pe.h, pe_r0 + n + 1)
                                dst = rlan[:, ((en * 2 + b) * 8 + lc) * 64:
                                           ((en * 2 + b) * 8 + lc) * 64 + 64]
                                ins = e.tensor_copy(out=dst, in_=ps[2 + n % 4][:, 0:64])
                                ins.then_inc(s_dv.h, 1)
                                n += 1
                s_dv.n += 64
                rl_done = s_dv.n

                @block.vector
                def _(e, rl_done=rl_done, set_idx=set_idx):
                    e.wait_ge(s_dv.h, rl_done)
                    p.wmark("v", e)
                    if set_idx == 0:
                        ins = e.memset(scur[:, :], 0.0)
                        p.wmark("v", e, p.mark("v", ins))
                    for c in range(64):
                        sl = slan.rearrange("q (g c2) -> q g c2", c2=64)
                        m1 = e.tensor_copy(out=sl[:, 0:16, c],
                                           in_=scur.rearrange("q (g o) -> q g o", o=1)[:, 0:16, 0]
                                           if False else scur[:, 0:16])
                        ma = p.mark("v", m1)
                        m2 = e.tensor_copy(out=sl[:, 16:32, c], in_=scur[:, 16:32])
                        mb = p.mark("v", m2)
                        rl = rlan.rearrange("q (g c2) -> q g c2", c2=64)
                        el = elan.rearrange("q (g c2) -> q g c2", c2=64)
                        p.wmark("v", e, mb)
                        m3 = e.tensor_tensor(out=d1t[:, 0:16], in0=scur[:, 0:16],
                                             in1=rl[:, 0:16, c], op=ALU.mult)
                        mc = p.mark("v", m3)
                        m4 = e.tensor_tensor(out=d1t[:, 16:32], in0=scur[:, 16:32],
                                             in1=rl[:, 16:32, c], op=ALU.mult)
                        mdd = p.mark("v", m4)
                        m5 = e.tensor_tensor(out=d2t[:, 0:16], in0=scur[:, 0:16],
                                             in1=rl[:, 32:48, c], op=ALU.mult)
                        me = p.mark("v", m5)
                        m6 = e.tensor_tensor(out=d2t[:, 16:32], in0=scur[:, 16:32],
                                             in1=rl[:, 48:64, c], op=ALU.mult)
                        mf = p.mark("v", m6)
                        p.wmark("v", e, mf)
                        m7 = e.tensor_tensor(out=scur[:, 0:16], in0=d1t[:, 0:16],
                                             in1=d1t[:, 16:32], op=ALU.add)
                        mg = p.mark("v", m7)
                        m8 = e.tensor_tensor(out=scur[:, 16:32], in0=d2t[:, 0:16],
                                             in1=d2t[:, 16:32], op=ALU.add)
                        mh = p.mark("v", m8)
                        p.wmark("v", e, mg)
                        m9 = e.tensor_tensor(out=scur[:, 0:16], in0=scur[:, 0:16],
                                             in1=el[:, 0:16, c], op=ALU.add)
                        mi = p.mark("v", m9)
                        p.wmark("v", e, mh)
                        m10 = e.tensor_tensor(out=scur[:, 16:32], in0=scur[:, 16:32],
                                              in1=el[:, 16:32, c], op=ALU.add)
                        mj = p.mark("v", m10)
                        p.wmark("v", e, mj)
                    e.engine_nop().then_inc(s_dv.h, 1)
                s_dv.n += 1
                p2_done = s_dv.n

                pe_s0 = s_pe.n

                @block.tensor
                def _(e, p2_done=p2_done, pe_s0=pe_s0, dvn=s_dv.n):
                    e.wait_ge(s_dv.h, p2_done)
                    n = 0
                    for b in range(2):
                        for sti in range(2):
                            for lc in range(8):
                                if n >= 4:
                                    e.wait_ge(s_dv.h, dvn + n - 3)
                                ins = e.transpose(ps[2 + n % 4][0:64, 0:128],
                                                  slan[:, ((sti * 2 + b) * 8 + lc) * 64:
                                                       ((sti * 2 + b) * 8 + lc) * 64 + 64],
                                                  consts["ident_f32"][:, :])
                                ins.then_inc(s_pe.h, 1)
                                n += 1
                s_pe.n = pe_s0 + 32

                @block.vector
                def _(e, pe_s0=pe_s0):
                    n = 0
                    for b in range(2):
                        for sti, sten in ((0, sfst), (1, ssst)):
                            for lc in range(8):
                                e.wait_ge(s_pe.h, pe_s0 + n + 1)
                                ins = e.tensor_copy(
                                    out=sten[b * 64:(b + 1) * 64, lc * 128:(lc + 1) * 128],
                                    in_=ps[2 + n % 4][0:64, 0:128])
                                ins.then_inc(s_dv.h, 1)
                                n += 1
                s_dv.n += 32
                st_done = s_dv.n

                @block.vector
                def _(e, st_done=st_done):
                    e.wait_ge(s_dv.h, st_done)
                    p.wmark("v", e)
                    for i in range(C):
                        fb = bcast_ap(fssc[:, :], i * 48, [[16 * 48, 128], [1, 16], [0, 64]])
                        sb = bcast_ap(fssc[:, :], i * 48 + 16, [[16 * 48, 128], [1, 16], [0, 64]])
                        u_i = usc[:, i * 1024:(i + 1) * 1024]
                        q_i = qsc[:, i * 1024:(i + 1) * 1024]
                        m1 = e.tensor_tensor(out=d1t.rearrange("q (h d) -> q h d", h=16),
                                             in0=sfst.rearrange("q (h d) -> q h d", h=16),
                                             in1=fb, op=ALU.mult)
                        ma = p.mark("v", m1)
                        m2 = e.tensor_tensor(out=d2t.rearrange("q (h d) -> q h d", h=16),
                                             in0=ssst.rearrange("q (h d) -> q h d", h=16),
                                             in1=sb, op=ALU.mult)
                        mb = p.mark("v", m2)
                        p.wmark("v", e, ma)
                        m3 = e.tensor_tensor(out=sfst[:], in0=d1t[:], in1=u_i, op=ALU.add)
                        mc = p.mark("v", m3)
                        p.wmark("v", e, mb)
                        m5 = e.tensor_tensor(out=ssst[:], in0=d2t[:], in1=u_i, op=ALU.add)
                        me = p.mark("v", m5)
                        p.wmark("v", e, mc)
                        m4 = e.tensor_tensor(out=d1t[:], in0=d1t[:], in1=d2t[:], op=ALU.add)
                        md = p.mark("v", m4)
                        p.wmark("v", e, md)
                        m6 = e.scalar_tensor_tensor(out=d2t[:], in0=d1t[:], scalar=0.5,
                                                    in1=q_i, op0=ALU.mult, op1=ALU.mult)
                        mf = p.mark("v", m6)
                        p.wmark("v", e, mf)
                        m7 = e.tensor_copy(out=u_i, in_=d2t[:])
                        mg = p.mark("v", m7)
                        m8 = e.scalar_tensor_tensor(out=d1t[:], in0=ssst[:], scalar=MIXC,
                                                    in1=sfst[:], op0=ALU.mult, op1=ALU.add)
                        mh = p.mark("v", m8)
                        p.wmark("v", e, mh)
                        m9 = e.scalar_tensor_tensor(out=ssst[:], in0=sfst[:], scalar=MIXC,
                                                    in1=ssst[:], op0=ALU.mult, op1=ALU.add)
                        mi = p.mark("v", m9)
                        p.wmark("v", e, mi)
                        m10 = e.tensor_copy(out=sfst[:], in_=d1t[:])
                        mj = p.mark("v", m10)
                        p.wmark("v", e, mj)
                    e.engine_nop().then_inc(s_dv.h, 1)
                s_dv.n += 1
                p3_done = s_dv.n

                @block.sync
                def _(e, hb0=hb0, p3_done=p3_done):
                    e.wait_ge(s_dv.h, p3_done)
                    for b in range(2):
                        dst = ot_d[b, hb0:hb0 + 1024, :].rearrange("(c i) l -> c (i l)", i=C)
                        e.dma_start(out=dst, in_=usc[b * 64:(b + 1) * 64, :]).then_inc(s_st.h, 16)
                s_st.n += 32

        def emit_O(b, half):
            hb0 = half * 1024
            sfx = "_%d_%d" % (b, half)
            with contextlib.ExitStack() as st2:
                xh = st2.enter_context(nc.sbuf_tensor("a0_xh2" + sfx, [128, 8 * 512], BF16))
                gsb = st2.enter_context(nc.sbuf_tensor("a0_gsb" + sfx, [128, 8 * 512], BF16))
                osb = st2.enter_context(nc.sbuf_tensor("a0_osb" + sfx, [128, 1024], BF16))
                ogb = st2.enter_context(nc.sbuf_tensor("a0_ogb" + sfx, [128, 8 * 512], BF16))
                wr2 = st2.enter_context(nc.sbuf_tensor("a0_wr2" + sfx, [128, 2 * 8 * 128], BF16))
                tpb = st2.enter_context(nc.psum_tensor("a0_tpb" + sfx, [128, 512], BF16))

                for t5 in range(2):
                    t0 = b * TQ + hb0 + t5 * 512
                    prev_pe = s_pe.n
                    prev_dv = s_dv.n
                    emit_ln_tile(p, res_dram, ws, t0, xh, 0,
                                 pre_apply_waits=[(s_pe.h, prev_pe), (s_st.h, s_st.n)],
                                 pre_load_waits=[(s_dv.h, prev_dv), (s_st.h, s_st.n)],
                                 pre_stats_waits=[(s_dv.h, prev_dv), (s_ac.h, s_ac.n)])
                    xh_ready = ws["s_ac"].n

                    g_base = s_ac.n
                    for m in range(8):
                        slot = m % 2

                        @block.gpsimd
                        def _(e, m=m, slot=slot, wf=wfree[slot],
                              stn=(s_st.n if m == 0 else 0), dvn=(s_dv.n if m == 0 else 0)):
                            if wf > 0:
                                e.wait_ge(s_pe.h, wf)
                            if stn > 0:
                                e.wait_ge(s_st.h, stn)
                            if dvn > 0:
                                e.wait_ge(s_dv.h, dvn)
                            e.dma_start(out=wr2[:, slot * 1024:(slot + 1) * 1024]
                                        .rearrange("q (k c) -> q k c", k=8),
                                        in_=wd["wgoT_b"].rearrange("(k q) n -> q k n", q=128)
                                        [:, :, m * 128:(m + 1) * 128]).then_inc(s_wr[slot].h, 16)
                        s_wr[slot].n += 16
                        wl = s_wr[slot].n
                        pe_g0 = s_pe.n

                        @block.tensor
                        def _(e, m=m, slot=slot, wl=wl, xh_ready=xh_ready, pe_g0=pe_g0,
                              acn=s_ac.n, g_base=g_base):
                            e.wait_ge(ws["s_ac"].h, xh_ready)
                            e.wait_ge(s_wr[slot].h, wl)
                            if acn > 0 and m == 0:
                                e.wait_ge(s_ac.h, acn)
                            if m >= 4:
                                e.wait_ge(s_ac.h, g_base + m - 3)
                            for k in range(8):
                                ins = e.matmul(ps[2 + m % 4][:, :],
                                               wr2[:, slot * 1024 + k * 128:slot * 1024 + (k + 1) * 128],
                                               xh[:, k * 512:(k + 1) * 512],
                                               start=(k == 0), stop=(k == 7),
                                               skip_group_check=True)
                            ins.then_inc(s_pe.h, 1)
                        s_pe.n = pe_g0 + 1
                        gm = s_pe.n
                        wfree[slot] = s_pe.n

                        @block.scalar
                        def _(e, m=m, gm=gm, dvx=s_dv.n):
                            e.wait_ge(s_pe.h, gm)
                            ins = e.activation(out=gsb[:, m * 512:(m + 1) * 512],
                                               in_=ps[2 + m % 4][:, :], func=AF.Sigmoid)
                            ins.then_inc(s_ac.h, 1)
                        s_ac.n += 1
                    g_done = s_ac.n

                    for t8 in range(4):
                        r0 = hb0 + t5 * 512 + t8 * 128
                        ldb = s_ld2[0].n

                        @block.gpsimd
                        def _(e, b=b, r0=r0, dvn=s_dv.n, aw=g_done, stn=s_st.n):
                            if dvn > 0:
                                e.wait_ge(s_dv.h, dvn)
                            e.wait_ge(s_st.h, stn)
                            e.dma_start(out=osb[:], in_=ot_d[b, r0:r0 + 128, :]
                                        ).then_inc(s_ld2[0].h, 16)
                        s_ld2[0].n += 16
                        o_ld = s_ld2[0].n
                        pe_t0 = s_pe.n

                        @block.tensor
                        def _(e, o_ld=o_ld, pe_t0=pe_t0, dvn=s_dv.n):
                            e.wait_ge(s_ld2[0].h, o_ld)
                            for ch in range(8):
                                if ch >= 1:
                                    e.wait_ge(s_dv.h, dvn + ch)
                                ins = e.transpose(tpb[:, 0:128],
                                                  osb[:, ch * 128:(ch + 1) * 128],
                                                  consts["ident_bf16"][:, :])
                                ins.then_inc(s_pe.h, 1)
                        s_pe.n = pe_t0 + 8

                        @block.vector
                        def _(e, t8=t8, pe_t0=pe_t0, g_done=g_done):
                            e.wait_ge(s_ac.h, g_done)
                            for ch in range(8):
                                e.wait_ge(s_pe.h, pe_t0 + ch + 1)
                                ins = e.tensor_mul(
                                    out=ogb[:, ch * 512 + t8 * 128:ch * 512 + (t8 + 1) * 128],
                                    in0=tpb[:, 0:128],
                                    in1=gsb[:, ch * 512 + t8 * 128:ch * 512 + (t8 + 1) * 128])
                                ins.then_inc(s_dv.h, 1)
                        s_dv.n += 8
                    og_done = s_dv.n

                    for m in range(8):
                        slot = m % 2

                        @block.gpsimd
                        def _(e, m=m, slot=slot, wf=wfree[slot]):
                            e.wait_ge(s_pe.h, wf)
                            e.dma_start(out=wr2[:, slot * 1024:(slot + 1) * 1024]
                                        .rearrange("q (k c) -> q k c", k=8),
                                        in_=wd["wgoT_b"].rearrange("(k q) n -> q k n", q=128)
                                        [:, :, 1024 + m * 128:1024 + (m + 1) * 128]
                                        ).then_inc(s_wr[slot].h, 16)
                        s_wr[slot].n += 16
                        wl = s_wr[slot].n
                        pe_o0 = s_pe.n

                        @block.tensor
                        def _(e, m=m, slot=slot, wl=wl, og_done=og_done, pe_o0=pe_o0,
                              dvw=s_dv.n + 1):
                            e.wait_ge(s_wr[slot].h, wl)
                            e.wait_ge(s_dv.h, og_done)
                            if m == 4:
                                e.wait_ge(s_dv.h, dvw)
                            for k in range(8):
                                ins = e.matmul(ps[2 + m % 4][:, :],
                                               wr2[:, slot * 1024 + k * 128:slot * 1024 + (k + 1) * 128],
                                               ogb[:, k * 512:(k + 1) * 512],
                                               start=(k == 0), stop=(k == 7),
                                               skip_group_check=True)
                            ins.then_inc(s_pe.h, 1)
                        s_pe.n = pe_o0 + 1
                        wfree[slot] = s_pe.n
                    proj_done = s_pe.n

                    @block.sync
                    def _(e, t0=t0, dvn=s_dv.n, ordn=p.sem("ord_v").n, stn=s_st.n):
                        if dvn > 0:
                            e.wait_ge(s_dv.h, dvn)
                        if ordn > 0:
                            e.wait_ge(p.sem("ord_v").h, ordn)
                        if stn > 0:
                            e.wait_ge(s_st.h, stn)
                        e.dma_start(out=ws["rt"].rearrange("q (j c) -> q j c", j=8),
                                    in_=res_dram.rearrange("(j q) c -> q j c", q=128)
                                    [:, :, t0:t0 + 512]).then_inc(ws["s_ld"].h, 16)
                    ws["s_ld"].n += 16
                    rt_ld = ws["s_ld"].n

                    @block.vector
                    def _(e, proj_done=proj_done, rt_ld=rt_ld):
                        e.wait_ge(ws["s_ld"].h, rt_ld)
                        for mw in range(2):
                            e.wait_ge(s_pe.h, proj_done - 8 + 4 * (mw + 1))
                            for m4 in range(4):
                                m = mw * 4 + m4
                                ins = e.tensor_tensor(out=ws["rt"][:, m * 512:(m + 1) * 512],
                                                      in0=ws["rt"][:, m * 512:(m + 1) * 512],
                                                      in1=ps[2 + m4][:, :], op=ALU.add)
                                if mw == 0 and m4 == 3:
                                    ins.then_inc(s_dv.h, 1)
                                elif not (mw == 1 and m4 == 3):
                                    p.wmark("v", e, p.mark("v", ins))
                        ins.then_inc(s_dv.h, 1)
                    s_dv.n += 2
                    wr_ok = s_dv.n

                    @block.sync
                    def _(e, t0=t0, wr_ok=wr_ok):
                        e.wait_ge(s_dv.h, wr_ok)
                        e.dma_start(out=res_dram.rearrange("(j q) c -> q j c", q=128)
                                    [:, :, t0:t0 + 512],
                                    in_=ws["rt"].rearrange("q (j c) -> q j c", j=8)
                                    ).then_inc(s_st.h, 16)
                    s_st.n += 16

        for half in range(2):
            for b in range(2):
                emit_P(b, half)
                emit_U(b, half)
            emit_S(half, half)
            for b in range(2):
                emit_O(b, half)

        @block.sync
        def _(e, pc=p.pc_info):
            e.wait_ge(s_st.h, s_st.n)
            if pc is not None:
                e.wait_ge(pc["sems"][0].h, pc["full"][0])
                e.wait_ge(pc["sems"][1].h, pc["full"][1])



import contextlib


def make_consts():
    out = {
        "ident_f32": np.eye(128, dtype=np.float32),
        "ident_bf16": np.eye(128, dtype=BF),
    }
    half = 16
    theta = (10000.0 ** (-(np.arange(half, dtype=np.float64) * 2.0 / 32.0)))
    tpos = np.arange(2048, dtype=np.float64)
    ang = tpos[None, :] * theta[:, None]
    c = np.cos(ang); si = np.sin(ang)
    c2 = np.zeros((32, 2048)); s2 = np.zeros((32, 2048))
    c2[0::2] = c; c2[1::2] = c
    s2[0::2] = -si; s2[1::2] = si
    c96 = np.tile(c2, (3, 1)).astype(np.float32)
    s96 = np.tile(s2, (3, 1)).astype(np.float32)
    out["kc2"] = c96
    out["ks2"] = s96
    pm = np.zeros((128, 128), np.float32)
    for i in range(128):
        pm[i, i ^ 1] = 1.0
    out["perm96"] = pm.astype(BF)
    cm = np.zeros((4, 128, 512), np.float32)
    for sft in range(4):
        for kp in range(128):
            cm[sft, kp, :] = (sft * 128 + kp <= np.arange(512)).astype(np.float32)
    out["cmask"] = cm.astype(BF)
    sel = np.zeros((8, 16, 128), np.float32)
    for lc in range(8):
        for pp in range(128):
            sel[lc, 2 * lc + pp // 64, pp] = 1.0
    out["sel2h"] = sel
    return out


WEIGHT_SPECS = {
    "l0a": {
        "wqkvT": ([1024, 3072], F32),
        "wgoT": ([1024, 2048], F32),
        "wbfsT": ([1024, 48], F32),
        "bfdT": ([128, 16], F32),
        "bsdT": ([128, 16], F32),
    },
    "l0f": {
        "gateT": ([1024, 14], F32),
        "w1": ([14, 1024, 2048], F32),
        "b1T": ([14, 128, 16], F32),
        "w2": ([14, 2048, 1024], F32),
        "b2T": ([14, 128, 8], F32),
    },
    "l1a": {
        "wqkvp": ([1024, 3072], F32),
        "wproj": ([1024, 1024], F32),
        "lq1": ([32, 1], F32),
        "lk1": ([32, 1], F32),
        "lq2": ([32, 1], F32),
        "lk2": ([32, 1], F32),
    },
    "l1f": {
        "maker_bd": ([4, 8, 128, 128], F32),
        "up_bd": ([4, 8, 128, 512], F32),
        "w_cprojT": ([16384, 1024], F32),
    },
}


def build_program(blocks, debug_res=False):
    consts_np = make_consts()
    nc = bass.Bass("TRN2", target_bir_lowering=False, debug=False)

    drams = {}
    if "tin" in blocks:
        drams["x"] = nc.dram_tensor("x", [NTOK, D], F32, kind="ExternalInput").ap()
    if "tout" in blocks:
        drams["y"] = nc.dram_tensor("y", [NTOK, D], F32, kind="ExternalOutput").ap()

    first, last = blocks[0], blocks[-1]
    if first != "tin":
        res = nc.dram_tensor("res_io", [D, NTOK], F32,
                             kind="ExternalInput" if last == "tout" else "ExternalInputOutput"
                             if False else "ExternalInput").ap()
        res_in_external = True
    else:
        res = None
        res_in_external = False
    if last != "tout":
        if res is None:
            res = nc.dram_tensor("res_out", [D, NTOK], F32, kind="ExternalOutput").ap()
        else:
            res_out = nc.dram_tensor("res_out", [D, NTOK], F32, kind="ExternalOutput").ap()
    else:
        res_out = None
    if res is None:
        res = nc.dram_tensor("res_scr", [D, NTOK], F32, kind="Internal").ap()

    wd = {}
    for b in blocks:
        for name, (shape, dt) in WEIGHT_SPECS.get(b, {}).items():
            wd[name] = nc.dram_tensor(name, shape, dt, kind="ExternalInput").ap()
    if "l1f" in blocks:
        wd["wctb"] = nc.dram_tensor("wctb", [16384, 1024], BF16, kind="Internal").ap()
    PRECAST = {"w1": [14 * 1024, 2048], "w2": [14 * 2048, 1024],
               "wqkvT": [1024, 3072], "wgoT": [1024, 2048],
               "wqkvp": [1024, 3072], "wproj": [1024, 1024]}
    for name, shp in PRECAST.items():
        if name in wd:
            wd[name + "_b"] = nc.dram_tensor(name + "_b", shp, BF16, kind="Internal").ap()

    consts = {}
    for name, arr in consts_np.items():
        dt = F32 if arr.dtype == np.float32 else BF16
        consts[name] = nc.dram_tensor(name, list(arr.shape), dt, kind="ExternalInput").ap()

    with contextlib.ExitStack() as stack:
        p = P(nc, None, stack)

        class _BlockCtx:
            def __enter__(self2):
                self2.cm = nc.Block()
                p.block = self2.cm.__enter__()
                return p.block

            def __exit__(self2, *a):
                r = self2.cm.__exit__(*a)
                p.block = None
                return r

        SB_CONSTS = ("ident_f32", "ident_bf16")
        csb = {}
        s_c = p.sem("const_ld")
        for name in SB_CONSTS:
            dt = F32 if consts_np[name].dtype == np.float32 else BF16
            t = stack.enter_context(nc.sbuf_tensor("c_" + name, list(consts_np[name].shape), dt))
            csb[name] = t

        with _BlockCtx() as block:
            @block.sync
            def _(e):
                for name in SB_CONSTS:
                    e.dma_start(out=csb[name][:], in_=consts[name]).then_inc(s_c.h, 16)
            s_c.n += 16 * len(SB_CONSTS)
            cw = s_c.n

            for eng in ["tensor", "vector", "scalar", "gpsimd"]:
                def mk(eng=eng):
                    def body(e):
                        e.wait_ge(s_c.h, cw)
                    return body
                getattr(block, eng)(mk())

        for name in consts:
            if name not in csb:
                csb[name] = consts[name]

        work_res = res
        if first != "tin" and last != "tout":
            s_cp = p.sem("res_cp")

            with _BlockCtx() as block:
                @block.sync
                def _(e):
                    e.dma_start(out=res_out, in_=res).then_inc(s_cp.h, 16)
                    e.wait_ge(s_cp.h, 16)
                s_cp.n += 16
            work_res = res_out

        pc_pairs = []
        for nm in ("wqkvT", "wgoT"):
            if nm in wd:
                pc_pairs.append((wd[nm], wd[nm + "_b"]))
        n_l0a_jobs = sum(ap.shape[0] // 128 for ap, _ in pc_pairs)
        if "w1" in wd:
            pc_pairs.append((wd["w1"].rearrange("e d f -> (e d) f"), wd["w1_b"]))
            pc_pairs.append((wd["w2"].rearrange("e f d -> (e f) d"), wd["w2_b"]))
        for nm in ("wqkvp", "wproj"):
            if nm in wd:
                pc_pairs.append((wd[nm], wd[nm + "_b"]))
        if "w_cprojT" in wd:
            pc_pairs.append((wd["w_cprojT"], wd["wctb"]))
        p.pc_info = None
        if pc_pairs and "l0a" not in blocks:
            with _BlockCtx():
                emit_precast(p, pc_pairs, "pc")
            pc_pairs = []

        dbg = {}
        for bi, b in enumerate(blocks):
            if debug_res:
                dbg[bi] = nc.dram_tensor("dbg_%d" % bi, [D, NTOK], F32,
                                         kind="ExternalOutput").ap()
            if b == "tin":
                with _BlockCtx():
                    emit_transpose_in(p, drams["x"], work_res, csb)
            elif b == "tout":
                with _BlockCtx():
                    emit_transpose_out(p, work_res, drams["y"], csb)
            elif b == "l0a":
                with _BlockCtx():
                    with contextlib.ExitStack() as pcs:
                        if pc_pairs:
                            emit_precast(p, pc_pairs, "pc", early_jobs=n_l0a_jobs,
                                         stack=pcs)
                            pc_pairs = []
                        emit_l0a(p, work_res, wd, csb)
            elif b == "l0f":
                with _BlockCtx():
                    emit_l0f(p, work_res, wd, csb)
            elif b == "l1a":
                with _BlockCtx():
                    emit_l1a(p, work_res, wd, csb)
            elif b == "l1f":
                with _BlockCtx():
                    emit_l1f(p, work_res, wd)
            else:
                raise ValueError(b)
            if debug_res:
                s_dbg = p.sem("dbg")

                with _BlockCtx() as block:
                    @block.sync
                    def _(e, bi=bi):
                        e.dma_start(out=dbg[bi], in_=work_res).then_inc(s_dbg.h, 16)
                        e.wait_ge(s_dbg.h, s_dbg.n + 16)
                    s_dbg.n += 16

    return nc, consts_np


def prep_weights(blocks, inputs):
    out = {}
    if "l0a" in blocks:
        wq = np.asarray(inputs["L0_attn_Wq"], np.float32)
        wk = np.asarray(inputs["L0_attn_Wk"], np.float32)
        wv = np.asarray(inputs["L0_attn_Wv"], np.float32)
        wo = np.asarray(inputs["L0_attn_Wo"], np.float32)
        wg = np.asarray(inputs["L0_attn_Wg"], np.float32)
        out["wqkvT"] = np.ascontiguousarray(
            np.concatenate([wq.T, wk.T, wv.T], axis=1))
        out["wgoT"] = np.ascontiguousarray(np.concatenate([wg.T, wo.T], axis=1))
        out["wbfsT"] = np.ascontiguousarray(np.concatenate(
            [np.asarray(inputs["L0_attn_Wb"], np.float32).T,
             np.asarray(inputs["L0_attn_Wfd"], np.float32).T,
             np.asarray(inputs["L0_attn_Wsd"], np.float32).T], axis=1))
        out["bfdT"] = np.ascontiguousarray(
            np.tile(np.asarray(inputs["L0_attn_bfd"], np.float32)[None, :], (128, 1)))
        out["bsdT"] = np.ascontiguousarray(
            np.tile(np.asarray(inputs["L0_attn_bsd"], np.float32)[None, :], (128, 1)))
    if "l0f" in blocks:
        out["gateT"] = np.ascontiguousarray(np.asarray(inputs["L0_ffn_gate_w"], np.float32).T)
        out["w1"] = np.ascontiguousarray(np.asarray(inputs["L0_ffn_w1"], np.float32))
        out["w2"] = np.ascontiguousarray(np.asarray(inputs["L0_ffn_w2"], np.float32))
        out["b1T"] = np.ascontiguousarray(
            np.asarray(inputs["L0_ffn_b1"], np.float32).reshape(14, 16, 128).transpose(0, 2, 1))
        out["b2T"] = np.ascontiguousarray(
            np.asarray(inputs["L0_ffn_b2"], np.float32).reshape(14, 8, 128).transpose(0, 2, 1))
    if "l1a" in blocks:
        wa = np.asarray(inputs["L1_attn_W_attn"], np.float32)
        cols = []
        for c in range(11):
            for h in range(3 * c, min(3 * c + 3, 32)):
                cols.extend(range(h * 32, h * 32 + 32))
        qcols = np.array(cols)
        out["wqkvp"] = np.ascontiguousarray(
            np.concatenate([wa[:, qcols], wa[:, 1024 + qcols], wa[:, 2048:]], axis=1))
        out["wproj"] = np.ascontiguousarray(np.asarray(inputs["L1_attn_W_proj"], np.float32))
        out["lq1"] = np.asarray(inputs["L1_attn_lambda_q1"], np.float32).reshape(32, 1)
        out["lk1"] = np.asarray(inputs["L1_attn_lambda_k1"], np.float32).reshape(32, 1)
        out["lq2"] = np.asarray(inputs["L1_attn_lambda_q2"], np.float32).reshape(32, 1)
        out["lk2"] = np.asarray(inputs["L1_attn_lambda_k2"], np.float32).reshape(32, 1)
    if "l1f" in blocks:
        mk = np.asarray(inputs["L1_ffn_rail_maker"], dtype=np.float32)
        up = np.asarray(inputs["L1_ffn_rail_up_proj"], dtype=np.float32)
        mk_bd = np.zeros((4, 8, 128, 128), np.float32)
        up_bd = np.zeros((4, 8, 128, 512), np.float32)
        for h in range(4):
            for c in range(8):
                for r in range(8):
                    blk = mk[h, c * 8 + r]
                    mk_bd[h, c, r * 16:(r + 1) * 16, r * 16:(r + 1) * 16] = blk.T
                    ub = up[h, c * 8 + r]
                    up_bd[h, c, r * 16:(r + 1) * 16, r * 64:(r + 1) * 64] = ub.T
        out["maker_bd"] = mk_bd
        out["up_bd"] = up_bd
        out["w_cprojT"] = np.ascontiguousarray(
            np.asarray(inputs["L1_ffn_w_cproj"], dtype=np.float32).T)
    return out


def kernel(**inputs):
    x = np.ascontiguousarray(np.asarray(inputs["L0_attn_x"], np.float32))
    xs = x.reshape(8, B_CORE * T, D)
    all_w = prep_weights(["l0a", "l0f", "l1a", "l1f"], inputs)

    blocks = ["tin", "l0a", "l0f", "l1a", "l1f", "tout"]
    nc, consts_np = build_program(blocks)
    in_maps = []
    for c in range(8):
        m = {"x": xs[c]}
        m.update(all_w)
        m.update(consts_np)
        in_maps.append(m)
    res = run_bass_kernel_spmd(nc, in_maps, core_ids=list(range(8)))
    out = np.stack([r["y"] for r in res.results], axis=0)
    return out.reshape(16, T, D)


# version 13 (final)
# speedup vs baseline: 1.1647x; 1.0031x over previous
import numpy as np
import ml_dtypes
import concourse.bass as bass
import concourse.mybir as mybir
from concourse.bass_utils import run_bass_kernel_spmd

BF = ml_dtypes.bfloat16
F32 = mybir.dt.float32
BF16 = mybir.dt.bfloat16
AF = mybir.ActivationFunctionType
ALU = mybir.AluOpType
AX = mybir.AxisListType

B_CORE = 2
T = 2048
D = 1024
NTOK = B_CORE * T
NCH = D // 128



class Sem:

    def __init__(self, stack, nc, name):
        self.h = stack.enter_context(nc.semaphore(name))
        self.n = 0

    def inc(self, instr, k=1):
        instr.then_inc(self.h, k)
        self.n += k
        return self.n


class P:

    def __init__(self, nc, block, stack):
        self.nc = nc
        self.block = block
        self.stack = stack
        self.sems = {}

    def sem(self, name):
        if name not in self.sems:
            self.sems[name] = Sem(self.stack, self.nc, name)
        return self.sems[name]

    def mark(self, eng, ins):
        s = self.sem("ord_" + eng)
        s.n += 1
        ins.then_inc(s.h, 1)
        return s.n

    def wmark(self, eng, e, n=None):
        s = self.sem("ord_" + eng)
        e.wait_ge(s.h, s.n if n is None else n)

    def barrier(self):
        return

    def _old_barrier(self):
        names = ["bar_sync", "bar_tensor", "bar_vector", "bar_scalar", "bar_gpsimd"]
        sems = [self.sem(n) for n in names]
        targets = []
        for eng_name, s in zip(["sync", "tensor", "vector", "scalar", "gpsimd"], names):
            pass
        engs = ["sync", "tensor", "vector", "scalar", "gpsimd"]
        for eng, s in zip(engs, sems):
            def mk(eng=eng, s=s):
                def body(e):
                    e.drain()
                    e.sem_inc(s.h, 1)
                    s.n += 1
                return body
            getattr(self.block, eng)(mk())
        waits = [(s.h, s.n) for s in sems]
        for eng in engs:
            def mk(eng=eng):
                def body(e):
                    for h, n in waits:
                        e.wait_ge(h, n)
                return body
            getattr(self.block, eng)(mk())


def bcast_ap(t_ap, offset_elems, dims):
    return bass.AP(tensor=t_ap.tensor, offset=t_ap.offset + offset_elems, ap=dims)



def emit_transpose_in(p, x_dram, res_dram, consts):
    _emit_transpose(p, "ti", res_dram, consts, x_dram=x_dram, direction="in")


def emit_transpose_out(p, res_dram, y_dram, consts):
    _emit_transpose(p, "to", res_dram, consts, y_dram=y_dram, direction="out")


def _emit_transpose(p, tag, res_dram, consts, x_dram=None, y_dram=None, direction="in"):
    nc, block = p.nc, p.block
    ntile = NTOK // 128
    s_ld = [p.sem(tag + "_ld0"), p.sem(tag + "_ld1")]
    s_pe = p.sem(tag + "_pe")
    s_ev = p.sem(tag + "_ev")
    s_st = [p.sem(tag + "_st0"), p.sem(tag + "_st1")]
    ld0 = [s.n for s in s_ld]
    pe0, ev0 = s_pe.n, s_ev.n
    st0 = [s.n for s in s_st]
    rv = res_dram.rearrange("(j q) c -> q j c", q=128)

    with (
        nc.sbuf_tensor(tag + "_xt", [128, 2 * D], F32) as xt,
        nc.sbuf_tensor(tag + "_yt", [128, 2 * D], F32) as yt,
        nc.psum_tensor(tag + "_ps0", [128, 1024], F32) as ps0,
        nc.psum_tensor(tag + "_ps1", [128, 1024], F32) as ps1,
    ):
        ps = [ps0, ps1]

        @block.sync
        def _(e):
            for i in range(ntile):
                k = i % 2
                if i >= 2:
                    e.wait_ge(s_pe.h, pe0 + 8 * (i - 1))
                if direction == "in":
                    src = x_dram[i * 128:(i + 1) * 128, :]
                    dst = xt[:, k * D:k * D + D]
                else:
                    src = rv[:, :, i * 128:(i + 1) * 128]
                    dst = xt.rearrange("q (bb j c) -> q bb j c", bb=2, j=8)[:, k, :, :]
                e.dma_start(out=dst, in_=src).then_inc(s_ld[k].h, 16)
        for k in range(2):
            s_ld[k].n = ld0[k] + 16 * ((ntile + 1 - k) // 2)

        @block.tensor
        def _(e):
            ident = consts["ident_f32"]
            for i in range(ntile):
                k = i % 2
                e.wait_ge(s_ld[k].h, ld0[k] + 16 * (i // 2 + 1))
                if i >= 2:
                    e.wait_ge(s_ev.h, ev0 + (i - 1))
                for j in range(8):
                    pt = ps[k][:, j * 128:j * 128 + 128]
                    ins = e.transpose(pt, xt[:, k * D + j * 128:k * D + j * 128 + 128],
                                      ident[:])
                    ins.then_inc(s_pe.h, 1)
        s_pe.n = pe0 + 8 * ntile

        @block.vector
        def _(e):
            for i in range(ntile):
                k = i % 2
                if i >= 2:
                    e.wait_ge(s_st[k].h, st0[k] + 16 * (i // 2))
                e.wait_ge(s_pe.h, pe0 + 8 * (i + 1))
                ins = e.tensor_copy(out=yt[:, k * D:k * D + D], in_=ps[k][:, 0:D])
                ins.then_inc(s_ev.h, 1)
        s_ev.n = ev0 + ntile

        @block.gpsimd
        def _(e):
            for i in range(ntile):
                k = i % 2
                e.wait_ge(s_ev.h, ev0 + (i + 1))
                if direction == "in":
                    src = yt.rearrange("q (bb j c) -> q bb j c", bb=2, j=8)[:, k, :, :]
                    dst = rv[:, :, i * 128:(i + 1) * 128]
                else:
                    src = yt[:, k * D:k * D + D]
                    dst = y_dram[i * 128:(i + 1) * 128, :]
                e.dma_start(out=dst, in_=src).then_inc(s_st[k].h, 16)
            for k in range(2):
                e.wait_ge(s_st[k].h, st0[k] + 16 * ((ntile + 1 - k) // 2))
        for k in range(2):
            s_st[k].n = st0[k] + 16 * ((ntile + 1 - k) // 2)



LN_EPS = 1e-5


def ln_workspace(p, stack, tag, ps0=None, ps1=None):
    nc = p.nc
    if ps0 is None:
        ps0 = stack.enter_context(nc.psum_tensor(tag + "_p0", [128, 512], F32))
        ps1 = stack.enter_context(nc.psum_tensor(tag + "_p1", [128, 512], F32))
    ws = {
        "rt": stack.enter_context(nc.sbuf_tensor(tag + "_rt", [128, 8 * 512], F32)),
        "ones1": stack.enter_context(nc.sbuf_tensor(tag + "_o1", [1, 128], F32)),
        "onesc": stack.enter_context(nc.sbuf_tensor(tag + "_oc", [128, 1], F32)),
        "srow": stack.enter_context(nc.sbuf_tensor(tag + "_srow", [1, 4 * 512], F32)),
        "tmp2": stack.enter_context(nc.sbuf_tensor(tag + "_tmp2", [128, 1024], F32)),
        "xb2": stack.enter_context(nc.sbuf_tensor(tag + "_xb2", [128, 1024], BF16)),
        "sq2": stack.enter_context(nc.sbuf_tensor(tag + "_sq2", [128, 1024], BF16)),
        "ones1b": stack.enter_context(nc.sbuf_tensor(tag + "_o1b", [1, 128], BF16)),
        "onescb": stack.enter_context(nc.sbuf_tensor(tag + "_ocb", [128, 1], BF16)),
        "srowb": stack.enter_context(nc.sbuf_tensor(tag + "_srb", [1, 1024], BF16)),
        "eps1": stack.enter_context(nc.sbuf_tensor(tag + "_eps", [1, 1], F32)),
        "spc": stack.enter_context(nc.sbuf_tensor(tag + "_spc", [1, 2], F32)),
        "ps0": ps0,
        "ps1": ps1,
        "s_ld": p.sem(tag + "_lld"),
        "s_pe": p.sem(tag + "_lpe"),
        "s_dv": p.sem(tag + "_ldv"),
        "s_ac": p.sem(tag + "_lac"),
    }
    return ws


def emit_ln_init(p, ws):
    block = p.block
    s_dv = ws["s_dv"]

    @block.vector
    def _(e):
        e.memset(ws["ones1"][:], 1.0)
        e.memset(ws["eps1"][:], LN_EPS)
        e.memset(ws["spc"][:], 0.0)
        e.memset(ws["ones1b"][:], 1.0)
        e.memset(ws["onescb"][:], 1.0)
        ins = e.memset(ws["onesc"][:], 1.0)
        ins.then_inc(s_dv.h, 1)
    s_dv.n += 1
    ws["init_done"] = s_dv.n


def emit_ln_tile(p, res_dram, ws, t0, xhat, xh_off, xhat_f32=None, pre_apply_waits=(), pre_load_waits=(), pre_stats_waits=(), xh_stride=512, acc_out=None, acc_off=0, acc_stride=512):
    nc, block = p.nc, p.block
    rv = res_dram.rearrange("(j q) c -> q j c", q=128)
    s_ld, s_pe, s_dv, s_ac = ws["s_ld"], ws["s_pe"], ws["s_dv"], ws["s_ac"]
    rt, srow = ws["rt"], ws["srow"]
    tmp2 = ws["tmp2"]
    ps0, ps1 = ws["ps0"], ws["ps1"]

    ld_t = s_ld.n + 16
    dv_base = s_dv.n
    pe_base = s_pe.n
    ac_base = s_ac.n

    ordv = p.sem("ord_v")
    ordv_n = ordv.n

    @block.sync
    def _(e):
        if pe_base > 0:
            e.wait_ge(s_pe.h, pe_base)
        if ordv_n > 0:
            e.wait_ge(ordv.h, ordv_n)
        for h_, n_ in pre_load_waits:
            if n_ > 0:
                e.wait_ge(h_, n_)
        e.dma_start(out=rt.rearrange("q (j c) -> q j c", j=8),
                    in_=rv[:, :, t0:t0 + 512]).then_inc(s_ld.h, 16)
    s_ld.n = ld_t

    sq_base = s_dv.n
    ac_cp_base = s_ac.n
    pe_base2 = s_pe.n

    @block.scalar
    def _(e, pe_base2=pe_base2, ac_base=ac_base, ac_cp_base=ac_cp_base):
        e.wait_ge(s_ld.h, ld_t)
        if ac_base > 0:
            e.wait_ge(s_ac.h, ac_base)
        for j in range(8):
            if j >= 2:
                e.wait_ge(s_pe.h, pe_base2 + j - 1)
            ins = e.activation(out=ws["xb2"][:, (j % 2) * 512:(j % 2) * 512 + 512],
                               in_=rt[:, j * 512:(j + 1) * 512], func=AF.Copy)
            ins.then_inc(s_ac.h, 1)
    s_ac.n += 8

    @block.vector
    def _(e, sq_base=sq_base, pe_base2=pe_base2):
        e.wait_ge(s_ld.h, ld_t)
        e.wait_ge(s_dv.h, sq_base)
        for j in range(8):
            if j >= 2:
                e.wait_ge(s_pe.h, pe_base2 + j - 1)
            ins = e.tensor_mul(out=ws["sq2"][:, (j % 2) * 512:(j % 2) * 512 + 512],
                               in0=rt[:, j * 512:(j + 1) * 512],
                               in1=rt[:, j * 512:(j + 1) * 512])
            ins.then_inc(s_dv.h, 1)
    s_dv.n += 8
    sq_done_all = s_dv.n

    @block.tensor
    def _(e, sq_base=sq_base, ac_base=ac_base, ac_cp_base=ac_cp_base):
        e.wait_ge(s_dv.h, ws["init_done"])
        if ac_base > 0:
            e.wait_ge(s_ac.h, ac_base)
        for h_, n_ in pre_stats_waits:
            if n_ > 0:
                e.wait_ge(h_, n_)
        for j in range(8):
            e.wait_ge(s_ac.h, ac_cp_base + j + 1)
            e.matmul(ps0[0:1, 0:512], ws["onescb"][:],
                     ws["xb2"][:, (j % 2) * 512:(j % 2) * 512 + 512],
                     start=(j == 0), stop=(j == 7), skip_group_check=True)
            e.wait_ge(s_dv.h, sq_base + j + 1)
            ins = e.matmul(ps1[0:1, 0:512], ws["onescb"][:],
                           ws["sq2"][:, (j % 2) * 512:(j % 2) * 512 + 512],
                           start=(j == 0), stop=(j == 7), skip_group_check=True)
            ins.then_inc(s_pe.h, 1)
    s_pe.n += 8
    stats_done = s_pe.n

    @block.vector
    def _(e):
        e.wait_ge(s_pe.h, stats_done)
        e.wait_ge(s_dv.h, sq_done_all)
        mu = srow[:, 0:512]
        t_ = srow[:, 512:1024]
        ins = e.tensor_scalar_mul(out=mu, in0=ps0[0:1, 0:512], scalar1=1.0 / D)
        p.wmark("v", e, p.mark("v", ins))
        ins = e.tensor_mul(out=t_, in0=mu, in1=mu)
        p.wmark("v", e, p.mark("v", ins))
        e.scalar_tensor_tensor(out=t_, in0=ps1[0:1, 0:512], scalar=1.0 / D,
                               in1=t_, op0=ALU.mult, op1=ALU.subtract).then_inc(s_dv.h, 1)
    s_dv.n += 1
    var_done = s_dv.n

    @block.scalar
    def _(e):
        e.wait_ge(s_dv.h, var_done)
        t_ = srow[:, 512:1024]
        a = srow[:, 1024:1536]
        ins = e.activation(out=t_, in_=t_, func=AF.Ln, bias=ws["eps1"][:])
        p.wmark("s", e, p.mark("s", ins))
        ins = e.activation(out=a, in_=t_, func=AF.Exp, scale=-0.5)
        ins.then_inc(s_dv.h, 1)
    s_dv.n += 1
    a_done = s_dv.n

    @block.vector
    def _(e):
        e.wait_ge(s_dv.h, a_done)
        p.wmark("v", e)
        b = srow[:, 1536:2048]
        ins = e.scalar_tensor_tensor(out=b, in0=srow[:, 0:512], scalar=-1.0,
                                     in1=srow[:, 1024:1536], op0=ALU.mult, op1=ALU.mult)
        p.wmark("v", e, p.mark("v", ins))
        ins = e.tensor_copy(out=ws["srowb"][:, 0:512], in_=srow[:, 1024:1536])
        p.wmark("v", e, p.mark("v", ins))
        ins = e.tensor_copy(out=ws["srowb"][:, 512:1024], in_=srow[:, 1536:2048])
        ins.then_inc(s_dv.h, 1)
    s_dv.n += 1
    b_done = s_dv.n

    @block.tensor
    def _(e):
        e.wait_ge(s_dv.h, b_done)
        e.matmul(ps0[:, 0:512], ws["ones1b"][:], ws["srowb"][:, 0:512], start=True, stop=True)
        ins = e.matmul(ps1[:, 0:512], ws["ones1b"][:], ws["srowb"][:, 512:1024], start=True, stop=True)
        ins.then_inc(s_pe.h, 1)
    s_pe.n += 1
    bc_done = s_pe.n

    @block.vector
    def _(e):
        e.wait_ge(s_pe.h, bc_done)
        if ac_base > 0:
            e.wait_ge(s_ac.h, ac_base)
        for h_, n_ in pre_apply_waits:
            if n_ > 0:
                e.wait_ge(h_, n_)
        for j in range(8):
            jj = j % 2
            ins = e.tensor_mul(out=ws["tmp2"][:, jj * 512:(jj + 1) * 512],
                               in0=rt[:, j * 512:(j + 1) * 512], in1=ps0[:, 0:512])
            p.wmark("v", e, p.mark("v", ins))
            ins = e.tensor_tensor(out=xhat[:, xh_off + j * xh_stride:xh_off + j * xh_stride + 512],
                                  in0=ws["tmp2"][:, jj * 512:(jj + 1) * 512],
                                  in1=ps1[:, 0:512], op=ALU.add)
            if xhat_f32 is not None:
                ins = e.tensor_tensor(out=xhat_f32[:, j * 512:(j + 1) * 512],
                                      in0=ws["tmp2"][:, jj * 512:(jj + 1) * 512],
                                      in1=ps1[:, 0:512], op=ALU.add)
            if acc_out is not None:
                ins = e.tensor_tensor(out=acc_out[:, acc_off + j * acc_stride:acc_off + j * acc_stride + 512],
                                      in0=ws["tmp2"][:, jj * 512:(jj + 1) * 512],
                                      in1=ps1[:, 0:512], op=ALU.add)
            if j < 7:
                p.wmark("v", e, p.mark("v", ins))
        ins.then_inc(s_ac.h, 1)
    s_ac.n += 1



def emit_precast(p, pairs, tag, early_jobs=None, stack=None):
    nc, block = p.nc, p.block
    s_ld = [p.sem(tag + "_ld0"), p.sem(tag + "_ld1")]
    s_st = [p.sem(tag + "_st0"), p.sem(tag + "_st1")]
    ld0 = [x.n for x in s_ld]
    st0 = [x.n for x in s_st]
    W = 4096
    jobs = []
    for src, dst in pairs:
        R, M = src.shape
        assert R % 128 == 0 and M <= W
        for r0 in range(0, R, 128):
            jobs.append((src, dst, r0, M))
    if stack is not None:
        buf = stack.enter_context(nc.sbuf_tensor(tag + "_buf", [128, 2 * W], BF16))
        cmgr = contextlib.nullcontext()
    else:
        cmgr = nc.sbuf_tensor(tag + "_buf", [128, 2 * W], BF16)
    with cmgr as _buf:
        if stack is None:
            buf = _buf
        @block.gpsimd
        def _(e):
            for i, (src, dst, r0, M) in enumerate(jobs):
                k = i % 2
                if i >= 2:
                    e.wait_ge(s_st[k].h, st0[k] + 16 * (i // 2))
                e.dma_start(out=buf[:, k * W:k * W + M],
                            in_=src[r0:r0 + 128, :]).then_inc(s_ld[k].h, 16)
        for k in range(2):
            s_ld[k].n = ld0[k] + 16 * ((len(jobs) + 1 - k) // 2)

        @block.sync
        def _(e):
            for i, (src, dst, r0, M) in enumerate(jobs):
                k = i % 2
                e.wait_ge(s_ld[k].h, ld0[k] + 16 * (i // 2 + 1))
                e.dma_start(out=dst[r0:r0 + 128, :],
                            in_=buf[:, k * W:k * W + M]).then_inc(s_st[k].h, 16)
        for k in range(2):
            s_st[k].n = st0[k] + 16 * ((len(jobs) + 1 - k) // 2)
        if early_jobs is None:
            @block.sync
            def _(e):
                for k in range(2):
                    e.wait_ge(s_st[k].h, s_st[k].n)
        if early_jobs is not None:
            ne = early_jobs
            p.pc_info = {
                "sems": (s_st[0], s_st[1]),
                "early": (st0[0] + 16 * ((ne + 1) // 2), st0[1] + 16 * (ne // 2)),
                "full": (s_st[0].n, s_st[1].n),
            }


def emit_l1f(p, res_dram, wd):
    nc, block = p.nc, p.block
    rv = res_dram.rearrange("(j q) c -> q j c", q=128)

    with contextlib.ExitStack() as stack:
        ps = [stack.enter_context(nc.psum_tensor("f1_ps%d" % i, [128, 512], F32))
              for i in range(8)]
        ws = ln_workspace(p, stack, "f1", ps0=ps[0], ps1=ps[1])
        xhat = stack.enter_context(nc.sbuf_tensor("f1_xh", [128, 8 * 512], BF16))
        mk_sb = stack.enter_context(nc.sbuf_tensor("f1_mk", [128, 32 * 128], BF16))
        up_sb = stack.enter_context(nc.sbuf_tensor("f1_up", [128, 32 * 512], BF16))
        y_sb = stack.enter_context(nc.sbuf_tensor("f1_y", [128, 16 * 512], BF16))
        h_sb = stack.enter_context(nc.sbuf_tensor("f1_h", [128, 64 * 512], BF16))
        oacc = stack.enter_context(nc.sbuf_tensor("f1_oa", [128, 8 * 512], F32))
        wband = stack.enter_context(nc.sbuf_tensor("f1_wb", [128, 4 * 1024], BF16))

        s_w = p.sem("f1_w")
        s_band = [p.sem("f1_b%d" % i) for i in range(4)]
        s_pe = p.sem("f1_pe")
        s_dv = p.sem("f1_dv")
        s_ac = p.sem("f1_ac")
        s_st = p.sem("f1_st")

        emit_ln_init(p, ws)

        @block.gpsimd
        def _(e):
            e.dma_start(out=mk_sb.rearrange("q (g m) -> q g m", g=32),
                        in_=wd["maker_bd"].rearrange("h c q m -> q (h c) m")).then_inc(s_w.h, 16)
            e.dma_start(out=up_sb.rearrange("q (g m) -> q g m", g=32),
                        in_=wd["up_bd"].rearrange("h c q m -> q (h c) m")).then_inc(s_w.h, 16)
        s_w.n += 32
        w_done = s_w.n

        @block.tensor
        def _(e):
            e.wait_ge(s_w.h, w_done)

        band_gen = [0] * 4
        wctbv = wd["wctb"]

        for tt in range(8):
            t0 = tt * 512
            prev_pe = s_pe.n
            prev_dv = s_dv.n
            emit_ln_tile(p, res_dram, ws, t0, xhat, 0,
                         pre_apply_waits=[(s_pe.h, prev_pe)],
                         pre_load_waits=[(s_dv.h, prev_dv)],
                         pre_stats_waits=[(s_dv.h, prev_dv)])
            xh_ready = ws["s_ac"].n

            for half in range(2):
                peY0, dvY0 = s_pe.n, s_dv.n

                @block.tensor
                def _(e, half=half, xh_ready=xh_ready, peY0=peY0, dvY0=dvY0):
                    e.wait_ge(ws["s_ac"].h, xh_ready)
                    if dvY0 > 0:
                        e.wait_ge(s_dv.h, dvY0)
                    for gi in range(16):
                        hh, ch = gi // 8, gi % 8
                        g = (half * 2 + hh) * 8 + ch
                        if gi >= 4:
                            e.wait_ge(s_dv.h, dvY0 + gi - 3)
                        e.matmul(ps[gi % 4][:], mk_sb[:, g * 128:(g + 1) * 128],
                                 xhat[:, ch * 512:(ch + 1) * 512],
                                 start=True, stop=True).then_inc(s_pe.h, 1)
                s_pe.n = peY0 + 16

                @block.vector
                def _(e, peY0=peY0):
                    for gi in range(16):
                        e.wait_ge(s_pe.h, peY0 + gi + 1)
                        e.tensor_copy(out=y_sb[:, gi * 512:(gi + 1) * 512],
                                      in_=ps[gi % 4][:]).then_inc(s_dv.h, 1)
                s_dv.n = dvY0 + 16
                dvY_end = s_dv.n

                peZ0, acZ0 = s_pe.n, s_ac.n

                @block.tensor
                def _(e, half=half, peZ0=peZ0, acZ0=acZ0, dvY_end=dvY_end):
                    e.wait_ge(s_dv.h, dvY_end)
                    for gi in range(16):
                        hh, ch = gi // 8, gi % 8
                        g = (half * 2 + hh) * 8 + ch
                        if gi >= 2:
                            e.wait_ge(s_ac.h, acZ0 + gi - 1)
                        bb = 4 * (gi % 2)
                        for m in range(4):
                            ins = e.matmul(ps[bb + m][:],
                                           up_sb[:, g * 512 + m * 128:g * 512 + m * 128 + 128],
                                           y_sb[:, gi * 512:(gi + 1) * 512],
                                           start=True, stop=True)
                        ins.then_inc(s_pe.h, 1)
                s_pe.n = peZ0 + 16

                @block.scalar
                def _(e, half=half, peZ0=peZ0):
                    for gi in range(16):
                        e.wait_ge(s_pe.h, peZ0 + gi + 1)
                        bb = 4 * (gi % 2)
                        for m in range(4):
                            kc = (gi // 8) * 32 + (gi % 8) * 4 + m
                            ins = e.activation(out=h_sb[:, kc * 512:(kc + 1) * 512],
                                               in_=ps[bb + m][:], func=AF.Gelu)
                        ins.then_inc(s_ac.h, 1)
                s_ac.n = acZ0 + 16
                ac_end = s_ac.n

                peC0 = s_pe.n
                bdma0 = [band_gen[r] for r in range(4)]

                @block.sync
                def _(e, half=half, peC0=peC0, bdma0=bdma0):
                    for kc in range(64):
                        r = kc % 4
                        gen = bdma0[r] + kc // 4
                        if kc >= 4:
                            e.wait_ge(s_pe.h, peC0 + kc - 3)
                        elif peC0 > 0:
                            e.wait_ge(s_pe.h, peC0)
                        e.dma_start(out=wband[:, r * 1024:(r + 1) * 1024],
                                    in_=wctbv[half * 8192 + kc * 128:half * 8192 + (kc + 1) * 128, :]
                                    ).then_inc(s_band[r].h, 16)
                for r in range(4):
                    band_gen[r] += 16
                    s_band[r].n += 16 * 16

                @block.tensor
                def _(e, half=half, peC0=peC0, ac_end=ac_end, bdma0=bdma0):
                    e.wait_ge(s_ac.h, ac_end)
                    for kc in range(64):
                        r = kc % 4
                        e.wait_ge(s_band[r].h, 16 * (bdma0[r] + kc // 4 + 1))
                        for m in range(8):
                            ins = e.matmul(ps[m][:],
                                           wband[:, r * 1024 + m * 128:r * 1024 + (m + 1) * 128],
                                           h_sb[:, kc * 512:(kc + 1) * 512],
                                           start=(kc == 0), stop=(kc == 63),
                                           skip_group_check=True)
                        ins.then_inc(s_pe.h, 1)
                s_pe.n = peC0 + 64
                peC_end = s_pe.n

                st_prev = s_st.n
                dv_half0 = s_dv.n

                @block.vector
                def _(e, half=half, peC_end=peC_end, st_prev=st_prev,
                      dv_half0=dv_half0):
                    e.wait_ge(s_pe.h, peC_end)
                    if half == 0:
                        if st_prev > 0:
                            e.wait_ge(s_st.h, st_prev)
                        for m in range(8):
                            ins = e.tensor_copy(out=oacc[:, m * 512:(m + 1) * 512], in_=ps[m][:])
                        ins.then_inc(s_dv.h, 1)
                    else:
                        e.wait_ge(s_dv.h, dv_half0)
                        for m in range(8):
                            ins = e.tensor_tensor(out=oacc[:, m * 512:(m + 1) * 512],
                                                  in0=oacc[:, m * 512:(m + 1) * 512],
                                                  in1=ps[m][:], op=ALU.add)
                        p.wmark("v", e, p.mark("v", ins))
                        for m in range(8):
                            ins = e.tensor_tensor(out=oacc[:, m * 512:(m + 1) * 512],
                                                  in0=oacc[:, m * 512:(m + 1) * 512],
                                                  in1=ws["rt"][:, m * 512:(m + 1) * 512],
                                                  op=ALU.add)
                        ins.then_inc(s_dv.h, 1)
                s_dv.n += 1

            dv_done = s_dv.n

            @block.sync
            def _(e, t0=t0, dv_done=dv_done):
                e.wait_ge(s_dv.h, dv_done)
                e.dma_start(out=rv[:, :, t0:t0 + 512],
                            in_=oacc.rearrange("q (j c) -> q j c", j=8)).then_inc(s_st.h, 16)
            s_st.n += 16

        @block.sync
        def _(e):
            e.wait_ge(s_st.h, s_st.n)




def emit_precast(p, pairs, tag, early_jobs=None, stack=None):
    nc, block = p.nc, p.block
    s_ld = [p.sem(tag + "_ld0"), p.sem(tag + "_ld1")]
    s_st = [p.sem(tag + "_st0"), p.sem(tag + "_st1")]
    ld0 = [x.n for x in s_ld]
    st0 = [x.n for x in s_st]
    W = 4096
    jobs = []
    for src, dst in pairs:
        R, M = src.shape
        assert R % 128 == 0 and M <= W
        for r0 in range(0, R, 128):
            jobs.append((src, dst, r0, M))
    if stack is not None:
        buf = stack.enter_context(nc.sbuf_tensor(tag + "_buf", [128, 2 * W], BF16))
        cmgr = contextlib.nullcontext()
    else:
        cmgr = nc.sbuf_tensor(tag + "_buf", [128, 2 * W], BF16)
    with cmgr as _buf:
        if stack is None:
            buf = _buf
        @block.gpsimd
        def _(e):
            for i, (src, dst, r0, M) in enumerate(jobs):
                k = i % 2
                if i >= 2:
                    e.wait_ge(s_st[k].h, st0[k] + 16 * (i // 2))
                e.dma_start(out=buf[:, k * W:k * W + M],
                            in_=src[r0:r0 + 128, :]).then_inc(s_ld[k].h, 16)
        for k in range(2):
            s_ld[k].n = ld0[k] + 16 * ((len(jobs) + 1 - k) // 2)

        @block.sync
        def _(e):
            for i, (src, dst, r0, M) in enumerate(jobs):
                k = i % 2
                e.wait_ge(s_ld[k].h, ld0[k] + 16 * (i // 2 + 1))
                e.dma_start(out=dst[r0:r0 + 128, :],
                            in_=buf[:, k * W:k * W + M]).then_inc(s_st[k].h, 16)
        for k in range(2):
            s_st[k].n = st0[k] + 16 * ((len(jobs) + 1 - k) // 2)
        if early_jobs is None:
            @block.sync
            def _(e):
                for k in range(2):
                    e.wait_ge(s_st[k].h, s_st[k].n)
        if early_jobs is not None:
            ne = early_jobs
            p.pc_info = {
                "sems": (s_st[0], s_st[1]),
                "early": (st0[0] + 16 * ((ne + 1) // 2), st0[1] + 16 * (ne // 2)),
                "full": (s_st[0].n, s_st[1].n),
            }


def emit_l1f(p, res_dram, wd):
    nc, block = p.nc, p.block
    rv = res_dram.rearrange("(j q) c -> q j c", q=128)

    with contextlib.ExitStack() as stack:
        ps = [stack.enter_context(nc.psum_tensor("f1_ps%d" % i, [128, 512], F32))
              for i in range(8)]
        ws = ln_workspace(p, stack, "f1", ps0=ps[0], ps1=ps[1])
        xhat = stack.enter_context(nc.sbuf_tensor("f1_xh", [128, 8 * 512], BF16))
        mk_sb = stack.enter_context(nc.sbuf_tensor("f1_mk", [128, 32 * 128], BF16))
        up_sb = stack.enter_context(nc.sbuf_tensor("f1_up", [128, 32 * 512], BF16))
        y_sb = stack.enter_context(nc.sbuf_tensor("f1_y", [128, 16 * 512], BF16))
        h_sb = stack.enter_context(nc.sbuf_tensor("f1_h", [128, 64 * 512], BF16))
        oacc = stack.enter_context(nc.sbuf_tensor("f1_oa", [128, 8 * 512], F32))
        wband = stack.enter_context(nc.sbuf_tensor("f1_wb", [128, 4 * 1024], BF16))

        s_w = p.sem("f1_w")
        s_band = [p.sem("f1_b%d" % i) for i in range(4)]
        s_pe = p.sem("f1_pe")
        s_dv = p.sem("f1_dv")
        s_ac = p.sem("f1_ac")
        s_st = p.sem("f1_st")

        emit_ln_init(p, ws)

        @block.gpsimd
        def _(e):
            e.dma_start(out=mk_sb.rearrange("q (g m) -> q g m", g=32),
                        in_=wd["maker_bd"].rearrange("h c q m -> q (h c) m")).then_inc(s_w.h, 16)
            e.dma_start(out=up_sb.rearrange("q (g m) -> q g m", g=32),
                        in_=wd["up_bd"].rearrange("h c q m -> q (h c) m")).then_inc(s_w.h, 16)
        s_w.n += 32
        w_done = s_w.n

        @block.tensor
        def _(e):
            e.wait_ge(s_w.h, w_done)

        band_gen = [0] * 4
        wctbv = wd["wctb"]

        for tt in range(8):
            t0 = tt * 512
            prev_pe = s_pe.n
            prev_dv = s_dv.n
            emit_ln_tile(p, res_dram, ws, t0, xhat, 0,
                         pre_apply_waits=[(s_pe.h, prev_pe)],
                         pre_load_waits=[(s_dv.h, prev_dv)],
                         pre_stats_waits=[(s_dv.h, prev_dv)])
            xh_ready = ws["s_ac"].n

            for half in range(2):
                peY0, dvY0 = s_pe.n, s_dv.n

                @block.tensor
                def _(e, half=half, xh_ready=xh_ready, peY0=peY0, dvY0=dvY0):
                    e.wait_ge(ws["s_ac"].h, xh_ready)
                    if dvY0 > 0:
                        e.wait_ge(s_dv.h, dvY0)
                    for gi in range(16):
                        hh, ch = gi // 8, gi % 8
                        g = (half * 2 + hh) * 8 + ch
                        if gi >= 4:
                            e.wait_ge(s_dv.h, dvY0 + gi - 3)
                        e.matmul(ps[gi % 4][:], mk_sb[:, g * 128:(g + 1) * 128],
                                 xhat[:, ch * 512:(ch + 1) * 512],
                                 start=True, stop=True).then_inc(s_pe.h, 1)
                s_pe.n = peY0 + 16

                @block.vector
                def _(e, peY0=peY0):
                    for gi in range(16):
                        e.wait_ge(s_pe.h, peY0 + gi + 1)
                        e.tensor_copy(out=y_sb[:, gi * 512:(gi + 1) * 512],
                                      in_=ps[gi % 4][:]).then_inc(s_dv.h, 1)
                s_dv.n = dvY0 + 16
                dvY_end = s_dv.n

                peZ0, acZ0 = s_pe.n, s_ac.n

                @block.tensor
                def _(e, half=half, peZ0=peZ0, acZ0=acZ0, dvY_end=dvY_end):
                    e.wait_ge(s_dv.h, dvY_end)
                    for gi in range(16):
                        hh, ch = gi // 8, gi % 8
                        g = (half * 2 + hh) * 8 + ch
                        if gi >= 2:
                            e.wait_ge(s_ac.h, acZ0 + gi - 1)
                        bb = 4 * (gi % 2)
                        for m in range(4):
                            ins = e.matmul(ps[bb + m][:],
                                           up_sb[:, g * 512 + m * 128:g * 512 + m * 128 + 128],
                                           y_sb[:, gi * 512:(gi + 1) * 512],
                                           start=True, stop=True)
                        ins.then_inc(s_pe.h, 1)
                s_pe.n = peZ0 + 16

                @block.scalar
                def _(e, half=half, peZ0=peZ0):
                    for gi in range(16):
                        e.wait_ge(s_pe.h, peZ0 + gi + 1)
                        bb = 4 * (gi % 2)
                        for m in range(4):
                            kc = (gi // 8) * 32 + (gi % 8) * 4 + m
                            ins = e.activation(out=h_sb[:, kc * 512:(kc + 1) * 512],
                                               in_=ps[bb + m][:], func=AF.Gelu)
                        ins.then_inc(s_ac.h, 1)
                s_ac.n = acZ0 + 16
                ac_end = s_ac.n

                peC0 = s_pe.n
                bdma0 = [band_gen[r] for r in range(4)]

                @block.sync
                def _(e, half=half, peC0=peC0, bdma0=bdma0):
                    for kc in range(64):
                        r = kc % 4
                        gen = bdma0[r] + kc // 4
                        if kc >= 4:
                            e.wait_ge(s_pe.h, peC0 + kc - 3)
                        elif peC0 > 0:
                            e.wait_ge(s_pe.h, peC0)
                        e.dma_start(out=wband[:, r * 1024:(r + 1) * 1024],
                                    in_=wctbv[half * 8192 + kc * 128:half * 8192 + (kc + 1) * 128, :]
                                    ).then_inc(s_band[r].h, 16)
                for r in range(4):
                    band_gen[r] += 16
                    s_band[r].n += 16 * 16

                @block.tensor
                def _(e, half=half, peC0=peC0, ac_end=ac_end, bdma0=bdma0):
                    e.wait_ge(s_ac.h, ac_end)
                    for kc in range(64):
                        r = kc % 4
                        e.wait_ge(s_band[r].h, 16 * (bdma0[r] + kc // 4 + 1))
                        for m in range(8):
                            ins = e.matmul(ps[m][:],
                                           wband[:, r * 1024 + m * 128:r * 1024 + (m + 1) * 128],
                                           h_sb[:, kc * 512:(kc + 1) * 512],
                                           start=(kc == 0), stop=(kc == 63),
                                           skip_group_check=True)
                        ins.then_inc(s_pe.h, 1)
                s_pe.n = peC0 + 64
                peC_end = s_pe.n

                st_prev = s_st.n
                dv_half0 = s_dv.n

                @block.vector
                def _(e, half=half, peC_end=peC_end, st_prev=st_prev,
                      dv_half0=dv_half0):
                    e.wait_ge(s_pe.h, peC_end)
                    if half == 0:
                        if st_prev > 0:
                            e.wait_ge(s_st.h, st_prev)
                        for m in range(8):
                            ins = e.tensor_copy(out=oacc[:, m * 512:(m + 1) * 512], in_=ps[m][:])
                        ins.then_inc(s_dv.h, 1)
                    else:
                        e.wait_ge(s_dv.h, dv_half0)
                        for m in range(8):
                            ins = e.tensor_tensor(out=oacc[:, m * 512:(m + 1) * 512],
                                                  in0=oacc[:, m * 512:(m + 1) * 512],
                                                  in1=ps[m][:], op=ALU.add)
                        p.wmark("v", e, p.mark("v", ins))
                        for m in range(8):
                            ins = e.tensor_tensor(out=oacc[:, m * 512:(m + 1) * 512],
                                                  in0=oacc[:, m * 512:(m + 1) * 512],
                                                  in1=ws["rt"][:, m * 512:(m + 1) * 512],
                                                  op=ALU.add)
                        ins.then_inc(s_dv.h, 1)
                s_dv.n += 1

            dv_done = s_dv.n

            @block.sync
            def _(e, t0=t0, dv_done=dv_done):
                e.wait_ge(s_dv.h, dv_done)
                e.dma_start(out=rv[:, :, t0:t0 + 512],
                            in_=oacc.rearrange("q (j c) -> q j c", j=8)).then_inc(s_st.h, 16)
            s_st.n += 16

        @block.sync
        def _(e):
            e.wait_ge(s_st.h, s_st.n)



def emit_l0f(p, res_dram, wd, consts):
    nc, block = p.nc, p.block
    rv = res_dram.rearrange("(j q) c -> q j c", q=128)
    G_TOK = 1024
    N_GRP = NTOK // G_TOK
    N_TL = G_TOK // 512
    N_TB = G_TOK // 128

    with contextlib.ExitStack() as stack:
        ps = [stack.enter_context(nc.psum_tensor("f0_ps%d" % i, [128, 512], F32))
              for i in range(8)]
        ws = ln_workspace(p, stack, "f0", ps0=ps[0], ps1=ps[1])
        xhat = stack.enter_context(nc.sbuf_tensor("f0_xh", [128, 8 * G_TOK], BF16))
        acc = stack.enter_context(nc.sbuf_tensor("f0_acc", [128, 8 * G_TOK], F32))
        w1r = stack.enter_context(nc.sbuf_tensor("f0_w1", [128, 2 * 8192], BF16))
        w2r = stack.enter_context(nc.sbuf_tensor("f0_w2", [128, 2 * 8192], BF16))
        hb = stack.enter_context(nc.sbuf_tensor("f0_hb", [128, 8 * 512], BF16))
        gql = stack.enter_context(nc.sbuf_tensor("f0_gq", [128, G_TOK], F32))
        gate_sb = stack.enter_context(nc.sbuf_tensor("f0_gt", [128, 8 * 14], BF16))
        gA = stack.enter_context(nc.sbuf_tensor("f0_gA", [128, 16 * 14], F32))
        gB = stack.enter_context(nc.sbuf_tensor("f0_gB", [128, 16 * 14], F32))
        gS = stack.enter_context(nc.sbuf_tensor("f0_gS", [128, 6 * 16], F32))
        grow = stack.enter_context(nc.sbuf_tensor("f0_gr", [14, G_TOK], F32))
        gtmp = stack.enter_context(nc.sbuf_tensor("f0_gm", [1, G_TOK], F32))
        gtmpb = stack.enter_context(nc.sbuf_tensor("f0_gmb", [1, G_TOK], BF16))
        b1s = stack.enter_context(nc.sbuf_tensor("f0_b1", [128, 16], F32))
        b2s = stack.enter_context(nc.sbuf_tensor("f0_b2", [128, 8], F32))
        tm = stack.enter_context(nc.sbuf_tensor("f0_tm", [128, 2 * 512], F32))

        s_pe = p.sem("f0_pe")
        s_dv = p.sem("f0_dv")
        s_ac = p.sem("f0_ac")
        s_w1 = [p.sem("f0_w1a"), p.sem("f0_w1b")]
        s_w2 = [p.sem("f0_w2a"), p.sem("f0_w2b")]
        s_g = p.sem("f0_g")
        s_st = p.sem("f0_st")

        emit_ln_init(p, ws)

        @block.gpsimd
        def _(e):
            e.dma_start(out=gate_sb.rearrange("q (c n) -> q c n", c=8),
                        in_=wd["gateT"].rearrange("(c q) n -> q c n", q=128)
                        ).then_inc(s_g.h, 16)
        s_g.n += 16
        gate_done = s_g.n

        w1v = wd["w1_b"].rearrange("(e d) f -> e d f", e=14)
        w2v = wd["w2_b"].rearrange("(e f) d -> e f d", e=14)
        w_free = {"w1": [0, 0], "w2": [0, 0]}

        for grp in range(N_GRP):
            gbase = grp * G_TOK

            for t in range(N_TL):
                prev_pe = s_pe.n
                prev_dv = s_dv.n
                emit_ln_tile(p, res_dram, ws, gbase + t * 512, xhat, t * 512,
                             xhat_f32=None, xh_stride=G_TOK,
                             acc_out=acc, acc_off=t * 512, acc_stride=G_TOK,
                             pre_apply_waits=[(s_pe.h, prev_pe), (s_st.h, s_st.n)],
                             pre_load_waits=[(s_dv.h, prev_dv)],
                             pre_stats_waits=[(s_dv.h, prev_dv), (s_ac.h, s_ac.n)])

            pe_g0 = s_pe.n
            ln_apply = ws["s_ac"].n

            @block.tensor
            def _(e, pe_g0=pe_g0, ln_apply=ln_apply, dv0=s_dv.n):
                e.wait_ge(ws["s_ac"].h, ln_apply)
                e.wait_ge(s_g.h, gate_done)
                if dv0 > 0:
                    e.wait_ge(s_dv.h, dv0)
                for tb in range(N_TB):
                    pb = ps[2 + tb % 2]
                    for k in range(8):
                        ins = e.matmul(pb[:, 0:14],
                                       xhat[:, k * G_TOK + tb * 128:k * G_TOK + (tb + 1) * 128],
                                       gate_sb[:, k * 14:(k + 1) * 14],
                                       start=(k == 0), stop=(k == 7))
                    ins.then_inc(s_pe.h, 1)
                    if tb >= 1:
                        e.wait_ge(s_dv.h, dv0 + 2 * tb)
            s_pe.n = pe_g0 + N_TB

            @block.vector
            def _(e, pe_g0=pe_g0, dv0=s_dv.n):
                for tb in range(N_TB):
                    pb = ps[2 + tb % 2]
                    e.wait_ge(s_pe.h, pe_g0 + tb + 1)
                    m1 = gS[:, 0 * 16 + tb:0 * 16 + tb + 1]
                    ins = e.reduce_max(out=m1, in_=pb[:, 0:14], axis=AX.X)
                    p.wmark("v", e, p.mark("v", ins))
                    ins = e.tensor_scalar(out=gB[:, tb * 14:(tb + 1) * 14], in0=pb[:, 0:14],
                                          scalar1=m1, scalar2=None, op0=ALU.is_ge)
                    p.wmark("v", e, p.mark("v", ins))
                    ins = e.scalar_tensor_tensor(out=gA[:, tb * 14:(tb + 1) * 14],
                                                 in0=gB[:, tb * 14:(tb + 1) * 14],
                                                 scalar=-1e30, in1=pb[:, 0:14],
                                                 op0=ALU.mult, op1=ALU.add)
                    ins.then_inc(s_dv.h, 1)
                    e.wait_ge(s_dv.h, dv0 + 2 * tb + 1)
                    m2 = gS[:, 1 * 16 + tb:1 * 16 + tb + 1]
                    ins = e.reduce_max(out=m2, in_=gA[:, tb * 14:(tb + 1) * 14], axis=AX.X)
                    p.wmark("v", e, p.mark("v", ins))
                    ins = e.tensor_tensor(out=gS[:, 2 * 16 + tb:2 * 16 + tb + 1],
                                          in0=m1, in1=m2, op=ALU.subtract)
                    ins.then_inc(s_dv.h, 1)
                p.wmark("v", e)
            s_dv.n += 2 * N_TB
            d_done = s_dv.n

            @block.scalar
            def _(e, d_done=d_done):
                e.wait_ge(s_dv.h, d_done)
                ins = e.activation(out=gS[:, 3 * 16:3 * 16 + N_TB], in_=gS[:, 2 * 16:2 * 16 + N_TB],
                                   func=AF.Sigmoid)
                ins.then_inc(s_ac.h, 1)
            s_ac.n += 1
            sig_done = s_ac.n

            @block.vector
            def _(e, sig_done=sig_done):
                e.wait_ge(s_ac.h, sig_done)
                ins = e.tensor_scalar(out=gS[:, 4 * 16:4 * 16 + N_TB], in0=gS[:, 3 * 16:3 * 16 + N_TB],
                                      scalar1=-1.0, scalar2=1.0, op0=ALU.mult, op1=ALU.add)
                p.wmark("v", e, p.mark("v", ins))
                for tb in range(N_TB):
                    w0 = gS[:, 3 * 16 + tb:3 * 16 + tb + 1]
                    w1_ = gS[:, 4 * 16 + tb:4 * 16 + tb + 1]
                    m2 = gS[:, 1 * 16 + tb:1 * 16 + tb + 1]
                    ins = e.tensor_scalar(out=gA[:, tb * 14:(tb + 1) * 14],
                                          in0=gA[:, tb * 14:(tb + 1) * 14],
                                          scalar1=m2, scalar2=w1_,
                                          op0=ALU.is_ge, op1=ALU.mult)
                    p.wmark("v", e, p.mark("v", ins))
                    ins = e.scalar_tensor_tensor(out=gB[:, tb * 14:(tb + 1) * 14],
                                                 in0=gB[:, tb * 14:(tb + 1) * 14], scalar=w0,
                                                 in1=gA[:, tb * 14:(tb + 1) * 14],
                                                 op0=ALU.mult, op1=ALU.add)
                    if tb < N_TB - 1:
                        p.wmark("v", e, p.mark("v", ins))
                    else:
                        ins.then_inc(s_dv.h, 1)
            s_dv.n += 1
            G_done = s_dv.n

            pe_t0 = s_pe.n

            @block.tensor
            def _(e, G_done=G_done, pe_t0=pe_t0, dvb=s_dv.n):
                e.wait_ge(s_dv.h, G_done)
                for tb in range(N_TB):
                    pb = ps[4 + tb % 2]
                    if tb >= 2:
                        e.wait_ge(s_dv.h, dvb + tb - 1)
                    ins = e.transpose(pb[0:14, 0:128], gB[:, tb * 14:(tb + 1) * 14],
                                      consts["ident_f32"][:])
                    ins.then_inc(s_pe.h, 1)
            s_pe.n = pe_t0 + N_TB

            @block.vector
            def _(e, pe_t0=pe_t0, g_all=s_g.n):
                e.wait_ge(s_g.h, g_all)
                for tb in range(N_TB):
                    pb = ps[4 + tb % 2]
                    e.wait_ge(s_pe.h, pe_t0 + tb + 1)
                    ins = e.tensor_copy(out=grow[:, tb * 128:(tb + 1) * 128],
                                        in_=pb[0:14, 0:128])
                    ins.then_inc(s_dv.h, 1)
            s_dv.n += N_TB
            grow_done = s_dv.n

            for ei in range(14):
                gld0 = s_g.n

                @block.gpsimd
                def _(e, ei=ei, grp=grp, grow_done=grow_done, dve_need=s_dv.n,
                      ac_need=s_ac.n):
                    if ei == 0:
                        e.wait_ge(s_dv.h, grow_done)
                    else:
                        e.wait_ge(s_dv.h, dve_need)
                    if ac_need > 0:
                        e.wait_ge(s_ac.h, ac_need)
                    e.dma_start(out=gtmp[:], in_=grow[ei:ei + 1, :]).then_inc(s_g.h, 16)
                    e.dma_start(out=b1s[:], in_=wd["b1T"][ei]).then_inc(s_g.h, 16)
                    e.dma_start(out=b2s[:], in_=wd["b2T"][ei]).then_inc(s_g.h, 16)
                s_g.n += 48
                g_ld = s_g.n

                pe_b0 = s_pe.n

                @block.vector
                def _(e, g_ld=g_ld, dv0=s_dv.n, pen=s_pe.n):
                    e.wait_ge(s_g.h, g_ld)
                    e.wait_ge(s_pe.h, pen)
                    p.wmark("v", e)
                    ins = e.tensor_copy(out=gtmpb[:], in_=gtmp[:])
                    ins.then_inc(s_dv.h, 1)
                s_dv.n += 1
                gb_cast = s_dv.n

                @block.tensor
                def _(e, gb_cast=gb_cast, pe_b0=pe_b0, dv0=s_dv.n):
                    e.wait_ge(s_dv.h, gb_cast)
                    for t in range(N_TL):
                        pb = ps[2 + t % 2]
                        ins = e.matmul(pb[:], ws["ones1b"][:],
                                       gtmpb[:, t * 512:(t + 1) * 512], start=True, stop=True)
                        ins.then_inc(s_pe.h, 1)
                        if t >= 2:
                            e.wait_ge(s_dv.h, dv0 + t - 1)
                s_pe.n = pe_b0 + N_TL

                @block.vector
                def _(e, pe_b0=pe_b0):
                    for t in range(N_TL):
                        pb = ps[2 + t % 2]
                        e.wait_ge(s_pe.h, pe_b0 + t + 1)
                        ins = e.tensor_copy(out=gql[:, t * 512:(t + 1) * 512], in_=pb[:])
                        ins.then_inc(s_dv.h, 1)
                s_dv.n += N_TL
                gq_done = s_dv.n

                for fh in range(2):
                    slot = fh

                    @block.sync
                    def _(e, ei=ei, fh=fh, slot=slot, grp=grp,
                          w1_need=w_free["w1"][fh], w2_need=w_free["w2"][fh]):
                        if w1_need > 0:
                            e.wait_ge(s_pe.h, w1_need)
                        e.dma_start(out=w1r[:, slot * 8192:(slot + 1) * 8192]
                                    .rearrange("q (kc c) -> q kc c", kc=8),
                                    in_=w1v[ei].rearrange("(kc q) f -> q kc f", q=128)
                                    [:, :, fh * 1024:(fh + 1) * 1024]).then_inc(s_w1[slot].h, 16)

                    @block.gpsimd
                    def _(e, ei=ei, fh=fh, slot=slot, grp=grp,
                          w2_need=w_free["w2"][fh]):
                        if w2_need > 0:
                            e.wait_ge(s_pe.h, w2_need)
                        e.dma_start(out=w2r[:, slot * 8192:(slot + 1) * 8192]
                                    .rearrange("q (kc c) -> q kc c", kc=8),
                                    in_=w2v[ei][fh * 1024:(fh + 1) * 1024, :]
                                    .rearrange("(kc q) d -> q kc d", q=128)).then_inc(s_w2[slot].h, 16)
                    s_w1[slot].n += 16
                    s_w2[slot].n += 16

                for fh in range(2):
                    slot = fh
                    w1_ld = s_w1[slot].n
                    w2_ld = s_w2[slot].n
                    for t in range(N_TL):
                        pe_h0 = s_pe.n
                        ac_h0 = s_ac.n
                        dv_h0 = s_dv.n

                        @block.tensor
                        def _(e, fh=fh, t=t, slot=slot, pe_h0=pe_h0, w1_ld=w1_ld,
                              w2_ld=w2_ld, gq_done=gq_done, dv_h0=dv_h0, ac_h0=ac_h0):
                            e.wait_ge(s_w1[slot].h, w1_ld)
                            if fh == 0 and t == 0:
                                e.wait_ge(s_dv.h, gq_done)
                            for w in range(2):
                                if dv_h0 > 0:
                                    e.wait_ge(s_dv.h, dv_h0 - (1 - w))
                                for m in range(4 * w, 4 * w + 4):
                                    for k in range(8):
                                        ins = e.matmul(
                                            ps[m][:],
                                            w1r[:, slot * 8192 + (k * 8 + m) * 128:
                                                slot * 8192 + (k * 8 + m) * 128 + 128],
                                            xhat[:, k * G_TOK + t * 512:k * G_TOK + (t + 1) * 512],
                                            start=(k == 0), stop=(k == 7),
                                            skip_group_check=True)
                                ins.then_inc(s_pe.h, 1)
                            e.wait_ge(s_w2[slot].h, w2_ld)
                            for w in range(2):
                                e.wait_ge(s_ac.h, ac_h0 + 2)
                                for m in range(4 * w, 4 * w + 4):
                                    for k in range(8):
                                        ins = e.matmul(
                                            ps[m][:],
                                            w2r[:, slot * 8192 + (k * 8 + m) * 128:
                                                slot * 8192 + (k * 8 + m) * 128 + 128],
                                            hb[:, k * 512:(k + 1) * 512],
                                            start=(k == 0), stop=(k == 7),
                                            skip_group_check=True)
                                ins.then_inc(s_pe.h, 1)
                        s_pe.n = pe_h0 + 4
                        if t == N_TL - 1:
                            w_free["w1"][fh] = pe_h0 + 2
                            w_free["w2"][fh] = s_pe.n

                        @block.scalar
                        def _(e, fh=fh, pe_h0=pe_h0, ac_h0=ac_h0, dv_h0=dv_h0):
                            for w in range(2):
                                e.wait_ge(s_pe.h, pe_h0 + w + 1)
                                if dv_h0 > 0:
                                    pass
                                for m in range(4 * w, 4 * w + 4):
                                    ins = e.activation(out=hb[:, m * 512:(m + 1) * 512],
                                                       in_=ps[m][:], func=AF.Gelu,
                                                       bias=b1s[:, fh * 8 + m:fh * 8 + m + 1])
                                ins.then_inc(s_ac.h, 1)
                        s_ac.n = ac_h0 + 2

                        @block.vector
                        def _(e, t=t, pe_h0=pe_h0, dv_h0=dv_h0):
                            for w in range(2):
                                e.wait_ge(s_pe.h, pe_h0 + 2 + w + 1)
                                for m in range(4 * w, 4 * w + 4):
                                    sl = m % 2
                                    ins = e.tensor_mul(out=tm[:, sl * 512:(sl + 1) * 512],
                                                       in0=ps[m][:],
                                                       in1=gql[:, t * 512:(t + 1) * 512])
                                    p.wmark("v", e, p.mark("v", ins))
                                    ins = e.tensor_tensor(
                                        out=acc[:, m * G_TOK + t * 512:m * G_TOK + (t + 1) * 512],
                                        in0=acc[:, m * G_TOK + t * 512:m * G_TOK + (t + 1) * 512],
                                        in1=tm[:, sl * 512:(sl + 1) * 512], op=ALU.add)
                                    if m != 4 * w + 3:
                                        p.wmark("v", e, p.mark("v", ins))
                                ins.then_inc(s_dv.h, 1)
                        s_dv.n += 2

                @block.vector
                def _(e, dvp=s_dv.n):
                    p.wmark("v", e)
                    e.wait_ge(s_dv.h, dvp)
                    for m in range(8):
                        for t in range(N_TL):
                            ins = e.scalar_tensor_tensor(
                                out=acc[:, m * G_TOK + t * 512:m * G_TOK + (t + 1) * 512],
                                in0=gql[:, t * 512:(t + 1) * 512],
                                scalar=b2s[:, m:m + 1],
                                in1=acc[:, m * G_TOK + t * 512:m * G_TOK + (t + 1) * 512],
                                op0=ALU.mult, op1=ALU.add)
                    ins.then_inc(s_dv.h, 1)
                s_dv.n += 1

            all_exp = s_dv.n
            for t in range(N_TL):
                ld_t = None

                @block.sync
                def _(e, t=t, grp=grp, all_exp=all_exp, dvn=s_dv.n):
                    e.wait_ge(s_dv.h, dvn)
                    e.dma_start(out=ws["rt"].rearrange("q (j c) -> q j c", j=8),
                                in_=rv[:, :, grp * G_TOK + t * 512:grp * G_TOK + (t + 1) * 512]
                                ).then_inc(ws["s_ld"].h, 16)
                ws["s_ld"].n += 16
                rt_ld = ws["s_ld"].n

                @block.vector
                def _(e, t=t, rt_ld=rt_ld, all_exp=all_exp):
                    e.wait_ge(ws["s_ld"].h, rt_ld)
                    e.wait_ge(s_dv.h, all_exp)
                    p.wmark("v", e)
                    for m in range(8):
                        sl = m % 2
                        ins = e.tensor_scalar(out=tm[:, sl * 512:(sl + 1) * 512],
                                              in0=acc[:, m * G_TOK + t * 512:m * G_TOK + (t + 1) * 512],
                                              scalar1=-100.0, scalar2=100.0,
                                              op0=ALU.max, op1=ALU.min)
                        p.wmark("v", e, p.mark("v", ins))
                        ins = e.tensor_tensor(
                            out=acc[:, m * G_TOK + t * 512:m * G_TOK + (t + 1) * 512],
                            in0=ws["rt"][:, m * 512:(m + 1) * 512],
                            in1=tm[:, sl * 512:(sl + 1) * 512], op=ALU.add)
                        if m < 7:
                            p.wmark("v", e, p.mark("v", ins))
                    ins.then_inc(s_dv.h, 1)
                s_dv.n += 1
                wr_ok = s_dv.n

                @block.sync
                def _(e, t=t, grp=grp, wr_ok=wr_ok):
                    e.wait_ge(s_dv.h, wr_ok)
                    e.dma_start(out=rv[:, :, grp * G_TOK + t * 512:grp * G_TOK + (t + 1) * 512],
                                in_=acc.rearrange("q (j c) -> q j c", j=8)
                                [:, :, t * 512:(t + 1) * 512]).then_inc(s_st.h, 16)
                s_st.n += 16

        @block.sync
        def _(e):
            e.wait_ge(s_st.h, s_st.n)



QCH = [3] * 10 + [2]
QW_ = [w * 32 for w in QCH]


def emit_l1a(p, res_dram, wd, consts):
    nc, block = p.nc, p.block
    TQ = 2048

    with contextlib.ExitStack() as stack:
        psA = stack.enter_context(nc.psum_tensor("a1_psA", [128, 512], F32))
        pd0 = stack.enter_context(nc.psum_tensor("a1_pd0", [128, 1024], F32))
        pd1 = stack.enter_context(nc.psum_tensor("a1_pd1", [128, 1024], F32))
        psN = stack.enter_context(nc.psum_tensor("a1_psN", [128, 512], F32))
        tp0 = stack.enter_context(nc.psum_tensor("a1_tp0", [128, 512], BF16))
        ws = ln_workspace(p, stack, "a1", ps0=pd1[:, 0:512], ps1=pd1[:, 512:1024])
        qwin = stack.enter_context(nc.sbuf_tensor("a1_qw", [128, 11 * 512], BF16))
        kr = stack.enter_context(nc.sbuf_tensor("a1_kr", [128, 11 * TQ], BF16))
        dw = stack.enter_context(nc.sbuf_tensor("a1_dw", [128, 8 * 512], BF16))
        vte = stack.enter_context(nc.sbuf_tensor("a1_vt", [128, 16 * 16 * 65], BF16))
        xd = stack.enter_context(nc.sbuf_tensor("a1_xd", [128, 8 * TQ], BF16))
        wring = stack.enter_context(nc.sbuf_tensor("a1_wrg", [128, 2 * 1024], BF16))
        wprj = stack.enter_context(nc.sbuf_tensor("a1_wp", [128, 2 * 1024], BF16))
        rq = stack.enter_context(nc.sbuf_tensor("a1_rq", [128, 512], BF16))
        ropec = stack.enter_context(nc.sbuf_tensor("a1_rc", [96, 2 * 2048], BF16))
        perm = stack.enter_context(nc.sbuf_tensor("a1_pm", [128, 128], BF16))
        masks = stack.enter_context(nc.sbuf_tensor("a1_mk", [128, 4 * 512], BF16))
        prb = stack.enter_context(nc.sbuf_tensor("a1_pb", [128, 2 * 1024], BF16))
        dn = stack.enter_context(nc.sbuf_tensor("a1_dn", [128, 3 * 512], F32))
        asb = stack.enter_context(nc.sbuf_tensor("a1_asb", [128, 4 * 512], F32))
        srow2 = stack.enter_context(nc.sbuf_tensor("a1_s2", [1, 512], F32))
        srow2b = stack.enter_context(nc.sbuf_tensor("a1_s2b", [1, 2 * 512], BF16))
        lamv = stack.enter_context(nc.sbuf_tensor("a1_lv", [32, 4], F32))
        lamb = stack.enter_context(nc.sbuf_tensor("a1_lb", [128, 2], F32))
        c08 = None
        vsb = stack.enter_context(nc.sbuf_tensor("a1_vsb", [128, 512], BF16))

        s_pe = p.sem("a1_pe")
        s_dv = p.sem("a1_dv")
        s_ac = p.sem("a1_ac")
        s_w = p.sem("a1_w")
        s_wr = [p.sem("a1_wr0"), p.sem("a1_wr1")]
        s_st = p.sem("a1_st")
        wfree = {0: 0, 1: 0}
        obank = [pd0[:, 0:512], pd0[:, 512:1024], pd1[:, 0:512], pd1[:, 512:1024]]

        emit_ln_init(p, ws)

        @block.gpsimd
        def _(e):
            e.dma_start(out=ropec[:, 0:2048], in_=consts["kc2"]).then_inc(s_w.h, 16)
            e.dma_start(out=ropec[:, 2048:4096], in_=consts["ks2"]).then_inc(s_w.h, 16)
            e.dma_start(out=perm[:], in_=consts["perm96"]).then_inc(s_w.h, 16)
            e.dma_start(out=masks.rearrange("q (s c) -> q s c", s=4),
                        in_=consts["cmask"].rearrange("s q c -> q s c")).then_inc(s_w.h, 16)
            e.dma_start(out=lamv[:, 0:1], in_=wd["lq1"]).then_inc(s_w.h, 16)
            e.dma_start(out=lamv[:, 1:2], in_=wd["lk1"]).then_inc(s_w.h, 16)
            e.dma_start(out=lamv[:, 2:3], in_=wd["lq2"]).then_inc(s_w.h, 16)
            e.dma_start(out=lamv[:, 3:4], in_=wd["lk2"]).then_inc(s_w.h, 16)
        s_w.n += 8 * 16
        w_done = s_w.n

        @block.tensor
        def _(e):
            e.wait_ge(s_w.h, w_done)
            e.wait_ge(ws["s_dv"].h, ws["init_done"])
            e.matmul(psN[0:1, 0:1], lamv[:, 0:1], lamv[:, 1:2], start=True, stop=True)
            ins = e.matmul(psN[0:1, 1:2], lamv[:, 2:3], lamv[:, 3:4], start=True, stop=True)
            ins.then_inc(s_pe.h, 1)
        s_pe.n += 1
        lam_mm = s_pe.n

        @block.scalar
        def _(e, lam_mm=lam_mm):
            e.wait_ge(s_pe.h, lam_mm)
            ins = e.activation(out=lamb[0:1, 0:2], in_=psN[0:1, 0:2], func=AF.Exp)
            ins.then_inc(s_ac.h, 1)
        s_ac.n += 1
        lam_e = s_ac.n

        @block.vector
        def _(e, lam_e=lam_e):
            e.wait_ge(s_ac.h, lam_e)
            ins = e.tensor_tensor(out=lamb[0:1, 0:1], in0=lamb[0:1, 0:1],
                                  in1=lamb[0:1, 1:2], op=ALU.subtract)
            p.wmark("v", e, p.mark("v", ins))
            ins = e.tensor_scalar_add(out=lamb[0:1, 0:1], in0=lamb[0:1, 0:1], scalar1=0.2)
            p.wmark("v", e, p.mark("v", ins))
            ins = e.memset(lamb[:, 1:2], 0.0)
            ins.then_inc(s_dv.h, 1)
        s_dv.n += 1
        lam_done = s_dv.n

        @block.tensor
        def _(e, lam_done=lam_done):
            e.wait_ge(s_dv.h, lam_done)
            ins = e.matmul(psN[:, 0:1], ws["ones1"][:], lamb[0:1, 0:1],
                           start=True, stop=True)
            ins.then_inc(s_pe.h, 1)
        s_pe.n += 1
        lam_b = s_pe.n

        @block.vector
        def _(e, lam_b=lam_b):
            e.wait_ge(s_pe.h, lam_b)
            ins = e.tensor_copy(out=lamb[:, 1:2], in_=psN[:, 0:1])
            ins.then_inc(s_dv.h, 1)
        s_dv.n += 1

        wqkv = wd["wqkvp_b"]
        offs = []
        o = 0
        for wdt in QW_:
            offs.append((o, wdt)); o += wdt
        koffs = []
        for wdt in QW_:
            koffs.append((o, wdt)); o += wdt
        voffs = [(o + 128 * i, 128) for i in range(8)]

        for b in range(B_CORE):
            tb0 = b * TQ

            for t in range(4):
                prev_pe = s_pe.n
                prev_dv = s_dv.n
                emit_ln_tile(p, res_dram, ws, tb0 + t * 512, xd, t * 512,
                             xh_stride=TQ,
                             pre_apply_waits=[(s_pe.h, prev_pe), (s_st.h, s_st.n)],
                             pre_load_waits=[(s_dv.h, prev_dv), (s_st.h, s_st.n)],
                             pre_stats_waits=[(s_dv.h, prev_dv), (s_ac.h, s_ac.n)])
            xh_ready = ws["s_ac"].n

            specs = ([("k", c, koffs[c][0], koffs[c][1]) for c in range(11)]
                     + [("v", c, voffs[c][0], voffs[c][1]) for c in range(8)])
            def emit_proj_chunk(kind, c, co, cw, slot, tlist, qw=None):

                @block.gpsimd
                def _(e, co=co, cw=cw, slot=slot, wf=wfree[slot]):
                    if wf > 0:
                        e.wait_ge(s_pe.h, wf)
                    e.dma_start(out=wring[:, slot * 1024:slot * 1024 + 8 * cw]
                                .rearrange("q (k c) -> q k c", k=8),
                                in_=wqkv.rearrange("(k q) n -> q k n", q=128)
                                [:, :, co:co + cw]).then_inc(s_wr[slot].h, 16)
                s_wr[slot].n += 16
                wr_ld = s_wr[slot].n

                for t in tlist:
                    pe_c0 = s_pe.n

                    @block.tensor
                    def _(e, cw=cw, slot=slot, wr_ld=wr_ld, xh_ready=xh_ready,
                          t=t, dv_need=s_dv.n, ac_need=s_ac.n):
                        e.wait_ge(ws["s_ac"].h, xh_ready)
                        e.wait_ge(s_wr[slot].h, wr_ld)
                        if dv_need > 0:
                            e.wait_ge(s_dv.h, dv_need)
                        if ac_need > 0:
                            e.wait_ge(s_ac.h, ac_need)
                        for k in range(8):
                            ins = e.matmul(psA[0:cw, :],
                                           wring[:, slot * 1024 + k * cw:slot * 1024 + (k + 1) * cw],
                                           xd[:, k * TQ + t * 512:k * TQ + (t + 1) * 512],
                                           start=(k == 0), stop=(k == 7))
                        ins.then_inc(s_pe.h, 1)
                    s_pe.n = pe_c0 + 1
                    proj_t = s_pe.n

                    if kind in ("q", "k"):
                        @block.vector
                        def _(e, cw=cw, proj_t=proj_t, dvn=s_dv.n):
                            e.wait_ge(s_pe.h, proj_t)
                            if dvn > 0:
                                e.wait_ge(s_dv.h, dvn)
                            p.wmark("v", e)
                            ins = e.tensor_copy(out=rq[0:cw, :], in_=psA[0:cw, :])
                            ins.then_inc(s_dv.h, 1)
                        s_dv.n += 1
                        rq_done = s_dv.n

                        pe_p0 = s_pe.n

                        @block.tensor
                        def _(e, cw=cw, rq_done=rq_done, dvw=s_dv.n):
                            e.wait_ge(s_dv.h, rq_done)
                            ins = e.matmul(psN[0:cw, :], perm[0:cw, 0:cw], rq[0:cw, :],
                                           start=True, stop=True)
                            ins.then_inc(s_pe.h, 1)
                        s_pe.n = pe_p0 + 1
                        perm_done = s_pe.n

                        @block.vector
                        def _(e, kind=kind, c=c, cw=cw, t=t, perm_done=perm_done):
                            e.wait_ge(s_pe.h, perm_done)
                            tab = 0
                            ins = e.tensor_mul(out=dn[0:cw, 0:512], in0=rq[0:cw, :],
                                               in1=ropec[0:cw, tab + t * 512:tab + (t + 1) * 512])
                            p.wmark("v", e, p.mark("v", ins))
                            ins = e.tensor_mul(out=dn[0:cw, 512:1024], in0=psN[0:cw, :],
                                               in1=ropec[0:cw, tab + 2048 + t * 512:tab + 2048 + (t + 1) * 512])
                            p.wmark("v", e, p.mark("v", ins))
                            if kind == "q":
                                dst = qwin[:, c * 512:(c + 1) * 512]
                            else:
                                dst = kr[:, c * TQ + t * 512:c * TQ + (t + 1) * 512]
                            ins = e.tensor_tensor(out=dst[0:cw, :],
                                                  in0=dn[0:cw, 0:512], in1=dn[0:cw, 512:1024],
                                                  op=ALU.add)
                            ins.then_inc(s_dv.h, 1)
                        s_dv.n += 1
                    else:
                        @block.scalar
                        def _(e, proj_t=proj_t, dvn=s_dv.n):
                            e.wait_ge(s_pe.h, proj_t)
                            if dvn > 0:
                                e.wait_ge(s_dv.h, dvn)
                            ins = e.copy(out=vsb[:], in_=psA[:])
                            ins.then_inc(s_ac.h, 1)
                        s_ac.n += 1
                        v_ev = s_ac.n

                        pe_t0 = s_pe.n

                        @block.tensor
                        def _(e, v_ev=v_ev, pe_t0=pe_t0, dvn=s_dv.n,
                              ordn=p.sem("ord_v").n, acn=s_ac.n):
                            e.wait_ge(s_ac.h, v_ev)
                            if ordn > 0:
                                e.wait_ge(p.sem("ord_v").h, ordn)
                            for i4 in range(4):
                                if i4 >= 1:
                                    e.wait_ge(s_dv.h, dvn + i4)
                                ins = e.transpose(tp0[:, 0:128],
                                                  vsb[:, i4 * 128:(i4 + 1) * 128],
                                                  consts["ident_bf16"][:])
                                ins.then_inc(s_pe.h, 1)
                        s_pe.n = pe_t0 + 4

                        @block.vector
                        def _(e, c=c, t=t, pe_t0=pe_t0):
                            for i4 in range(4):
                                e.wait_ge(s_pe.h, pe_t0 + i4 + 1)
                                kt = t * 4 + i4
                                vh0, vh1 = 2 * c, 2 * c + 1
                                ins = e.tensor_copy(
                                    out=vte[:, (vh0 * 16 + kt) * 65:(vh0 * 16 + kt) * 65 + 64],
                                    in_=tp0[:, 0:64])
                                p.wmark("v", e, p.mark("v", ins))
                                ins = e.tensor_copy(
                                    out=vte[:, (vh1 * 16 + kt) * 65:(vh1 * 16 + kt) * 65 + 64],
                                    in_=tp0[:, 64:128])
                                ins.then_inc(s_dv.h, 1)
                        s_dv.n += 4
                wfree[slot] = s_pe.n

            for si, (kind, c, co, cw) in enumerate(specs):
                emit_proj_chunk(kind, c, co, cw, si % 2, [0, 1, 2, 3])

            @block.vector
            def _(e, dvn=s_dv.n):
                e.wait_ge(s_dv.h, dvn)
                va = bcast_ap(vte[:, :], 64, [[16 * 16 * 65, 128], [65, 256], [0, 1]])
                ins = e.memset(va, 1.0)
                ins.then_inc(s_dv.h, 1)
            s_dv.n += 1
            vones = s_dv.n

            for qw in range(4):
                KT = 4 * (qw + 1)
                for c in range(11):
                    emit_proj_chunk("q", c, offs[c][0], offs[c][1], c % 2, [qw], qw=qw)
                for j in range(8):
                    heads = [2 * j, 2 * j + 1, 2 * j + 16, 2 * j + 17]
                    pe_q0 = s_pe.n
                    ac_q0 = s_ac.n
                    dv_q0 = s_dv.n
                    NP = KT // 2
                    pdx = [pd0, pd1]
                    sc_idx = {}
                    pv_idx = {}
                    ev_idx = {}
                    cnt = 0
                    for hq in range(4):
                        for ktp in range(NP):
                            cnt += 2
                            sc_idx[(hq, ktp)] = cnt
                            if ktp >= 1:
                                cnt += 2
                                pv_idx[(hq, ktp - 1)] = cnt
                        cnt += 2
                        pv_idx[(hq, NP - 1)] = cnt
                    diag_ord = {}
                    nd = 0
                    diag_pref = []
                    for hq in range(4):
                        for ktp in range(NP):
                            if 2 * ktp + 1 >= qw * 4:
                                nd += 1
                                diag_ord[(hq, ktp)] = nd
                        diag_pref.append(nd)
                    ev_dv = [dv_q0 + 1 + diag_pref[hq] + hq + 1 for hq in range(4)]

                    @block.tensor
                    def _(e, qw=qw, heads=heads, KT=KT, NP=NP, pe_q0=pe_q0,
                          ac_q0=ac_q0, dv_q0=dv_q0, vones=vones,
                          diag_ord=dict(diag_ord), ordn=p.sem("ord_v").n,
                          ev_dv=list(ev_dv)):
                        e.wait_ge(s_dv.h, vones)
                        e.wait_ge(s_dv.h, dv_q0)
                        if ac_q0 > 0:
                            e.wait_ge(s_ac.h, ac_q0)
                        if ordn > 0:
                            e.wait_ge(p.sem("ord_v").h, ordn)

                        def pv(hq, ktp):
                            h = heads[hq]
                            vh = h % 16
                            if ktp == 0 and hq >= 1:
                                e.wait_ge(s_dv.h, ev_dv[hq - 1])
                            if (hq, ktp) in diag_ord:
                                e.wait_ge(s_dv.h, dv_q0 + 1 + hq + diag_ord[(hq, ktp)])
                            else:
                                pg = hq * NP + ktp
                                e.wait_ge(s_ac.h, ac_q0 + pg + 1)
                            for u in range(2):
                                kt = 2 * ktp + u
                                e.matmul(psA[0:65, :],
                                         vte[:, (vh * 16 + kt) * 65:(vh * 16 + kt + 1) * 65],
                                         prb[:, (ktp % 2) * 1024 + u * 512:
                                             (ktp % 2) * 1024 + (u + 1) * 512],
                                         start=(kt == 0), stop=(kt == KT - 1),
                                         skip_group_check=True).then_inc(s_pe.h, 1)

                        for hq in range(4):
                            h = heads[hq]
                            cq, off = h // 3, (h % 3) * 32
                            for ktp in range(NP):
                                pg = hq * NP + ktp
                                if pg >= 2:
                                    e.wait_ge(s_ac.h, ac_q0 + pg - 1)
                                for u in range(2):
                                    kt = 2 * ktp + u
                                    e.matmul(pdx[ktp % 2][:, u * 512:(u + 1) * 512],
                                             kr[:, cq * TQ + kt * 128:cq * TQ + (kt + 1) * 128]
                                             [off:off + 32, :],
                                             qwin[:, cq * 512:(cq + 1) * 512]
                                             [off:off + 32, :],
                                             start=True, stop=True).then_inc(s_pe.h, 1)
                                if ktp >= 1:
                                    pv(hq, ktp - 1)
                            pv(hq, NP - 1)
                    s_pe.n = pe_q0 + 4 * (2 * KT)

                    @block.scalar
                    def _(e, KT=KT, NP=NP, qw=qw, pe_q0=pe_q0, ac_q0=ac_q0,
                          sc_idx=dict(sc_idx), pv_idx=dict(pv_idx)):
                        for hq in range(4):
                            for ktp in range(NP):
                                pg = hq * NP + ktp
                                e.wait_ge(s_pe.h, pe_q0 + sc_idx[(hq, ktp)])
                                if pg >= 2:
                                    c0, c1 = divmod(pg - 2, NP)
                                    e.wait_ge(s_pe.h, pe_q0 + pv_idx[(c0, c1)])
                                ins = e.activation(
                                    out=prb[:, (ktp % 2) * 1024:(ktp % 2) * 1024 + 1024],
                                    in_=pdx[ktp % 2][:, :], func=AF.Exp,
                                    scale=float(1.0 / np.sqrt(32.0)))
                                ins.then_inc(s_ac.h, 1)
                    s_ac.n = ac_q0 + 4 * NP

                    @block.vector
                    def _(e, KT=KT, NP=NP, qw=qw, heads=heads, ac_q0=ac_q0, dv_q0=dv_q0,
                          pe_q0=pe_q0, pv_idx=dict(pv_idx), diag_ord=dict(diag_ord),
                          ordn=p.sem("ord_v").n):
                        if ordn > 0:
                            e.wait_ge(p.sem("ord_v").h, ordn)
                        e.engine_nop().then_inc(s_dv.h, 1)
                        for hq in range(4):
                            for ktp in range(NP):
                                if (hq, ktp) in diag_ord:
                                    pg = hq * NP + ktp
                                    e.wait_ge(s_ac.h, ac_q0 + pg + 1)
                                    kt0 = 2 * ktp
                                    sm = (kt0 - qw * 4)
                                    ins = e.tensor_mul(
                                        out=prb[:, (ktp % 2) * 1024:(ktp % 2) * 1024 + 1024],
                                        in0=prb[:, (ktp % 2) * 1024:(ktp % 2) * 1024 + 1024],
                                        in1=masks[:, sm * 512:(sm + 2) * 512])
                                    ins.then_inc(s_dv.h, 1)
                            e.wait_ge(s_pe.h, pe_q0 + pv_idx[(hq, NP - 1)])
                            ins = e.tensor_copy(out=asb[0:65, hq * 512:hq * 512 + 512],
                                                in_=psA[0:65, :])
                            ins.then_inc(s_dv.h, 1)
                    s_dv.n = dv_q0 + 1 + nd + 4
                    a_evac = s_dv.n

                    for pi in range(2):
                        h0 = heads[pi]
                        emit_l1a_pair(p, asb, pi, pi + 2, h0, qw, b, dw, dn, srow2,
                                      lamb, c08, ws, consts, a_evac, s_pe, s_dv, s_ac, TQ,
                                      ps2=psN, srow2b=srow2b)

                dasm_done = p.sem("ord_v").n
                t = qw
                pe_o0 = s_pe.n

                wp_ld = []
                for m in range(8):
                    slot = m % 2

                    @block.gpsimd
                    def _(e, m=m, slot=slot, wf=wfree[slot], pe_o0=pe_o0):
                        if m >= 2:
                            e.wait_ge(s_pe.h, pe_o0 + m - 1)
                        elif wf > 0:
                            e.wait_ge(s_pe.h, wf)
                        e.dma_start(out=wprj[:, slot * 1024:(slot + 1) * 1024]
                                    .rearrange("q (k c) -> q k c", k=8),
                                    in_=wd["wproj_b"].rearrange("(k q) n -> q k n", q=128)
                                    [:, :, m * 128:(m + 1) * 128]).then_inc(s_wr[slot].h, 16)
                    s_wr[slot].n += 16
                    wp_ld.append(s_wr[slot].n)

                @block.tensor
                def _(e, t=t, pe_o0=pe_o0, dasm_done=dasm_done, dvn=s_dv.n, b=b,
                      dv_w1=s_dv.n + 1, wp_ld=tuple(wp_ld)):
                    e.wait_ge(p.sem("ord_v").h, dasm_done)
                    if dvn > 0:
                        e.wait_ge(s_dv.h, dvn)
                    for m in range(8):
                        slot = m % 2
                        e.wait_ge(s_wr[slot].h, wp_ld[m])
                        if m == 4:
                            e.wait_ge(s_dv.h, dv_w1)
                        for k in range(8):
                            ins = e.matmul(obank[m % 4][:, :],
                                           wprj[:, slot * 1024 + k * 128:slot * 1024 + (k + 1) * 128],
                                           dw[:, k * 512:(k + 1) * 512],
                                           start=(k == 0), stop=(k == 7),
                                           skip_group_check=True)
                        ins.then_inc(s_pe.h, 1)
                s_pe.n = pe_o0 + 8
                wfree[0] = s_pe.n
                wfree[1] = s_pe.n

                @block.sync
                def _(e, t=t, b=b, lds=ws["s_ld"].n, dvn=s_dv.n, ordn=p.sem("ord_v").n,
                      stn=s_st.n):
                    if dvn > 0:
                        e.wait_ge(s_dv.h, dvn)
                    if ordn > 0:
                        e.wait_ge(p.sem("ord_v").h, ordn)
                    if stn > 0:
                        e.wait_ge(s_st.h, stn)
                    e.dma_start(out=ws["rt"].rearrange("q (j c) -> q j c", j=8),
                                in_=res_dram.rearrange("(j q) c -> q j c", q=128)
                                [:, :, b * TQ + t * 512:b * TQ + (t + 1) * 512]
                                ).then_inc(ws["s_ld"].h, 16)
                ws["s_ld"].n += 16
                rt_ld = ws["s_ld"].n

                @block.vector
                def _(e, t=t, pe_o0=pe_o0, rt_ld=rt_ld):
                    e.wait_ge(ws["s_ld"].h, rt_ld)
                    for mw in range(2):
                        e.wait_ge(s_pe.h, pe_o0 + 4 * (mw + 1))
                        for m4 in range(4):
                            m = mw * 4 + m4
                            ins = e.tensor_tensor(out=ws["rt"][:, m * 512:(m + 1) * 512],
                                                  in0=ws["rt"][:, m * 512:(m + 1) * 512],
                                                  in1=obank[m4][:, :], op=ALU.add)
                            if mw == 0 and m4 == 3:
                                ins.then_inc(s_dv.h, 1)
                            elif not (mw == 1 and m4 == 3):
                                p.wmark("v", e, p.mark("v", ins))
                    ins.then_inc(s_dv.h, 1)
                s_dv.n += 2
                wr_ok = s_dv.n

                @block.sync
                def _(e, t=t, b=b, wr_ok=wr_ok):
                    e.wait_ge(s_dv.h, wr_ok)
                    e.dma_start(out=res_dram.rearrange("(j q) c -> q j c", q=128)
                                [:, :, b * TQ + t * 512:b * TQ + (t + 1) * 512],
                                in_=ws["rt"].rearrange("q (j c) -> q j c", j=8)
                                ).then_inc(s_st.h, 16)
                s_st.n += 16

        @block.sync
        def _(e):
            e.wait_ge(s_st.h, s_st.n)


def emit_l1a_pair(p, asb, ia, ib, h0, qw, b, xd, dn, srow2, lamb, c08, ws,
                  consts, a_evac, s_pe, s_dv, s_ac, TQ, ps2=None, srow2b=None):
    nc, block = p.nc, p.block
    A1 = asb[0:64, ia * 512:ia * 512 + 512]
    A2 = asb[0:64, ib * 512:ib * 512 + 512]
    s1 = asb[64:65, ia * 512:ia * 512 + 512]
    s2 = asb[64:65, ib * 512:ib * 512 + 512]

    @block.scalar
    def _(e, a_evac=a_evac, acn=s_ac.n, dvn=s_dv.n):
        e.wait_ge(s_dv.h, max(a_evac, dvn))
        ins = e.activation(out=srow2[:, 0:512], in_=s1, func=AF.Ln)
        p.wmark("s", e, p.mark("s", ins))
        ins = e.activation(out=srow2[:, 0:512], in_=srow2[:, 0:512], func=AF.Exp,
                           scale=-1.0)
        ins.then_inc(s_ac.h, 1)
    s_ac.n += 1
    r1_done = s_ac.n

    @block.vector
    def _(e, r1_done=r1_done, pen=s_pe.n):
        e.wait_ge(s_ac.h, r1_done)
        e.wait_ge(s_pe.h, pen)
        p.wmark("v", e)
        ins = e.tensor_copy(out=srow2b[:, 0:512], in_=srow2[:, 0:512])
        ins.then_inc(s_dv.h, 1)
    s_dv.n += 1
    r1b_done = s_dv.n

    @block.scalar
    def _(e, r1b_done=r1b_done):
        e.wait_ge(s_dv.h, r1b_done)
        ins = e.activation(out=srow2[:, 0:512], in_=s2, func=AF.Ln)
        p.wmark("s", e, p.mark("s", ins))
        ins = e.activation(out=srow2[:, 0:512], in_=srow2[:, 0:512], func=AF.Exp,
                           scale=-1.0)
        ins.then_inc(s_ac.h, 1)
    s_ac.n += 1
    r2_done = s_ac.n

    @block.vector
    def _(e, r2_done=r2_done):
        e.wait_ge(s_ac.h, r2_done)
        p.wmark("v", e)
        ins = e.tensor_scalar_mul(out=srow2b[:, 512:1024], in0=srow2[:, 0:512],
                                  scalar1=lamb[0:1, 0:1])
        ins.then_inc(s_dv.h, 1)
    s_dv.n += 1
    rec_done = s_dv.n

    pe_b0 = s_pe.n

    @block.tensor
    def _(e, rec_done=rec_done, pe_b0=pe_b0, ordn=p.sem("ord_v").n):
        e.wait_ge(s_dv.h, rec_done)
        if ordn > 0:
            e.wait_ge(p.sem("ord_v").h, ordn)
        e.matmul(ps2[0:64, 0:512], ws["ones1b"][:, 0:64], srow2b[:, 0:512],
                 start=True, stop=True).then_inc(s_pe.h, 1)
        ins = e.matmul(ps2[64:128, 0:512], ws["ones1b"][:, 64:128], srow2b[:, 512:1024],
                       start=True, stop=True)
        ins.then_inc(s_pe.h, 1)
    s_pe.n = pe_b0 + 2
    bc_done = s_pe.n

    @block.vector
    def _(e, bc_done=bc_done):
        e.wait_ge(s_pe.h, bc_done)
        p.wmark("v", e)
        ins = e.tensor_mul(out=dn[0:64, 1024:1536], in0=A1, in1=ps2[0:64, 0:512])
        p.wmark("v", e, p.mark("v", ins))
        ins = e.tensor_mul(out=dn[0:64, 0:512], in0=A2, in1=ps2[64:128, 0:512])
        p.wmark("v", e, p.mark("v", ins))
        ins = e.tensor_tensor(out=dn[0:64, 1024:1536], in0=dn[0:64, 1024:1536],
                              in1=dn[0:64, 0:512], op=ALU.subtract)
        p.wmark("v", e, p.mark("v", ins))
        ins = e.tensor_mul(out=dn[0:64, 0:512], in0=dn[0:64, 1024:1536],
                           in1=dn[0:64, 1024:1536])
        ins.then_inc(s_dv.h, 1)
    s_dv.n += 1
    sq_done = s_dv.n

    pe_m0 = s_pe.n

    @block.tensor
    def _(e, sq_done=sq_done, pe_m0=pe_m0):
        e.wait_ge(s_dv.h, sq_done)
        e.matmul(ps2[0:1, 0:512], ws["onesc"][0:64, :], dn[0:64, 0:512],
                 start=True, stop=True).then_inc(s_pe.h, 1)
    s_pe.n = pe_m0 + 1

    @block.scalar
    def _(e, pe_m0=pe_m0):
        e.wait_ge(s_pe.h, pe_m0 + 1)
        ins = e.activation(out=srow2[:, 0:512], in_=ps2[0:1, 0:512], func=AF.Ln,
                           bias=ws["eps1"][:], scale=1.0 / 64)
        p.wmark("s", e, p.mark("s", ins))
        ins = e.activation(out=srow2[:, 0:512], in_=srow2[:, 0:512], func=AF.Exp,
                           scale=-0.5)
        ins.then_inc(s_ac.h, 1)
    s_ac.n += 1
    rstd_done = s_ac.n

    @block.vector
    def _(e, rstd_done=rstd_done):
        e.wait_ge(s_ac.h, rstd_done)
        p.wmark("v", e)
        ins = e.tensor_scalar_mul(out=srow2b[:, 0:512], in0=srow2[:, 0:512],
                                  scalar1=0.8)
        ins.then_inc(s_dv.h, 1)
    s_dv.n += 1
    rstdb = s_dv.n

    pe_m1 = s_pe.n

    @block.tensor
    def _(e, rstdb=rstdb, pe_m1=pe_m1):
        e.wait_ge(s_dv.h, rstdb)
        e.matmul(ps2[0:64, 0:512], ws["ones1b"][:, 0:64], srow2b[:, 0:512],
                 start=True, stop=True).then_inc(s_pe.h, 1)
    s_pe.n = pe_m1 + 1

    @block.vector
    def _(e, pe_m1=pe_m1, h0=h0):
        e.wait_ge(s_pe.h, pe_m1 + 1)
        ch, ro = h0 // 2, (h0 % 2) * 64
        ins = e.tensor_mul(
            out=xd[:, ch * 512:(ch + 1) * 512][ro:ro + 64, :],
            in0=dn[0:64, 1024:1536], in1=ps2[0:64, 0:512])
        p.wmark("v", e, p.mark("v", ins))



MIXC = 0.05


def emit_l0a(p, res_dram, wd, consts):
    nc, block = p.nc, p.block
    TQ = 2048
    C = 16

    qt_d = nc.dram_tensor("l0a_qt", [2, TQ, 1024], BF16, kind="Internal").ap()
    kt_d = nc.dram_tensor("l0a_kt", [2, TQ, 1024], BF16, kind="Internal").ap()
    vt_d = nc.dram_tensor("l0a_vt", [2, TQ, 1024], BF16, kind="Internal").ap()
    ut_d = nc.dram_tensor("l0a_ut", [2, TQ, 1024], BF16, kind="Internal").ap()
    ot_d = nc.dram_tensor("l0a_ot", [2, TQ, 1024], BF16, kind="Internal").ap()
    fs_d = nc.dram_tensor("l0a_fs", [2, TQ, 48], F32, kind="Internal").ap()

    with contextlib.ExitStack() as stack:
        ps = [stack.enter_context(nc.psum_tensor("a0_ps%d" % i, [128, 512], F32))
              for i in range(6)]
        ws = ln_workspace(p, stack, "a0", ps0=ps[0], ps1=ps[1])
        scur = stack.enter_context(nc.sbuf_tensor("a0_scur", [128, 2 * 16], F32))
        elan = stack.enter_context(nc.sbuf_tensor("a0_el", [128, 2 * 2 * 8 * 64], F32))
        slan = stack.enter_context(nc.sbuf_tensor("a0_sl", [128, 2 * 2 * 8 * 64], F32))
        rlan = stack.enter_context(nc.sbuf_tensor("a0_rl", [128, 4 * 2 * 8 * 64], F32))
        rten = stack.enter_context(nc.sbuf_tensor("a0_rt4", [128, 4 * 16], F32))
        rscr = stack.enter_context(nc.sbuf_tensor("a0_rsc", [128, 64], F32))
        mcon = stack.enter_context(nc.sbuf_tensor("a0_mc", [128, 16], F32))
        rrow = stack.enter_context(nc.sbuf_tensor("a0_rr", [16, 4 * 2 * 64], F32))
        selm = stack.enter_context(nc.sbuf_tensor("a0_sel", [16, 8 * 128], F32))
        wbfs = stack.enter_context(nc.sbuf_tensor("a0_wbfs", [128, 8 * 48], BF16))
        bfs = stack.enter_context(nc.sbuf_tensor("a0_bfs", [128, 32], F32))
        fssc = stack.enter_context(nc.sbuf_tensor("a0_fs", [128, 16 * 48], F32))
        sfst = stack.enter_context(nc.sbuf_tensor("a0_sf", [128, 1024], F32))
        ssst = stack.enter_context(nc.sbuf_tensor("a0_ss", [128, 1024], F32))
        d1t = stack.enter_context(nc.sbuf_tensor("a0_d1", [128, 1024], F32))
        d2t = stack.enter_context(nc.sbuf_tensor("a0_d2", [128, 1024], F32))

        s_pe = p.sem("a0_pe")
        s_dv = p.sem("a0_dv")
        s_ac = p.sem("a0_ac")
        s_gp = p.sem("a0_gp")
        s_w = p.sem("a0_w")
        s_wr = [p.sem("a0_wr0"), p.sem("a0_wr1")]
        s_ld2 = [p.sem("a0_ld0"), p.sem("a0_ld1")]
        s_stg = [p.sem("a0_sg0"), p.sem("a0_sg1")]
        s_st = p.sem("a0_st")
        wfree = {0: 0, 1: 0}

        emit_ln_init(p, ws)

        @block.gpsimd
        def _(e):
            e.dma_start(out=wbfs.rearrange("q (k n) -> q k n", k=8),
                        in_=wd["wbfsT"].rearrange("(k q) n -> q k n", q=128)
                        ).then_inc(s_w.h, 16)
            e.dma_start(out=bfs[:, 0:16], in_=wd["bfdT"]).then_inc(s_w.h, 16)
            e.dma_start(out=bfs[:, 16:32], in_=wd["bsdT"]).then_inc(s_w.h, 16)
            e.dma_start(out=selm.rearrange("q (l c) -> q l c", l=8),
                        in_=consts["sel2h"].rearrange("l q c -> q l c")).then_inc(s_w.h, 16)
        s_w.n += 64
        w_done = s_w.n

        def emit_P(b, half):
            hb0 = half * 1024
            sfx = "_%d_%d" % (b, half)
            with contextlib.ExitStack() as st2:
                xh = st2.enter_context(nc.sbuf_tensor("a0_xh" + sfx, [128, 8 * 1024], BF16))
                wring = st2.enter_context(nc.sbuf_tensor("a0_wrg" + sfx, [128, 2 * 8 * 512], BF16))
                stg = st2.enter_context(nc.sbuf_tensor("a0_stg" + sfx, [128, 1024], BF16))
                fst = st2.enter_context(nc.sbuf_tensor("a0_fst" + sfx, [128, 48], F32))

                for t5 in range(2):
                    prev_pe = s_pe.n
                    prev_dv = s_dv.n
                    emit_ln_tile(p, res_dram, ws, b * TQ + hb0 + t5 * 512, xh, t5 * 512,
                                 xh_stride=1024,
                                 pre_apply_waits=[(s_pe.h, prev_pe), (s_st.h, s_st.n)],
                                 pre_load_waits=[(s_dv.h, prev_dv), (s_st.h, s_st.n)],
                                 pre_stats_waits=[(s_dv.h, prev_dv), (s_ac.h, s_ac.n)])
                xh_ready = ws["s_ac"].n

                for pi in range(6):
                    proj = pi // 2
                    oh = pi % 2
                    slot = pi % 2
                    dname = [qt_d, kt_d, vt_d][proj]

                    @block.gpsimd
                    def _(e, pi=pi, proj=proj, oh=oh, slot=slot, wf=wfree[slot],
                          stn=(s_st.n if pi == 0 else 0), dvn=(s_dv.n if pi == 0 else 0),
                          pc=p.pc_info):
                        if wf > 0:
                            e.wait_ge(s_pe.h, wf)
                        if stn > 0:
                            e.wait_ge(s_st.h, stn)
                        if dvn > 0:
                            e.wait_ge(s_dv.h, dvn)
                        if pc is not None:
                            e.wait_ge(pc["sems"][0].h, pc["early"][0])
                            e.wait_ge(pc["sems"][1].h, pc["early"][1])
                        e.dma_start(out=wring[:, slot * 4096:(slot + 1) * 4096]
                                    .rearrange("q (k c) -> q k c", k=8),
                                    in_=wd["wqkvT_b"].rearrange("(k q) n -> q k n", q=128)
                                    [:, :, proj * 1024 + oh * 512:proj * 1024 + oh * 512 + 512]
                                    ).then_inc(s_wr[slot].h, 16)
                    s_wr[slot].n += 16
                    wr_ld = s_wr[slot].n

                    for t8 in range(8):
                        pe_c0 = s_pe.n
                        bank = 2 + (t8 % 2)

                        @block.tensor
                        def _(e, slot=slot, t8=t8, bank=bank, wr_ld=wr_ld,
                              xh_ready=xh_ready, acn=s_ac.n):
                            e.wait_ge(ws["s_ac"].h, xh_ready)
                            e.wait_ge(s_wr[slot].h, wr_ld)
                            if acn > 0:
                                e.wait_ge(s_ac.h, acn)
                            for k in range(8):
                                ins = e.matmul(ps[bank][:, :],
                                               xh[:, k * 1024 + t8 * 128:k * 1024 + (t8 + 1) * 128],
                                               wring[:, slot * 4096 + k * 512:slot * 4096 + (k + 1) * 512],
                                               start=(k == 0), stop=(k == 7))
                            ins.then_inc(s_pe.h, 1)
                        s_pe.n = pe_c0 + 1
                        mm_done = s_pe.n

                        sslot = t8 % 2

                        @block.scalar
                        def _(e, bank=bank, mm_done=mm_done, sslot=sslot,
                              stn=s_stg[t8 % 2].n):
                            e.wait_ge(s_pe.h, mm_done)
                            if stn > 0:
                                e.wait_ge(s_stg[sslot].h, stn)
                            ins = e.activation(out=stg[:, sslot * 512:sslot * 512 + 512],
                                               in_=ps[bank][:, :], func=AF.Silu)
                            ins.then_inc(s_ac.h, 1)
                        s_ac.n += 1
                        ev_done = s_ac.n

                        @block.sync
                        def _(e, b=b, hb0=hb0, t8=t8, oh=oh, dname=dname, ev_done=ev_done,
                              sslot=sslot):
                            e.wait_ge(s_ac.h, ev_done)
                            e.dma_start(out=dname[b, hb0 + t8 * 128:hb0 + (t8 + 1) * 128,
                                                  oh * 512:oh * 512 + 512],
                                        in_=stg[:, sslot * 512:sslot * 512 + 512]
                                        ).then_inc(s_stg[sslot].h, 16)
                        s_stg[sslot].n += 16
                    wfree[slot] = s_pe.n

                for t8 in range(8):
                    pe_c0 = s_pe.n

                    @block.tensor
                    def _(e, t8=t8, xh_ready=xh_ready, wdn=w_done, dvn=s_dv.n):
                        e.wait_ge(ws["s_ac"].h, xh_ready)
                        e.wait_ge(s_w.h, wdn)
                        if dvn > 0:
                            e.wait_ge(s_dv.h, dvn)
                        for k in range(8):
                            ins = e.matmul(ps[4][:, 0:48],
                                           xh[:, k * 1024 + t8 * 128:k * 1024 + (t8 + 1) * 128],
                                           wbfs[:, k * 48:(k + 1) * 48],
                                           start=(k == 0), stop=(k == 7))
                        ins.then_inc(s_pe.h, 1)
                    s_pe.n = pe_c0 + 1
                    mm_done = s_pe.n

                    @block.vector
                    def _(e, mm_done=mm_done, acn=s_ac.n, stn=s_st.n):
                        e.wait_ge(s_pe.h, mm_done)
                        p.wmark("v", e)
                        if acn > 0:
                            e.wait_ge(s_ac.h, acn)
                        if stn > 0:
                            e.wait_ge(s_st.h, stn)
                        ins = e.tensor_tensor(out=fst[:, 16:32], in0=ps[4][:, 16:32],
                                              in1=bfs[:, 0:16], op=ALU.add)
                        p.wmark("v", e, p.mark("v", ins))
                        ins = e.tensor_tensor(out=fst[:, 32:48], in0=ps[4][:, 32:48],
                                              in1=bfs[:, 16:32], op=ALU.add)
                        p.wmark("v", e, p.mark("v", ins))
                        ins = e.tensor_copy(out=fst[:, 0:16], in_=ps[4][:, 0:16])
                        ins.then_inc(s_dv.h, 1)
                    s_dv.n += 1
                    badd = s_dv.n

                    @block.scalar
                    def _(e, badd=badd, stn=s_st.n):
                        e.wait_ge(s_dv.h, badd)
                        if stn > 0:
                            e.wait_ge(s_st.h, stn)
                        ins = e.activation(out=fst[:, 0:48], in_=fst[:, 0:48], func=AF.Sigmoid)
                        ins.then_inc(s_ac.h, 1)
                    s_ac.n += 1
                    sg_done = s_ac.n

                    @block.sync
                    def _(e, b=b, hb0=hb0, t8=t8, sg_done=sg_done):
                        e.wait_ge(s_ac.h, sg_done)
                        e.dma_start(out=fs_d[b, hb0 + t8 * 128:hb0 + (t8 + 1) * 128, 0:16],
                                    in_=fst[:, 16:32]).then_inc(s_st.h, 16)
                        e.dma_start(out=fs_d[b, hb0 + t8 * 128:hb0 + (t8 + 1) * 128, 16:32],
                                    in_=fst[:, 32:48]).then_inc(s_st.h, 16)
                        e.dma_start(out=fs_d[b, hb0 + t8 * 128:hb0 + (t8 + 1) * 128, 32:48],
                                    in_=fst[:, 0:16]).then_inc(s_st.h, 16)
                    s_st.n += 48

        def emit_U(b, half):
            hb0 = half * 1024
            sfx = "_%d_%d" % (b, half)
            with contextlib.ExitStack() as st2:
                ksb = st2.enter_context(nc.sbuf_tensor("a0_ksb" + sfx, [128, 1024], F32))
                vsb = st2.enter_context(nc.sbuf_tensor("a0_vsb" + sfx, [128, 1024], F32))
                bsb = st2.enter_context(nc.sbuf_tensor("a0_bsb" + sfx, [128, 16], F32))
                usb = st2.enter_context(nc.sbuf_tensor("a0_usb" + sfx, [128, 1024], BF16))
                for t8 in range(8):
                    k0 = s_ld2[0].n

                    @block.gpsimd
                    def _(e, b=b, hb0=hb0, t8=t8, dvn=s_dv.n, stn=(s_st.n if t8 == 0 else 0),
                          sg0=(s_stg[0].n if t8 == 0 else 0), sg1=(s_stg[1].n if t8 == 0 else 0)):
                        if dvn > 0:
                            e.wait_ge(s_dv.h, dvn)
                        if stn > 0:
                            e.wait_ge(s_st.h, stn)
                        if sg0 > 0:
                            e.wait_ge(s_stg[0].h, sg0)
                        if sg1 > 0:
                            e.wait_ge(s_stg[1].h, sg1)
                        r0, r1 = hb0 + t8 * 128, hb0 + (t8 + 1) * 128
                        e.dma_start(out=ksb[:], in_=kt_d[b, r0:r1, :]).then_inc(s_ld2[0].h, 16)
                        e.dma_start(out=vsb[:], in_=vt_d[b, r0:r1, :]).then_inc(s_ld2[0].h, 16)
                        e.dma_start(out=bsb[:], in_=fs_d[b, r0:r1, 32:48]).then_inc(s_ld2[0].h, 16)
                    s_ld2[0].n += 48
                    ld_done = s_ld2[0].n

                    @block.vector
                    def _(e, ld_done=ld_done, stn=s_st.n):
                        e.wait_ge(s_ld2[0].h, ld_done)
                        if stn > 0:
                            e.wait_ge(s_st.h, stn)
                        ins = e.tensor_mul(out=ksb[:], in0=ksb[:], in1=vsb[:])
                        p.wmark("v", e, p.mark("v", ins))
                        bb = bcast_ap(bsb[:, :], 0, [[16, 128], [1, 16], [0, 64]])
                        ins = e.tensor_tensor(out=usb.rearrange("q (h d) -> q h d", h=16),
                                              in0=ksb.rearrange("q (h d) -> q h d", h=16),
                                              in1=bb, op=ALU.mult)
                        ins.then_inc(s_dv.h, 1)
                    s_dv.n += 1
                    u_done = s_dv.n

                    @block.sync
                    def _(e, b=b, hb0=hb0, t8=t8, u_done=u_done):
                        e.wait_ge(s_dv.h, u_done)
                        e.dma_start(out=ut_d[b, hb0 + t8 * 128:hb0 + (t8 + 1) * 128, :],
                                    in_=usb[:]).then_inc(s_st.h, 16)
                    s_st.n += 16

        def emit_S(half, set_idx):
            hb0 = half * 1024
            sfx = "_%d" % half
            with contextlib.ExitStack() as st2:
                usc = st2.enter_context(nc.sbuf_tensor("a0_usc" + sfx, [128, 16 * 1024], BF16))
                qsc = st2.enter_context(nc.sbuf_tensor("a0_qsc" + sfx, [128, 16 * 1024], BF16))

                ld0 = s_ld2[1].n

                @block.gpsimd
                def _(e, hb0=hb0, stn=s_st.n, dvn=s_dv.n, gpn=s_gp.n,
                      sg0=s_stg[0].n, sg1=s_stg[1].n):
                    e.wait_ge(s_st.h, stn)
                    if sg0 > 0:
                        e.wait_ge(s_stg[0].h, sg0)
                    if sg1 > 0:
                        e.wait_ge(s_stg[1].h, sg1)
                    if dvn > 0:
                        e.wait_ge(s_dv.h, dvn)
                    for b in range(2):
                        src = ut_d[b, hb0:hb0 + 1024, :].rearrange("(c i) l -> c (i l)", i=C)
                        e.dma_start(out=usc[b * 64:(b + 1) * 64, :], in_=src).then_inc(s_ld2[1].h, 16)
                        srq = qt_d[b, hb0:hb0 + 1024, :].rearrange("(c i) l -> c (i l)", i=C)
                        e.dma_start(out=qsc[b * 64:(b + 1) * 64, :], in_=srq).then_inc(s_ld2[1].h, 16)
                        srf = fs_d[b, hb0:hb0 + 1024, :].rearrange("(c i) l -> c (i l)", i=C)
                        e.dma_start(out=fssc[b * 64:(b + 1) * 64, :], in_=srf).then_inc(s_ld2[1].h, 16)
                s_ld2[1].n += 96
                ld_done = s_ld2[1].n

                @block.vector
                def _(e, dvn=s_dv.n):
                    if dvn > 0:
                        e.wait_ge(s_dv.h, dvn)
                    e.memset(sfst[:], 0.0)
                    e.memset(ssst[:], 0.0)
                    ins = e.memset(d1t[:], 0.0)
                    ins.then_inc(s_dv.h, 1)
                s_dv.n += 1

                @block.gpsimd
                def _(e, gpn=s_gp.n):
                    if gpn > 0:
                        e.wait_ge(s_gp.h, gpn)
                    e.memset(rten[:, 0:16], 1.0)
                    e.memset(rten[:, 16:32], 0.0)
                    e.memset(rten[:, 32:48], 0.0)
                    e.memset(mcon[:], MIXC)
                    ins = e.memset(rten[:, 48:64], 1.0)
                    ins.then_inc(s_gp.h, 1)
                s_gp.n += 1
                r_init = s_gp.n

                @block.vector
                def _(e, ld_done=ld_done, init_dv=s_dv.n):
                    e.wait_ge(s_ld2[1].h, ld_done)
                    e.wait_ge(s_dv.h, init_dv)
                    p.wmark("v", e)
                    for i in range(C):
                        fb = bcast_ap(fssc[:, :], i * 48, [[16 * 48, 128], [1, 16], [0, 64]])
                        sb = bcast_ap(fssc[:, :], i * 48 + 16, [[16 * 48, 128], [1, 16], [0, 64]])
                        u_i = usc[:, i * 1024:(i + 1) * 1024]
                        m1 = e.tensor_tensor(out=d1t.rearrange("q (h d) -> q h d", h=16),
                                             in0=sfst.rearrange("q (h d) -> q h d", h=16),
                                             in1=fb, op=ALU.mult)
                        ma = p.mark("v", m1)
                        m2 = e.tensor_tensor(out=d2t.rearrange("q (h d) -> q h d", h=16),
                                             in0=ssst.rearrange("q (h d) -> q h d", h=16),
                                             in1=sb, op=ALU.mult)
                        mb = p.mark("v", m2)
                        p.wmark("v", e, ma)
                        m3 = e.tensor_tensor(out=d1t[:], in0=d1t[:], in1=u_i, op=ALU.add)
                        mc = p.mark("v", m3)
                        p.wmark("v", e, mb)
                        m4 = e.tensor_tensor(out=d2t[:], in0=d2t[:], in1=u_i, op=ALU.add)
                        md = p.mark("v", m4)
                        p.wmark("v", e, md)
                        m5 = e.scalar_tensor_tensor(out=sfst[:], in0=d2t[:], scalar=MIXC,
                                                    in1=d1t[:], op0=ALU.mult, op1=ALU.add)
                        me = p.mark("v", m5)
                        m6 = e.scalar_tensor_tensor(out=ssst[:], in0=d1t[:], scalar=MIXC,
                                                    in1=d2t[:], op0=ALU.mult, op1=ALU.add)
                        mf = p.mark("v", m6)
                        p.wmark("v", e, mf)
                    e.engine_nop().then_inc(s_dv.h, 1)
                s_dv.n += 1
                e_done = s_dv.n

                @block.gpsimd
                def _(e, ld_done=ld_done, r_init=r_init):
                    e.wait_ge(s_ld2[1].h, ld_done)
                    e.wait_ge(s_gp.h, r_init)
                    for i in range(C):
                        f_i = fssc[:, i * 48:i * 48 + 16]
                        s_i = fssc[:, i * 48 + 16:i * 48 + 32]
                        for (ra, rb2) in ((0, 2), (1, 3)):
                            m1 = e.tensor_tensor(out=rscr[:, 0:16], in0=rten[:, ra * 16:(ra + 1) * 16],
                                                 in1=f_i, op=ALU.mult)
                            m2 = e.tensor_tensor(out=rscr[:, 16:32], in0=rten[:, rb2 * 16:(rb2 + 1) * 16],
                                                 in1=s_i, op=ALU.mult)
                            b2 = p.mark("g", m2)
                            p.wmark("g", e, b2)
                            m2a = e.tensor_tensor(out=rscr[:, 32:48], in0=rscr[:, 0:16],
                                                  in1=mcon[:], op=ALU.mult)
                            m2b = e.tensor_tensor(out=rscr[:, 48:64], in0=rscr[:, 16:32],
                                                  in1=mcon[:], op=ALU.mult)
                            b3 = p.mark("g", m2b)
                            p.wmark("g", e, b3)
                            m3 = e.tensor_tensor(out=rten[:, ra * 16:(ra + 1) * 16],
                                                 in0=rscr[:, 0:16], in1=rscr[:, 48:64], op=ALU.add)
                            m4 = e.tensor_tensor(out=rten[:, rb2 * 16:(rb2 + 1) * 16],
                                                 in0=rscr[:, 16:32], in1=rscr[:, 32:48], op=ALU.add)
                            d = p.mark("g", m4)
                            p.wmark("g", e, d)
                    e.engine_nop().then_inc(s_gp.h, 1)
                s_gp.n += 1
                r_done = s_gp.n


                pe_t0 = s_pe.n

                @block.tensor
                def _(e, e_done=e_done, r_done=r_done, pe_t0=pe_t0, dvn=s_dv.n,
                      acn=s_ac.n):
                    e.wait_ge(s_dv.h, e_done)
                    e.wait_ge(s_gp.h, r_done)
                    if acn > 0:
                        e.wait_ge(s_ac.h, acn)
                    n = 0
                    for b in range(2):
                        for sti, sten in ((0, sfst), (1, ssst)):
                            for lc in range(8):
                                if n >= 4:
                                    e.wait_ge(s_dv.h, dvn + n - 3)
                                ins = e.transpose(ps[2 + n % 4][0:128, 0:64],
                                                  sten[b * 64:(b + 1) * 64, lc * 128:(lc + 1) * 128],
                                                  consts["ident_f32"][b * 64:(b + 1) * 64,
                                                                      b * 64:(b + 1) * 64])
                                ins.then_inc(s_pe.h, 1)
                                n += 1
                    for b in range(2):
                        for en in range(4):
                            if True:
                                e.wait_ge(s_dv.h, dvn + n - 3)
                            ins = e.transpose(ps[2 + n % 4][0:16, 0:64],
                                              rten[b * 64:(b + 1) * 64, en * 16:(en + 1) * 16],
                                              consts["ident_f32"][b * 64:(b + 1) * 64,
                                                                  b * 64:(b + 1) * 64])
                            ins.then_inc(s_pe.h, 1)
                            n += 1
                s_pe.n = pe_t0 + 40

                @block.vector
                def _(e, pe_t0=pe_t0, dvn=s_dv.n):
                    n = 0
                    for b in range(2):
                        for sti in range(2):
                            for lc in range(8):
                                e.wait_ge(s_pe.h, pe_t0 + n + 1)
                                dst = elan[:, ((sti * 2 + b) * 8 + lc) * 64:
                                           ((sti * 2 + b) * 8 + lc) * 64 + 64]
                                ins = e.tensor_copy(out=dst, in_=ps[2 + n % 4][0:128, 0:64])
                                ins.then_inc(s_dv.h, 1)
                                n += 1
                    for b in range(2):
                        for en in range(4):
                            e.wait_ge(s_pe.h, pe_t0 + n + 1)
                            dst = rrow[:, (en * 2 + b) * 64:(en * 2 + b) * 64 + 64]
                            ins = e.tensor_copy(out=dst, in_=ps[2 + n % 4][0:16, 0:64])
                            ins.then_inc(s_dv.h, 1)
                            n += 1
                s_dv.n += 40
                tr_done = s_dv.n

                pe_r0 = s_pe.n

                @block.tensor
                def _(e, tr_done=tr_done, pe_r0=pe_r0, dvn=s_dv.n):
                    e.wait_ge(s_dv.h, tr_done)
                    n = 0
                    for en in range(4):
                        for b in range(2):
                            for lc in range(8):
                                if n >= 4:
                                    e.wait_ge(s_dv.h, dvn + n - 3)
                                ins = e.matmul(ps[2 + n % 4][:, 0:64],
                                               selm[:, lc * 128:(lc + 1) * 128],
                                               rrow[:, (en * 2 + b) * 64:(en * 2 + b) * 64 + 64],
                                               start=True, stop=True)
                                ins.then_inc(s_pe.h, 1)
                                n += 1
                s_pe.n = pe_r0 + 64

                @block.vector
                def _(e, pe_r0=pe_r0):
                    n = 0
                    for en in range(4):
                        for b in range(2):
                            for lc in range(8):
                                e.wait_ge(s_pe.h, pe_r0 + n + 1)
                                dst = rlan[:, ((en * 2 + b) * 8 + lc) * 64:
                                           ((en * 2 + b) * 8 + lc) * 64 + 64]
                                ins = e.tensor_copy(out=dst, in_=ps[2 + n % 4][:, 0:64])
                                ins.then_inc(s_dv.h, 1)
                                n += 1
                s_dv.n += 64
                rl_done = s_dv.n

                @block.vector
                def _(e, rl_done=rl_done, set_idx=set_idx):
                    e.wait_ge(s_dv.h, rl_done)
                    p.wmark("v", e)
                    if set_idx == 0:
                        ins = e.memset(scur[:, :], 0.0)
                        p.wmark("v", e, p.mark("v", ins))
                    for c in range(64):
                        sl = slan.rearrange("q (g c2) -> q g c2", c2=64)
                        m1 = e.tensor_copy(out=sl[:, 0:16, c],
                                           in_=scur.rearrange("q (g o) -> q g o", o=1)[:, 0:16, 0]
                                           if False else scur[:, 0:16])
                        ma = p.mark("v", m1)
                        m2 = e.tensor_copy(out=sl[:, 16:32, c], in_=scur[:, 16:32])
                        mb = p.mark("v", m2)
                        rl = rlan.rearrange("q (g c2) -> q g c2", c2=64)
                        el = elan.rearrange("q (g c2) -> q g c2", c2=64)
                        p.wmark("v", e, mb)
                        m3 = e.tensor_tensor(out=d1t[:, 0:16], in0=scur[:, 0:16],
                                             in1=rl[:, 0:16, c], op=ALU.mult)
                        mc = p.mark("v", m3)
                        m4 = e.tensor_tensor(out=d1t[:, 16:32], in0=scur[:, 16:32],
                                             in1=rl[:, 16:32, c], op=ALU.mult)
                        mdd = p.mark("v", m4)
                        m5 = e.tensor_tensor(out=d2t[:, 0:16], in0=scur[:, 0:16],
                                             in1=rl[:, 32:48, c], op=ALU.mult)
                        me = p.mark("v", m5)
                        m6 = e.tensor_tensor(out=d2t[:, 16:32], in0=scur[:, 16:32],
                                             in1=rl[:, 48:64, c], op=ALU.mult)
                        mf = p.mark("v", m6)
                        p.wmark("v", e, mf)
                        m7 = e.tensor_tensor(out=scur[:, 0:16], in0=d1t[:, 0:16],
                                             in1=d1t[:, 16:32], op=ALU.add)
                        mg = p.mark("v", m7)
                        m8 = e.tensor_tensor(out=scur[:, 16:32], in0=d2t[:, 0:16],
                                             in1=d2t[:, 16:32], op=ALU.add)
                        mh = p.mark("v", m8)
                        p.wmark("v", e, mg)
                        m9 = e.tensor_tensor(out=scur[:, 0:16], in0=scur[:, 0:16],
                                             in1=el[:, 0:16, c], op=ALU.add)
                        mi = p.mark("v", m9)
                        p.wmark("v", e, mh)
                        m10 = e.tensor_tensor(out=scur[:, 16:32], in0=scur[:, 16:32],
                                              in1=el[:, 16:32, c], op=ALU.add)
                        mj = p.mark("v", m10)
                        p.wmark("v", e, mj)
                    e.engine_nop().then_inc(s_dv.h, 1)
                s_dv.n += 1
                p2_done = s_dv.n

                pe_s0 = s_pe.n

                @block.tensor
                def _(e, p2_done=p2_done, pe_s0=pe_s0, dvn=s_dv.n):
                    e.wait_ge(s_dv.h, p2_done)
                    n = 0
                    for b in range(2):
                        for sti in range(2):
                            for lc in range(8):
                                if n >= 4:
                                    e.wait_ge(s_dv.h, dvn + n - 3)
                                ins = e.transpose(ps[2 + n % 4][0:64, 0:128],
                                                  slan[:, ((sti * 2 + b) * 8 + lc) * 64:
                                                       ((sti * 2 + b) * 8 + lc) * 64 + 64],
                                                  consts["ident_f32"][:, :])
                                ins.then_inc(s_pe.h, 1)
                                n += 1
                s_pe.n = pe_s0 + 32

                @block.vector
                def _(e, pe_s0=pe_s0):
                    n = 0
                    for b in range(2):
                        for sti, sten in ((0, sfst), (1, ssst)):
                            for lc in range(8):
                                e.wait_ge(s_pe.h, pe_s0 + n + 1)
                                ins = e.tensor_copy(
                                    out=sten[b * 64:(b + 1) * 64, lc * 128:(lc + 1) * 128],
                                    in_=ps[2 + n % 4][0:64, 0:128])
                                ins.then_inc(s_dv.h, 1)
                                n += 1
                s_dv.n += 32
                st_done = s_dv.n

                @block.vector
                def _(e, st_done=st_done):
                    e.wait_ge(s_dv.h, st_done)
                    p.wmark("v", e)
                    for i in range(C):
                        fb = bcast_ap(fssc[:, :], i * 48, [[16 * 48, 128], [1, 16], [0, 64]])
                        sb = bcast_ap(fssc[:, :], i * 48 + 16, [[16 * 48, 128], [1, 16], [0, 64]])
                        u_i = usc[:, i * 1024:(i + 1) * 1024]
                        q_i = qsc[:, i * 1024:(i + 1) * 1024]
                        m1 = e.tensor_tensor(out=d1t.rearrange("q (h d) -> q h d", h=16),
                                             in0=sfst.rearrange("q (h d) -> q h d", h=16),
                                             in1=fb, op=ALU.mult)
                        ma = p.mark("v", m1)
                        m2 = e.tensor_tensor(out=d2t.rearrange("q (h d) -> q h d", h=16),
                                             in0=ssst.rearrange("q (h d) -> q h d", h=16),
                                             in1=sb, op=ALU.mult)
                        mb = p.mark("v", m2)
                        p.wmark("v", e, ma)
                        m3 = e.tensor_tensor(out=sfst[:], in0=d1t[:], in1=u_i, op=ALU.add)
                        mc = p.mark("v", m3)
                        p.wmark("v", e, mb)
                        m5 = e.tensor_tensor(out=ssst[:], in0=d2t[:], in1=u_i, op=ALU.add)
                        me = p.mark("v", m5)
                        p.wmark("v", e, mc)
                        m4 = e.tensor_tensor(out=d1t[:], in0=d1t[:], in1=d2t[:], op=ALU.add)
                        md = p.mark("v", m4)
                        p.wmark("v", e, md)
                        m6 = e.scalar_tensor_tensor(out=d2t[:], in0=d1t[:], scalar=0.5,
                                                    in1=q_i, op0=ALU.mult, op1=ALU.mult)
                        mf = p.mark("v", m6)
                        p.wmark("v", e, mf)
                        m7 = e.tensor_copy(out=u_i, in_=d2t[:])
                        mg = p.mark("v", m7)
                        m8 = e.scalar_tensor_tensor(out=d1t[:], in0=ssst[:], scalar=MIXC,
                                                    in1=sfst[:], op0=ALU.mult, op1=ALU.add)
                        mh = p.mark("v", m8)
                        p.wmark("v", e, mh)
                        m9 = e.scalar_tensor_tensor(out=ssst[:], in0=sfst[:], scalar=MIXC,
                                                    in1=ssst[:], op0=ALU.mult, op1=ALU.add)
                        mi = p.mark("v", m9)
                        p.wmark("v", e, mi)
                        m10 = e.tensor_copy(out=sfst[:], in_=d1t[:])
                        mj = p.mark("v", m10)
                        p.wmark("v", e, mj)
                    e.engine_nop().then_inc(s_dv.h, 1)
                s_dv.n += 1
                p3_done = s_dv.n

                @block.sync
                def _(e, hb0=hb0, p3_done=p3_done):
                    e.wait_ge(s_dv.h, p3_done)
                    for b in range(2):
                        dst = ot_d[b, hb0:hb0 + 1024, :].rearrange("(c i) l -> c (i l)", i=C)
                        e.dma_start(out=dst, in_=usc[b * 64:(b + 1) * 64, :]).then_inc(s_st.h, 16)
                s_st.n += 32

        def emit_O(b, half):
            hb0 = half * 1024
            sfx = "_%d_%d" % (b, half)
            with contextlib.ExitStack() as st2:
                xh = st2.enter_context(nc.sbuf_tensor("a0_xh2" + sfx, [128, 8 * 512], BF16))
                gsb = st2.enter_context(nc.sbuf_tensor("a0_gsb" + sfx, [128, 8 * 512], BF16))
                osb = st2.enter_context(nc.sbuf_tensor("a0_osb" + sfx, [128, 1024], BF16))
                ogb = st2.enter_context(nc.sbuf_tensor("a0_ogb" + sfx, [128, 8 * 512], BF16))
                wr2 = st2.enter_context(nc.sbuf_tensor("a0_wr2" + sfx, [128, 2 * 8 * 128], BF16))
                tpb = st2.enter_context(nc.psum_tensor("a0_tpb" + sfx, [128, 512], BF16))

                for t5 in range(2):
                    t0 = b * TQ + hb0 + t5 * 512
                    prev_pe = s_pe.n
                    prev_dv = s_dv.n
                    emit_ln_tile(p, res_dram, ws, t0, xh, 0,
                                 pre_apply_waits=[(s_pe.h, prev_pe), (s_st.h, s_st.n)],
                                 pre_load_waits=[(s_dv.h, prev_dv), (s_st.h, s_st.n)],
                                 pre_stats_waits=[(s_dv.h, prev_dv), (s_ac.h, s_ac.n)])
                    xh_ready = ws["s_ac"].n

                    g_base = s_ac.n
                    for m in range(8):
                        slot = m % 2

                        @block.gpsimd
                        def _(e, m=m, slot=slot, wf=wfree[slot],
                              stn=(s_st.n if m == 0 else 0), dvn=(s_dv.n if m == 0 else 0)):
                            if wf > 0:
                                e.wait_ge(s_pe.h, wf)
                            if stn > 0:
                                e.wait_ge(s_st.h, stn)
                            if dvn > 0:
                                e.wait_ge(s_dv.h, dvn)
                            e.dma_start(out=wr2[:, slot * 1024:(slot + 1) * 1024]
                                        .rearrange("q (k c) -> q k c", k=8),
                                        in_=wd["wgoT_b"].rearrange("(k q) n -> q k n", q=128)
                                        [:, :, m * 128:(m + 1) * 128]).then_inc(s_wr[slot].h, 16)
                        s_wr[slot].n += 16
                        wl = s_wr[slot].n
                        pe_g0 = s_pe.n

                        @block.tensor
                        def _(e, m=m, slot=slot, wl=wl, xh_ready=xh_ready, pe_g0=pe_g0,
                              acn=s_ac.n, g_base=g_base):
                            e.wait_ge(ws["s_ac"].h, xh_ready)
                            e.wait_ge(s_wr[slot].h, wl)
                            if acn > 0 and m == 0:
                                e.wait_ge(s_ac.h, acn)
                            if m >= 4:
                                e.wait_ge(s_ac.h, g_base + m - 3)
                            for k in range(8):
                                ins = e.matmul(ps[2 + m % 4][:, :],
                                               wr2[:, slot * 1024 + k * 128:slot * 1024 + (k + 1) * 128],
                                               xh[:, k * 512:(k + 1) * 512],
                                               start=(k == 0), stop=(k == 7),
                                               skip_group_check=True)
                            ins.then_inc(s_pe.h, 1)
                        s_pe.n = pe_g0 + 1
                        gm = s_pe.n
                        wfree[slot] = s_pe.n

                        @block.scalar
                        def _(e, m=m, gm=gm, dvx=s_dv.n):
                            e.wait_ge(s_pe.h, gm)
                            ins = e.activation(out=gsb[:, m * 512:(m + 1) * 512],
                                               in_=ps[2 + m % 4][:, :], func=AF.Sigmoid)
                            ins.then_inc(s_ac.h, 1)
                        s_ac.n += 1
                    g_done = s_ac.n

                    for t8 in range(4):
                        r0 = hb0 + t5 * 512 + t8 * 128
                        ldb = s_ld2[0].n

                        @block.gpsimd
                        def _(e, b=b, r0=r0, dvn=s_dv.n, aw=g_done, stn=s_st.n):
                            if dvn > 0:
                                e.wait_ge(s_dv.h, dvn)
                            e.wait_ge(s_st.h, stn)
                            e.dma_start(out=osb[:], in_=ot_d[b, r0:r0 + 128, :]
                                        ).then_inc(s_ld2[0].h, 16)
                        s_ld2[0].n += 16
                        o_ld = s_ld2[0].n
                        pe_t0 = s_pe.n

                        @block.tensor
                        def _(e, o_ld=o_ld, pe_t0=pe_t0, dvn=s_dv.n):
                            e.wait_ge(s_ld2[0].h, o_ld)
                            for ch in range(8):
                                if ch >= 1:
                                    e.wait_ge(s_dv.h, dvn + ch)
                                ins = e.transpose(tpb[:, 0:128],
                                                  osb[:, ch * 128:(ch + 1) * 128],
                                                  consts["ident_bf16"][:, :])
                                ins.then_inc(s_pe.h, 1)
                        s_pe.n = pe_t0 + 8

                        @block.vector
                        def _(e, t8=t8, pe_t0=pe_t0, g_done=g_done):
                            e.wait_ge(s_ac.h, g_done)
                            for ch in range(8):
                                e.wait_ge(s_pe.h, pe_t0 + ch + 1)
                                ins = e.tensor_mul(
                                    out=ogb[:, ch * 512 + t8 * 128:ch * 512 + (t8 + 1) * 128],
                                    in0=tpb[:, 0:128],
                                    in1=gsb[:, ch * 512 + t8 * 128:ch * 512 + (t8 + 1) * 128])
                                ins.then_inc(s_dv.h, 1)
                        s_dv.n += 8
                    og_done = s_dv.n

                    for m in range(8):
                        slot = m % 2

                        @block.gpsimd
                        def _(e, m=m, slot=slot, wf=wfree[slot]):
                            e.wait_ge(s_pe.h, wf)
                            e.dma_start(out=wr2[:, slot * 1024:(slot + 1) * 1024]
                                        .rearrange("q (k c) -> q k c", k=8),
                                        in_=wd["wgoT_b"].rearrange("(k q) n -> q k n", q=128)
                                        [:, :, 1024 + m * 128:1024 + (m + 1) * 128]
                                        ).then_inc(s_wr[slot].h, 16)
                        s_wr[slot].n += 16
                        wl = s_wr[slot].n
                        pe_o0 = s_pe.n

                        @block.tensor
                        def _(e, m=m, slot=slot, wl=wl, og_done=og_done, pe_o0=pe_o0,
                              dvw=s_dv.n + 1):
                            e.wait_ge(s_wr[slot].h, wl)
                            e.wait_ge(s_dv.h, og_done)
                            if m == 4:
                                e.wait_ge(s_dv.h, dvw)
                            for k in range(8):
                                ins = e.matmul(ps[2 + m % 4][:, :],
                                               wr2[:, slot * 1024 + k * 128:slot * 1024 + (k + 1) * 128],
                                               ogb[:, k * 512:(k + 1) * 512],
                                               start=(k == 0), stop=(k == 7),
                                               skip_group_check=True)
                            ins.then_inc(s_pe.h, 1)
                        s_pe.n = pe_o0 + 1
                        wfree[slot] = s_pe.n
                    proj_done = s_pe.n

                    @block.sync
                    def _(e, t0=t0, dvn=s_dv.n, ordn=p.sem("ord_v").n, stn=s_st.n):
                        if dvn > 0:
                            e.wait_ge(s_dv.h, dvn)
                        if ordn > 0:
                            e.wait_ge(p.sem("ord_v").h, ordn)
                        if stn > 0:
                            e.wait_ge(s_st.h, stn)
                        e.dma_start(out=ws["rt"].rearrange("q (j c) -> q j c", j=8),
                                    in_=res_dram.rearrange("(j q) c -> q j c", q=128)
                                    [:, :, t0:t0 + 512]).then_inc(ws["s_ld"].h, 16)
                    ws["s_ld"].n += 16
                    rt_ld = ws["s_ld"].n

                    @block.vector
                    def _(e, proj_done=proj_done, rt_ld=rt_ld):
                        e.wait_ge(ws["s_ld"].h, rt_ld)
                        for mw in range(2):
                            e.wait_ge(s_pe.h, proj_done - 8 + 4 * (mw + 1))
                            for m4 in range(4):
                                m = mw * 4 + m4
                                ins = e.tensor_tensor(out=ws["rt"][:, m * 512:(m + 1) * 512],
                                                      in0=ws["rt"][:, m * 512:(m + 1) * 512],
                                                      in1=ps[2 + m4][:, :], op=ALU.add)
                                if mw == 0 and m4 == 3:
                                    ins.then_inc(s_dv.h, 1)
                                elif not (mw == 1 and m4 == 3):
                                    p.wmark("v", e, p.mark("v", ins))
                        ins.then_inc(s_dv.h, 1)
                    s_dv.n += 2
                    wr_ok = s_dv.n

                    @block.sync
                    def _(e, t0=t0, wr_ok=wr_ok):
                        e.wait_ge(s_dv.h, wr_ok)
                        e.dma_start(out=res_dram.rearrange("(j q) c -> q j c", q=128)
                                    [:, :, t0:t0 + 512],
                                    in_=ws["rt"].rearrange("q (j c) -> q j c", j=8)
                                    ).then_inc(s_st.h, 16)
                    s_st.n += 16

        for half in range(2):
            for b in range(2):
                emit_P(b, half)
                emit_U(b, half)
            emit_S(half, half)
            for b in range(2):
                emit_O(b, half)

        @block.sync
        def _(e, pc=p.pc_info):
            e.wait_ge(s_st.h, s_st.n)
            if pc is not None:
                e.wait_ge(pc["sems"][0].h, pc["full"][0])
                e.wait_ge(pc["sems"][1].h, pc["full"][1])



import contextlib


def make_consts():
    out = {
        "ident_f32": np.eye(128, dtype=np.float32),
        "ident_bf16": np.eye(128, dtype=BF),
    }
    half = 16
    theta = (10000.0 ** (-(np.arange(half, dtype=np.float64) * 2.0 / 32.0)))
    tpos = np.arange(2048, dtype=np.float64)
    ang = tpos[None, :] * theta[:, None]
    c = np.cos(ang); si = np.sin(ang)
    c2 = np.zeros((32, 2048)); s2 = np.zeros((32, 2048))
    c2[0::2] = c; c2[1::2] = c
    s2[0::2] = -si; s2[1::2] = si
    c96 = np.tile(c2, (3, 1)).astype(np.float32)
    s96 = np.tile(s2, (3, 1)).astype(np.float32)
    out["kc2"] = c96
    out["ks2"] = s96
    pm = np.zeros((128, 128), np.float32)
    for i in range(128):
        pm[i, i ^ 1] = 1.0
    out["perm96"] = pm.astype(BF)
    cm = np.zeros((4, 128, 512), np.float32)
    for sft in range(4):
        for kp in range(128):
            cm[sft, kp, :] = (sft * 128 + kp <= np.arange(512)).astype(np.float32)
    out["cmask"] = cm.astype(BF)
    sel = np.zeros((8, 16, 128), np.float32)
    for lc in range(8):
        for pp in range(128):
            sel[lc, 2 * lc + pp // 64, pp] = 1.0
    out["sel2h"] = sel
    return out


WEIGHT_SPECS = {
    "l0a": {
        "wqkvT": ([1024, 3072], F32),
        "wgoT": ([1024, 2048], F32),
        "wbfsT": ([1024, 48], F32),
        "bfdT": ([128, 16], F32),
        "bsdT": ([128, 16], F32),
    },
    "l0f": {
        "gateT": ([1024, 14], F32),
        "w1": ([14, 1024, 2048], F32),
        "b1T": ([14, 128, 16], F32),
        "w2": ([14, 2048, 1024], F32),
        "b2T": ([14, 128, 8], F32),
    },
    "l1a": {
        "wqkvp": ([1024, 3072], F32),
        "wproj": ([1024, 1024], F32),
        "lq1": ([32, 1], F32),
        "lk1": ([32, 1], F32),
        "lq2": ([32, 1], F32),
        "lk2": ([32, 1], F32),
    },
    "l1f": {
        "maker_bd": ([4, 8, 128, 128], F32),
        "up_bd": ([4, 8, 128, 512], F32),
        "w_cprojT": ([16384, 1024], F32),
    },
}


def build_program(blocks, debug_res=False):
    consts_np = make_consts()
    nc = bass.Bass("TRN2", target_bir_lowering=False, debug=False)

    drams = {}
    if "tin" in blocks:
        drams["x"] = nc.dram_tensor("x", [NTOK, D], F32, kind="ExternalInput").ap()
    if "tout" in blocks:
        drams["y"] = nc.dram_tensor("y", [NTOK, D], F32, kind="ExternalOutput").ap()

    first, last = blocks[0], blocks[-1]
    if first != "tin":
        res = nc.dram_tensor("res_io", [D, NTOK], F32,
                             kind="ExternalInput" if last == "tout" else "ExternalInputOutput"
                             if False else "ExternalInput").ap()
        res_in_external = True
    else:
        res = None
        res_in_external = False
    if last != "tout":
        if res is None:
            res = nc.dram_tensor("res_out", [D, NTOK], F32, kind="ExternalOutput").ap()
        else:
            res_out = nc.dram_tensor("res_out", [D, NTOK], F32, kind="ExternalOutput").ap()
    else:
        res_out = None
    if res is None:
        res = nc.dram_tensor("res_scr", [D, NTOK], F32, kind="Internal").ap()

    wd = {}
    for b in blocks:
        for name, (shape, dt) in WEIGHT_SPECS.get(b, {}).items():
            wd[name] = nc.dram_tensor(name, shape, dt, kind="ExternalInput").ap()
    if "l1f" in blocks:
        wd["wctb"] = nc.dram_tensor("wctb", [16384, 1024], BF16, kind="Internal").ap()
    PRECAST = {"w1": [14 * 1024, 2048], "w2": [14 * 2048, 1024],
               "wqkvT": [1024, 3072], "wgoT": [1024, 2048],
               "wqkvp": [1024, 3072], "wproj": [1024, 1024]}
    for name, shp in PRECAST.items():
        if name in wd:
            wd[name + "_b"] = nc.dram_tensor(name + "_b", shp, BF16, kind="Internal").ap()

    consts = {}
    for name, arr in consts_np.items():
        dt = F32 if arr.dtype == np.float32 else BF16
        consts[name] = nc.dram_tensor(name, list(arr.shape), dt, kind="ExternalInput").ap()

    with contextlib.ExitStack() as stack:
        p = P(nc, None, stack)

        class _BlockCtx:
            def __enter__(self2):
                self2.cm = nc.Block()
                p.block = self2.cm.__enter__()
                return p.block

            def __exit__(self2, *a):
                r = self2.cm.__exit__(*a)
                p.block = None
                return r

        SB_CONSTS = ("ident_f32", "ident_bf16")
        csb = {}
        s_c = p.sem("const_ld")
        for name in SB_CONSTS:
            dt = F32 if consts_np[name].dtype == np.float32 else BF16
            t = stack.enter_context(nc.sbuf_tensor("c_" + name, list(consts_np[name].shape), dt))
            csb[name] = t

        with _BlockCtx() as block:
            @block.sync
            def _(e):
                for name in SB_CONSTS:
                    e.dma_start(out=csb[name][:], in_=consts[name]).then_inc(s_c.h, 16)
            s_c.n += 16 * len(SB_CONSTS)
            cw = s_c.n

            for eng in ["tensor", "vector", "scalar", "gpsimd"]:
                def mk(eng=eng):
                    def body(e):
                        e.wait_ge(s_c.h, cw)
                    return body
                getattr(block, eng)(mk())

        for name in consts:
            if name not in csb:
                csb[name] = consts[name]

        work_res = res
        if first != "tin" and last != "tout":
            s_cp = p.sem("res_cp")

            with _BlockCtx() as block:
                @block.sync
                def _(e):
                    e.dma_start(out=res_out, in_=res).then_inc(s_cp.h, 16)
                    e.wait_ge(s_cp.h, 16)
                s_cp.n += 16
            work_res = res_out

        pc_pairs = []
        for nm in ("wqkvT", "wgoT"):
            if nm in wd:
                pc_pairs.append((wd[nm], wd[nm + "_b"]))
        n_l0a_jobs = sum(ap.shape[0] // 128 for ap, _ in pc_pairs)
        if "w1" in wd:
            pc_pairs.append((wd["w1"].rearrange("e d f -> (e d) f"), wd["w1_b"]))
            pc_pairs.append((wd["w2"].rearrange("e f d -> (e f) d"), wd["w2_b"]))
        for nm in ("wqkvp", "wproj"):
            if nm in wd:
                pc_pairs.append((wd[nm], wd[nm + "_b"]))
        if "w_cprojT" in wd:
            pc_pairs.append((wd["w_cprojT"], wd["wctb"]))
        p.pc_info = None
        if pc_pairs and "l0a" not in blocks:
            with _BlockCtx():
                emit_precast(p, pc_pairs, "pc")
            pc_pairs = []

        dbg = {}
        for bi, b in enumerate(blocks):
            if debug_res:
                dbg[bi] = nc.dram_tensor("dbg_%d" % bi, [D, NTOK], F32,
                                         kind="ExternalOutput").ap()
            if b == "tin":
                with _BlockCtx():
                    emit_transpose_in(p, drams["x"], work_res, csb)
            elif b == "tout":
                with _BlockCtx():
                    emit_transpose_out(p, work_res, drams["y"], csb)
            elif b == "l0a":
                with _BlockCtx():
                    with contextlib.ExitStack() as pcs:
                        if pc_pairs:
                            emit_precast(p, pc_pairs, "pc", early_jobs=n_l0a_jobs,
                                         stack=pcs)
                            pc_pairs = []
                        emit_l0a(p, work_res, wd, csb)
            elif b == "l0f":
                with _BlockCtx():
                    emit_l0f(p, work_res, wd, csb)
            elif b == "l1a":
                with _BlockCtx():
                    emit_l1a(p, work_res, wd, csb)
            elif b == "l1f":
                with _BlockCtx():
                    emit_l1f(p, work_res, wd)
            else:
                raise ValueError(b)
            if debug_res:
                s_dbg = p.sem("dbg")

                with _BlockCtx() as block:
                    @block.sync
                    def _(e, bi=bi):
                        e.dma_start(out=dbg[bi], in_=work_res).then_inc(s_dbg.h, 16)
                        e.wait_ge(s_dbg.h, s_dbg.n + 16)
                    s_dbg.n += 16

    return nc, consts_np


def prep_weights(blocks, inputs):
    out = {}
    if "l0a" in blocks:
        wq = np.asarray(inputs["L0_attn_Wq"], np.float32)
        wk = np.asarray(inputs["L0_attn_Wk"], np.float32)
        wv = np.asarray(inputs["L0_attn_Wv"], np.float32)
        wo = np.asarray(inputs["L0_attn_Wo"], np.float32)
        wg = np.asarray(inputs["L0_attn_Wg"], np.float32)
        out["wqkvT"] = np.ascontiguousarray(
            np.concatenate([wq.T, wk.T, wv.T], axis=1))
        out["wgoT"] = np.ascontiguousarray(np.concatenate([wg.T, wo.T], axis=1))
        out["wbfsT"] = np.ascontiguousarray(np.concatenate(
            [np.asarray(inputs["L0_attn_Wb"], np.float32).T,
             np.asarray(inputs["L0_attn_Wfd"], np.float32).T,
             np.asarray(inputs["L0_attn_Wsd"], np.float32).T], axis=1))
        out["bfdT"] = np.ascontiguousarray(
            np.tile(np.asarray(inputs["L0_attn_bfd"], np.float32)[None, :], (128, 1)))
        out["bsdT"] = np.ascontiguousarray(
            np.tile(np.asarray(inputs["L0_attn_bsd"], np.float32)[None, :], (128, 1)))
    if "l0f" in blocks:
        out["gateT"] = np.ascontiguousarray(np.asarray(inputs["L0_ffn_gate_w"], np.float32).T)
        out["w1"] = np.ascontiguousarray(np.asarray(inputs["L0_ffn_w1"], np.float32))
        out["w2"] = np.ascontiguousarray(np.asarray(inputs["L0_ffn_w2"], np.float32))
        out["b1T"] = np.ascontiguousarray(
            np.asarray(inputs["L0_ffn_b1"], np.float32).reshape(14, 16, 128).transpose(0, 2, 1))
        out["b2T"] = np.ascontiguousarray(
            np.asarray(inputs["L0_ffn_b2"], np.float32).reshape(14, 8, 128).transpose(0, 2, 1))
    if "l1a" in blocks:
        wa = np.asarray(inputs["L1_attn_W_attn"], np.float32)
        cols = []
        for c in range(11):
            for h in range(3 * c, min(3 * c + 3, 32)):
                cols.extend(range(h * 32, h * 32 + 32))
        qcols = np.array(cols)
        out["wqkvp"] = np.ascontiguousarray(
            np.concatenate([wa[:, qcols], wa[:, 1024 + qcols], wa[:, 2048:]], axis=1))
        out["wproj"] = np.ascontiguousarray(np.asarray(inputs["L1_attn_W_proj"], np.float32))
        out["lq1"] = np.asarray(inputs["L1_attn_lambda_q1"], np.float32).reshape(32, 1)
        out["lk1"] = np.asarray(inputs["L1_attn_lambda_k1"], np.float32).reshape(32, 1)
        out["lq2"] = np.asarray(inputs["L1_attn_lambda_q2"], np.float32).reshape(32, 1)
        out["lk2"] = np.asarray(inputs["L1_attn_lambda_k2"], np.float32).reshape(32, 1)
    if "l1f" in blocks:
        mk = np.asarray(inputs["L1_ffn_rail_maker"], dtype=np.float32)
        up = np.asarray(inputs["L1_ffn_rail_up_proj"], dtype=np.float32)
        mk_bd = np.zeros((4, 8, 128, 128), np.float32)
        up_bd = np.zeros((4, 8, 128, 512), np.float32)
        for h in range(4):
            for c in range(8):
                for r in range(8):
                    blk = mk[h, c * 8 + r]
                    mk_bd[h, c, r * 16:(r + 1) * 16, r * 16:(r + 1) * 16] = blk.T
                    ub = up[h, c * 8 + r]
                    up_bd[h, c, r * 16:(r + 1) * 16, r * 64:(r + 1) * 64] = ub.T
        out["maker_bd"] = mk_bd
        out["up_bd"] = up_bd
        out["w_cprojT"] = np.ascontiguousarray(
            np.asarray(inputs["L1_ffn_w_cproj"], dtype=np.float32).T)
    return out


def kernel(**inputs):
    x = np.ascontiguousarray(np.asarray(inputs["L0_attn_x"], np.float32))
    xs = x.reshape(8, B_CORE * T, D)
    all_w = prep_weights(["l0a", "l0f", "l1a", "l1f"], inputs)

    blocks = ["tin", "l0a", "l0f", "l1a", "l1f", "tout"]
    nc, consts_np = build_program(blocks)
    in_maps = []
    for c in range(8):
        m = {"x": xs[c]}
        m.update(all_w)
        m.update(consts_np)
        in_maps.append(m)
    res = run_bass_kernel_spmd(nc, in_maps, core_ids=list(range(8)))
    out = np.stack([r["y"] for r in res.results], axis=0)
    return out.reshape(16, T, D)
